# version 10
# baseline (speedup 1.0000x reference)
import numpy as np
import ml_dtypes
from contextlib import ExitStack

import concourse.bass as bass
import concourse.mybir as mybir
from concourse.bass_utils import run_bass_kernel_spmd

F32 = mybir.dt.float32
BF16 = mybir.dt.bfloat16
AF = mybir.ActivationFunctionType
ALU = mybir.AluOpType
AX = mybir.AxisListType
BF16NP = ml_dtypes.bfloat16

D = 1024
DC = 8
NH = 16
DH = 128
TF = 8192
TO = 4096
EPS = 1e-5
NEG = -1e10

SLOT_BUD = [16, 14, 12, 10, 8, 6, 4, 2]
NEEDS = {0: [16, 13, 12, 9, 8, 5, 4, 1],
         1: [15, 14, 11, 10, 7, 6, 3, 2]}

DBG_A = 0


def build_A():
    nc = bass.Bass("TRN2", target_bir_lowering=False)
    dr = lambda nm, s, dt, k="ExternalInput": nc.dram_tensor(nm, s, dt, kind=k).ap()

    xf_d = dr("xf", [TF, D], F32)
    xo_d = dr("xo", [TO, D], F32)
    w_in_d = dr("w_inT", [D, 8208], F32)
    b_in_d = dr("b_in", [8208], F32)
    ilw_d = dr("ilw", [D], F32)
    ilb_d = dr("ilb", [D], F32)
    w_out_d = dr("w_outT", [2048, D], F32)
    olw_d = dr("olw", [D], F32)
    olb_d = dr("olb", [D], F32)
    smf_d = dr("smf", [1, NH], F32)
    lgs_d = dr("lgs", [1, NH], F32)
    gw_d = dr("gwT", [D, 8], F32)
    lbb_d = dr("lbb", [1, 8], F32)
    masks_d = dr("masks", [8, 4, 128, 1024], BF16)
    ident_d = dr("ident", [128, 128], BF16)
    ones3_d = dr("ones3", [3, 128], BF16)

    xmid_d = dr("xmid", [TO, D], F32, "ExternalOutput")
    part_d = dr("part", [2, 8], F32, "ExternalOutput")

    scr = lambda nm, s, dt: nc.dram_tensor(nm, s, dt, kind="Internal").ap()
    xnf_s = scr("xnf_s", [DC, 128, TF], BF16)
    xno_s = scr("xno_s", [DC, 128, TO], BF16)
    kp_s = scr("kp_s", [NH, 128, TF], BF16)
    v_s = scr("v_s", [NH, TF // 128, 128, DH], BF16)
    q_s = scr("q_s", [NH, 128, TO], BF16)
    p_s = scr("p_s", [NH, 128, TO], BF16)
    pos3_s = scr("pos3_s", [NH, 3, TF], BF16)
    hs_s = scr("hs_s", [3, NH], F32)
    pos_s = scr("pos_s", [16, TF], F32)
    o_s = scr("o_s", [NH, 128, TO], BF16)
    l4_s = scr("l4_s", [4, 128], F32)
    l_s = scr("l_s", [8, NH, 4, 128], F32)
    sc8_s = scr("sc8_s", [8, 2], F32)

    dbg = {}
    if DBG_A >= 1:
        dbg["kp"] = dr("dbg_kp", [NH, 128, TF], BF16, "ExternalOutput")
        dbg["v"] = dr("dbg_v", [NH, TF // 128, 128, DH], BF16, "ExternalOutput")
        dbg["q"] = dr("dbg_q", [NH, 128, TO], BF16, "ExternalOutput")
        dbg["p"] = dr("dbg_p", [NH, 128, TO], BF16, "ExternalOutput")
        dbg["pos3"] = dr("dbg_pos3", [NH, 3, TF], BF16, "ExternalOutput")
        dbg["xnf"] = dr("dbg_xnf", [DC, 128, TF], BF16, "ExternalOutput")
    if DBG_A >= 2:
        dbg["o"] = dr("dbg_o", [NH, 128, TO], BF16, "ExternalOutput")

    gctx = ExitStack()
    gsb = lambda nm, s, dt: gctx.enter_context(nc.sbuf_tensor(nm, s, dt))
    pst = lambda nm, s, dt: gctx.enter_context(nc.psum_tensor(nm, s, dt))
    sem = lambda nm: gctx.enter_context(nc.semaphore(nm))

    ident = gsb("ident_s", [128, 128], BF16)
    hvec = gsb("hvec", [1, 3 * NH], F32)
    sig_b = gsb("sig_b", [128, NH], F32)
    oms_b = gsb("oms_b", [128, NH], F32)
    alp_b = gsb("alp_b", [128, NH], F32)
    bcol = gsb("bcol", [128, 65], F32)
    stats = gsb("stats", [128, 8], F32)
    seedc = gsb("seedc", [128, 1], F32)
    ones128f = gsb("ones128f", [1, 128], F32)
    onecol = gsb("onecol", [128, 1], F32)
    ones3 = gsb("ones3_s", [3, 128], BF16)
    carry = gsb("carry", [128, NH], F32)
    bvb = gsb("bvb", [128, 2048], F32)

    ps_a = [pst(f"ps_a{i}", [128, 1024], F32) for i in range(2)]
    ps_b = [pst(f"ps_b{i}", [128, 1024], BF16) for i in range(2)]
    ps_c = pst("ps_c", [128, 512], F32)
    ps_d = pst("ps_d", [128, 512], F32)

    s_ld = sem("s_ld")
    s_gl = sem("s_gl")
    s_pe = sem("s_pe")
    s_dv = sem("s_dv")
    s_sc = sem("s_sc")
    s_go = sem("s_go")
    s_lc = sem("s_lc")
    s_li = [sem("s_li0"), sem("s_li1")]
    s_lh = sem("s_lh")
    s_lm = [sem("s_lm0"), sem("s_lm1")]
    s_lw = sem("s_lw")
    s_sx = [sem("s_sx0"), sem("s_sx1")]
    s_sk = [sem("s_sk0"), sem("s_sk1")]
    s_sv = [sem("s_sv0"), sem("s_sv1")]
    s_gd = [sem("s_gd0"), sem("s_gd1")]

    n = {"ld": 0, "gl": 0, "pe": 0, "dv": 0, "sc": 0, "go": 0,
         "lc": 0, "li0": 0, "li1": 0, "lh": 0, "lm0": 0, "lm1": 0, "lw": 0,
         "sx0": 0, "sx1": 0, "sk0": 0, "sk1": 0, "sv0": 0, "sv1": 0,
         "gd0": 0, "gd1": 0}

    def bump(k, i=1):
        n[k] += i
        return n[k]

    with nc.Block() as blk:

        p1 = ExitStack()
        sb = lambda nm, s, dt: p1.enter_context(nc.sbuf_tensor(nm, s, dt))
        xin = [sb(f"xin{i}", [128, D], F32) for i in range(2)]
        x2t = sb("x2t", [128, D], F32)
        t2t = [sb(f"t2t{i}", [128, D], F32) for i in range(2)]
        xnbf = [sb(f"xnbf{i}", [128, D], BF16) for i in range(2)]
        xnT = [sb(f"xnT{i}", [128, D], BF16) for i in range(2)]
        wln_b = sb("wln_b", [128, D], F32)
        bln_b = sb("bln_b", [128, D], F32)


        @blk.sync
        def _(e):
            e.dma_start(out=ident[:], in_=ident_d).then_inc(s_lc, 16)
            e.dma_start(out=ones3[:], in_=ones3_d).then_inc(s_lc, 16)
            e.dma_start(out=wln_b[:], in_=ilw_d[None, :].broadcast_to((128, D))).then_inc(s_lc, 16)
            e.dma_start(out=bln_b[:], in_=ilb_d[None, :].broadcast_to((128, D))).then_inc(s_lc, 16)
            e.dma_start(out=hvec[:, 0:NH], in_=smf_d).then_inc(s_lc, 16)
            e.dma_start(out=hvec[:, NH:2 * NH], in_=lgs_d).then_inc(s_lc, 16)
            e.dma_start(out=bvb[:], in_=b_in_d[None, 4096:6144].broadcast_to((128, 2048))).then_inc(s_lc, 16)
        bump("lc", 112)

        @blk.gpsimd
        def _(e):
            for oc in range(64):
                e.dma_start(out=bcol[:, oc:oc + 1],
                            in_=b_in_d[oc * 128:(oc + 1) * 128][:, None]).then_inc(s_gl, 16)
            e.dma_start(out=bcol[0:16, 64:65],
                        in_=b_in_d[8192:8208][:, None]).then_inc(s_gl, 16)
        bump("gl", 65 * 16)
        ld0 = n["lc"]

        @blk.scalar
        def _(e, ld0=ld0):
            e.wait_ge(s_lc, ld0)
            e.activation(out=hvec[:, 2 * NH:3 * NH], in_=hvec[:, 0:NH],
                         func=AF.Sigmoid).then_inc(s_sc, 1)
        sc0 = bump("sc")

        @blk.vector
        def _(e, sc0=sc0):
            e.wait_ge(s_sc, sc0)
            e.tensor_scalar(out=hvec[:, 0:NH], in0=hvec[:, 2 * NH:3 * NH],
                            scalar1=-1.0, scalar2=1.0, op0=ALU.mult,
                            op1=ALU.add).then_inc(s_dv, 1)
            e.memset(carry[:], 0.0).then_inc(s_dv, 1)
            e.memset(seedc[:], -3.0e38).then_inc(s_dv, 1)
            e.memset(ones128f[:], 1.0).then_inc(s_dv, 1)
            e.memset(onecol[:], 1.0).then_inc(s_dv, 1)
        dv0 = bump("dv", 5)

        @blk.scalar
        def _(e, dv0=dv0):
            e.wait_ge(s_dv, dv0)
            e.activation(out=hvec[:, NH:2 * NH], in_=hvec[:, NH:2 * NH],
                         func=AF.Exp, scale=-2.0).then_inc(s_sc, 1)
        sc1 = bump("sc")

        @blk.vector
        def _(e, sc1=sc1):
            e.wait_ge(s_sc, sc1)
            e.tensor_scalar_mul(hvec[:, NH:2 * NH], hvec[:, NH:2 * NH],
                                float(1.0 / np.sqrt(128.0))).then_inc(s_dv, 1)
        dv1 = bump("dv")

        @blk.gpsimd
        def _(e, dv1=dv1):
            e.wait_ge(s_dv, dv1)
            e.dma_start(out=hs_s.rearrange("r h -> (r h)")[None, :],
                        in_=hvec[:, 0:3 * NH]).then_inc(s_gl, 16)
            e.wait_ge(s_gl, bump("gl", 16))
            for h in range(NH):
                e.dma_start(out=oms_b[:, h:h + 1],
                            in_=hs_s[0:1, h:h + 1].broadcast_to((128, 1))).then_inc(s_gl, 16)
                e.dma_start(out=alp_b[:, h:h + 1],
                            in_=hs_s[1:2, h:h + 1].broadcast_to((128, 1))).then_inc(s_gl, 16)
                e.dma_start(out=sig_b[:, h:h + 1],
                            in_=hs_s[2:3, h:h + 1].broadcast_to((128, 1))).then_inc(s_gl, 16)
        bump("gl", 48 * 16)

        ev_dv = [0, 0]
        ev_sc = [0, 0]
        ev_go = [0, 0]
        for t in range(96):
            b = t % 2
            own = t >= 64
            src = xo_d if own else xf_d
            row = (t - 64) * 128 if own else t * 128
            dstT = xno_s if own else xnf_s

            @blk.sync
            def _(e, b=b, src=src, row=row, w=ev_dv[b]):
                if w:
                    e.wait_ge(s_dv, w)
                e.dma_start(out=xin[b][:], in_=src[row:row + 128, :]).then_inc(s_li[b], 16)
            ld_t = bump(f"li{b}", 16)

            @blk.scalar
            def _(e, b=b, ld_t=ld_t):
                e.wait_ge(s_li[b], ld_t)
                e.activation(out=x2t[:], in_=xin[b][:], func=AF.Square,
                             accum_out=stats[:, 1:2]).then_inc(s_sc, 1)
            sc_sq = bump("sc")

            @blk.vector
            def _(e, b=b, sc_sq=sc_sq, base=n["dv"]):
                e.wait_ge(s_sc, sc_sq)
                e.reduce_sum(stats[:, 0:1], xin[b][:], axis=AX.X).then_inc(s_dv, 1)
                e.wait_ge(s_dv, base + 1)
                e.tensor_scalar_mul(stats[:, 2:3], stats[:, 0:1], 1.0 / D).then_inc(s_dv, 1)
                e.wait_ge(s_dv, base + 2)
                e.tensor_mul(stats[:, 3:4], stats[:, 2:3], stats[:, 2:3]).then_inc(s_dv, 1)
                e.wait_ge(s_dv, base + 3)
                e.scalar_tensor_tensor(out=stats[:, 3:4], in0=stats[:, 1:2],
                                       scalar=1.0 / D, in1=stats[:, 3:4],
                                       op0=ALU.mult, op1=ALU.subtract).then_inc(s_dv, 1)
                e.wait_ge(s_dv, base + 4)
                e.tensor_scalar(out=stats[:, 4:5], in0=stats[:, 3:4],
                                scalar1=float(1.0 + EPS), scalar2=float(EPS * EPS),
                                op0=ALU.mult, op1=ALU.add).then_inc(s_dv, 1)
            bump("dv", 5)
            dv_ab = n["dv"]

            @blk.scalar
            def _(e, dv_ab=dv_ab):
                e.wait_ge(s_dv, dv_ab)
                e.sqrt(stats[:, 5:6], stats[:, 4:5]).then_inc(s_sc, 1)
            sc_rt = bump("sc")

            @blk.vector
            def _(e, b=b, sc_rt=sc_rt, base=n["dv"]):
                e.wait_ge(s_sc, sc_rt)
                e.reciprocal(stats[:, 6:7], stats[:, 5:6]).then_inc(s_dv, 1)
                e.wait_ge(s_dv, base + 1)
                e.tensor_scalar(out=t2t[b][:], in0=xin[b][:], scalar1=stats[:, 2:3],
                                scalar2=stats[:, 6:7], op0=ALU.subtract,
                                op1=ALU.mult).then_inc(s_dv, 1)
                e.wait_ge(s_dv, base + 2)
                e.tensor_mul(t2t[b][:], t2t[b][:], wln_b[:]).then_inc(s_dv, 1)
                e.wait_ge(s_dv, base + 3)
                e.tensor_add(xnbf[b][:], t2t[b][:], bln_b[:]).then_inc(s_dv, 1)
            bump("dv", 4)
            ev_dv[b] = n["dv"]
            dv_nrm = n["dv"]

            @blk.tensor
            def _(e, b=b, dv_nrm=dv_nrm, w=ev_sc[b]):
                e.wait_ge(s_dv, dv_nrm)
                if w:
                    e.wait_ge(s_sc, w)
                for k in range(DC):
                    ins = e.transpose(ps_b[b][:, k * 128:(k + 1) * 128],
                                      xnbf[b][:, k * 128:(k + 1) * 128], ident[:])
                ins.then_inc(s_pe, 1)
            pe_tr = bump("pe")

            @blk.scalar
            def _(e, b=b, pe_tr=pe_tr, w=ev_go[b]):
                e.wait_ge(s_pe, pe_tr)
                if w:
                    e.wait_ge(s_sx[b], w)
                e.copy(xnT[b][:], ps_b[b][:]).then_inc(s_sc, 1)
            sc_ev = bump("sc")
            ev_sc[b] = sc_ev

            @blk.gpsimd
            def _(e, b=b, sc_ev=sc_ev, dstT=dstT, row=row):
                e.wait_ge(s_sc, sc_ev)
                e.dma_start(
                    out=dstT[:, :, row:row + 128].rearrange("c p q -> p c q"),
                    in_=xnT[b][:].rearrange("p (c q) -> p c q", c=DC),
                ).then_inc(s_sx[b], 16)
            ev_go[b] = bump(f"sx{b}", 16)

        a1_sx = (n["sx0"], n["sx1"])
        p1_keep = (wln_b, bln_b)
        del p1_keep
        p1.close()

        p2 = ExitStack()
        sb2 = lambda nm, s, dt: p2.enter_context(nc.sbuf_tensor(nm, s, dt))
        wt1 = sb2("wt1", [128, DC, 2048], BF16)
        wt2 = sb2("wt2", [128, DC, 2048], BF16)
        wyT = sb2("wyT", [128, DC, NH], BF16)
        xnt_in = [sb2(f"xnt_in{i}", [128, DC, 512], BF16) for i in range(2)]
        ktil = sb2("ktil", [128, 512], F32)
        khat = sb2("khat", [128, 512], F32)
        kout = [sb2(f"kout{i}", [128, 512], BF16) for i in range(2)]
        vout = [sb2(f"vout{i}", [128, 512], BF16) for i in range(2)]
        yrow = sb2("yrow", [16, TF], F32)
        posr = sb2("posr", [16, TF], F32)
        ppf = sb2("ppf", [128, 1024], F32)
        ppt = sb2("ppt", [128, 1024], F32)
        pph = sb2("pph", [128, 1024], BF16)
        ppm = sb2("ppm", [128, 1024], BF16)
        ppl = sb2("ppl", [128, 1024], BF16)
        qevac = [sb2(f"qevac{i}", [128, 512], BF16) for i in range(2)]
        @blk.tensor
        def _(e, w=n["gl"], w2=n["lc"], w3=a1_sx, w4=(n["li0"], n["li1"])):
            e.wait_ge(s_gl, w)
            e.wait_ge(s_lc, w2)
            e.wait_ge(s_sx[0], w3[0])
            e.wait_ge(s_sx[1], w3[1])
            e.wait_ge(s_li[0], w4[0])
            e.wait_ge(s_li[1], w4[1])
            e.sem_inc(s_pe, 1)
        a2bar = bump("pe")

        @blk.gpsimd
        def _(e, wbar=a2bar):
            e.wait_ge(s_pe, wbar)
            e.dma_start(out=wt1[:],
                        in_=w_in_d[:, 2048:4096].rearrange("(c p) j -> p c j", p=128)).then_inc(s_lw, 16)
            e.dma_start(out=wt2[:],
                        in_=w_in_d[:, 4096:6144].rearrange("(c p) j -> p c j", p=128)).then_inc(s_lw, 16)
            e.dma_start(out=wyT[:],
                        in_=w_in_d[:, 8192:8208].rearrange("(c p) j -> p c j", p=128)).then_inc(s_lw, 16)
        bump("lw", 48)
        a2a_lw = n["lw"]

        @blk.scalar
        def _(e, w=a2a_lw):
            e.wait_ge(s_lw, w)
            e.sem_inc(s_sc, 1)
        a2a_sc = bump("sc")

        xnt_free = [0, 0]
        kevac_dv = [0, 0]
        kout_go = [0, 0]
        vout_go = [0, 0]
        vevac_dv = [0, 0]
        y_dv = 0
        for tt in range(16):
            b = tt % 2
            tok0 = tt * 512

            @blk.sync
            def _(e, b=b, tok0=tok0, w=xnt_free[b], wg=a1_sx):
                e.wait_ge(s_sx[0], wg[0])
                e.wait_ge(s_sx[1], wg[1])
                if w:
                    e.wait_ge(s_pe, w)
                e.dma_start(out=xnt_in[b][:],
                            in_=xnf_s[:, :, tok0:tok0 + 512].rearrange("c p q -> p c q")).then_inc(s_li[b], 16)
            ld_kv = bump(f"li{b}", 16)

            for h in range(NH):
                @blk.tensor
                def _(e, b=b, h=h, ld_kv=ld_kv, wdv=kevac_dv[h % 2], wsc=a2a_sc):
                    e.wait_ge(s_li[b], ld_kv)
                    e.wait_ge(s_sc, wsc)
                    if wdv:
                        e.wait_ge(s_dv, wdv)
                    for c in range(DC):
                        ins = e.matmul(ps_a[h % 2][:, 0:512],
                                       wt1[:, c, h * 128:(h + 1) * 128],
                                       xnt_in[b][:, c, :],
                                       start=(c == 0), stop=(c == DC - 1))
                    ins.then_inc(s_pe, 1)
                pe_k = bump("pe")

                @blk.vector
                def _(e, h=h, pe_k=pe_k, base=n["dv"], wgo=kout_go[h % 2]):
                    e.wait_ge(s_pe, pe_k)
                    e.wait_ge(s_dv, base)
                    if wgo:
                        e.wait_ge(s_sk[h % 2], wgo)
                    e.tensor_scalar(out=ktil[:], in0=ps_a[h % 2][:, 0:512],
                                    scalar1=bcol[:, 16 + h:17 + h],
                                    scalar2=oms_b[:, h:h + 1],
                                    op0=ALU.add, op1=ALU.mult).then_inc(s_dv, 1)
                    e.tensor_scalar(out=khat[:], in0=ps_a[h % 2][:, 0:512],
                                    scalar1=bcol[:, 16 + h:17 + h],
                                    scalar2=sig_b[:, h:h + 1],
                                    op0=ALU.add, op1=ALU.mult).then_inc(s_dv, 1)
                    e.wait_ge(s_dv, base + 2)
                    e.tensor_add(kout[h % 2][:, 0:1], ktil[:, 0:1], carry[:, h:h + 1]).then_inc(s_dv, 1)
                    e.tensor_add(kout[h % 2][:, 1:512], ktil[:, 1:512], khat[:, 0:511]).then_inc(s_dv, 1)
                    e.wait_ge(s_dv, base + 4)
                    e.tensor_copy(carry[:, h:h + 1], khat[:, 511:512]).then_inc(s_dv, 1)
                bump("dv", 5)
                kevac_dv[h % 2] = n["dv"] - 3
                dv_k = n["dv"]

                @blk.gpsimd
                def _(e, h=h, tok0=tok0, dv_k=dv_k):
                    e.wait_ge(s_dv, dv_k)
                    e.dma_start(out=kp_s[h][:, tok0:tok0 + 512],
                                in_=kout[h % 2][:]).then_inc(s_sk[h % 2], 16)
                kout_go[h % 2] = bump(f"sk{h % 2}", 16)

            vi = 0
            for g in range(4):
                for ts4 in range(4):
                    @blk.tensor
                    def _(e, b=b, g=g, ts4=ts4, vi=vi, wdv=vevac_dv[vi % 2]):
                        if wdv:
                            e.wait_ge(s_dv, wdv)
                        for c in range(DC):
                            ins = e.matmul(ps_a[vi % 2][:, 512:1024],
                                           xnt_in[b][:, c, ts4 * 128:(ts4 + 1) * 128],
                                           wt2[:, c, g * 512:(g + 1) * 512],
                                           start=(c == 0), stop=(c == DC - 1))
                        ins.then_inc(s_pe, 1)
                    pe_v = bump("pe")

                    @blk.vector
                    def _(e, g=g, vi=vi, pe_v=pe_v, wgo=vout_go[vi % 2]):
                        e.wait_ge(s_pe, pe_v)
                        if wgo:
                            e.wait_ge(s_sv[vi % 2], wgo)
                        e.tensor_add(vout[vi % 2][:], ps_a[vi % 2][:, 512:1024],
                                     bvb[:, g * 512:(g + 1) * 512]).then_inc(s_dv, 1)
                    vevac_dv[vi % 2] = bump("dv")
                    dv_v = n["dv"]

                    @blk.gpsimd
                    def _(e, g=g, ts4=ts4, tok0=tok0, vi=vi, dv_v=dv_v):
                        e.wait_ge(s_dv, dv_v)
                        kc = tok0 // 128 + ts4
                        e.dma_start(
                            out=v_s[4 * g:4 * (g + 1), kc, :, :].rearrange("h p d -> p h d"),
                            in_=vout[vi % 2][:].rearrange("p (h d) -> p h d", h=4),
                        ).then_inc(s_sv[vi % 2], 16)
                    vout_go[vi % 2] = bump(f"sv{vi % 2}", 16)
                    vi += 1

            @blk.tensor
            def _(e, b=b, ld_kv=ld_kv, wdv=y_dv):
                if wdv:
                    e.wait_ge(s_dv, wdv)
                for c in range(DC):
                    ins = e.matmul(ps_d[0:16, :], wyT[:, c, :], xnt_in[b][:, c, :],
                                   start=(c == 0), stop=(c == DC - 1))
                ins.then_inc(s_pe, 1)
            pe_yt = bump("pe")
            xnt_free[b] = pe_yt

            @blk.vector
            def _(e, tok0=tok0, pe_yt=pe_yt):
                e.wait_ge(s_pe, pe_yt)
                e.tensor_scalar(out=yrow[:, tok0:tok0 + 512], in0=ps_d[0:16, :],
                                scalar1=bcol[0:16, 64:65], scalar2=None,
                                op0=ALU.add).then_inc(s_dv, 1)
            y_dv = bump("dv")

        a2b_dv = n["dv"]

        @blk.scalar
        def _(e, w=a2b_dv):
            e.wait_ge(s_dv, w)
            e.activation(out=yrow[:], in_=yrow[:], func=AF.Sigmoid).then_inc(s_sc, 1)
        sc_sig = bump("sc")

        @blk.vector
        def _(e, sc_sig=sc_sig):
            e.wait_ge(s_sc, sc_sig)
            e.tensor_tensor_scan(out=posr[:], data0=yrow[:], data1=yrow[:],
                                 initial=0.0, op0=ALU.add, op1=ALU.max).then_inc(s_dv, 1)
        dv_scan = bump("dv")

        @blk.gpsimd
        def _(e, dv_scan=dv_scan):
            e.wait_ge(s_dv, dv_scan)
            e.dma_start(out=pos_s, in_=posr[:]).then_inc(s_go, 16)
            e.wait_ge(s_go, bump("go", 16))
            e.dma_start(out=ppf[:],
                        in_=pos_s.rearrange("h (g q) -> (h g) q", g=8)).then_inc(s_gl, 16)
        gl_pp = bump("gl", 16)

        @blk.vector
        def _(e, gl_pp=gl_pp, base=n["dv"]):
            e.wait_ge(s_gl, gl_pp)
            e.tensor_copy(pph[:], ppf[:]).then_inc(s_dv, 1)
            e.wait_ge(s_dv, base + 1)
            e.tensor_sub(ppt[:], ppf[:], pph[:]).then_inc(s_dv, 1)
            e.wait_ge(s_dv, base + 2)
            e.tensor_copy(ppm[:], ppt[:]).then_inc(s_dv, 1)
            e.wait_ge(s_dv, base + 3)
            e.tensor_sub(ppt[:], ppt[:], ppm[:]).then_inc(s_dv, 1)
            e.wait_ge(s_dv, base + 4)
            e.tensor_copy(ppl[:], ppt[:]).then_inc(s_dv, 1)
        bump("dv", 5)
        dv_p3 = n["dv"]

        @blk.gpsimd
        def _(e, dv_p3=dv_p3):
            e.wait_ge(s_dv, dv_p3)
            for h in range(NH):
                for r, t_ in ((0, pph), (1, ppm), (2, ppl)):
                    e.dma_start(out=pos3_s[h, r].rearrange("(g q) -> g q", g=8),
                                in_=t_[h * 8:(h + 1) * 8, :]).then_inc(s_go, 16)
        bump("go", 48 * 16)

        @blk.gpsimd
        def _(e, wpe=n["pe"]):
            e.wait_ge(s_pe, wpe)
            e.dma_start(out=wt1[:],
                        in_=w_in_d[:, 0:2048].rearrange("(c p) j -> p c j", p=128)).then_inc(s_lw, 16)
            e.dma_start(out=wt2[:],
                        in_=w_in_d[:, 6144:8192].rearrange("(c p) j -> p c j", p=128)).then_inc(s_lw, 16)
        bump("lw", 32)
        a2d_lw = n["lw"]

        @blk.scalar
        def _(e, w=a2d_lw):
            e.wait_ge(s_lw, w)
            e.sem_inc(s_sc, 1)
        a2d_sc = bump("sc")

        qp_free = [0, 0]
        qev_go = [0, 0]
        qevac_dv = [0, 0]
        for tt in range(8):
            b = tt % 2
            tok0 = tt * 512

            @blk.sync
            def _(e, b=b, tok0=tok0, w=qp_free[b], wsc=a2d_sc):
                e.wait_ge(s_sc, wsc)
                if w:
                    e.wait_ge(s_pe, w)
                e.dma_start(out=xnt_in[b][:],
                            in_=xno_s[:, :, tok0:tok0 + 512].rearrange("c p q -> p c q")).then_inc(s_li[b], 16)
            ld_qp = bump(f"li{b}", 16)

            for which, wsrc, dst_s in ((0, wt1, q_s), (1, wt2, p_s)):
                for h in range(NH):
                    @blk.tensor
                    def _(e, b=b, h=h, wsrc=wsrc, ld_qp=ld_qp, wdv=qevac_dv[h % 2]):
                        e.wait_ge(s_li[b], ld_qp)
                        if wdv:
                            e.wait_ge(s_dv, wdv)
                        for c in range(DC):
                            ins = e.matmul(ps_a[h % 2][:, 0:512],
                                           wsrc[:, c, h * 128:(h + 1) * 128],
                                           xnt_in[b][:, c, :],
                                           start=(c == 0), stop=(c == DC - 1))
                        ins.then_inc(s_pe, 1)
                    pe_q = bump("pe")

                    @blk.vector
                    def _(e, h=h, which=which, pe_q=pe_q, wgo=qev_go[h % 2]):
                        e.wait_ge(s_pe, pe_q)
                        if wgo:
                            e.wait_ge(s_sk[h % 2], wgo)
                        if which == 0:
                            e.tensor_scalar(out=qevac[h % 2][:], in0=ps_a[h % 2][:, 0:512],
                                            scalar1=bcol[:, h:h + 1],
                                            scalar2=alp_b[:, h:h + 1],
                                            op0=ALU.add, op1=ALU.mult).then_inc(s_dv, 1)
                        else:
                            e.tensor_scalar(out=qevac[h % 2][:], in0=ps_a[h % 2][:, 0:512],
                                            scalar1=bcol[:, 48 + h:49 + h],
                                            scalar2=None, op0=ALU.add).then_inc(s_dv, 1)
                    qevac_dv[h % 2] = bump("dv")
                    dv_q = n["dv"]

                    @blk.gpsimd
                    def _(e, h=h, tok0=tok0, dst_s=dst_s, dv_q=dv_q):
                        e.wait_ge(s_dv, dv_q)
                        e.dma_start(out=dst_s[h][:, tok0:tok0 + 512],
                                    in_=qevac[h % 2][:]).then_inc(s_sk[h % 2], 16)
                    qev_go[h % 2] = bump(f"sk{h % 2}", 16)
                qp_free[b] = n["pe"]

        a2_tot = (n["go"], n["sk0"], n["sk1"], n["sv0"], n["sv1"])

        p2.close()

        p3s = ExitStack()
        sb3 = lambda nm, s, dt: p3s.enter_context(nc.sbuf_tensor(nm, s, dt))
        ktA = sb3("ktA", [128, TF], BF16)
        vA = sb3("vA", [128, 64, DH], BF16)
        qt2 = [sb3(f"qt2{i}", [128, 512], BF16) for i in range(2)]
        p3A = sb3("p3A", [3, TF], BF16)
        cache = sb3("cache", [128, 2 * TF], F32)
        mloc = sb3("mloc", [128, 2, 8], F32)
        mrun = sb3("mrun", [128, 2], F32)
        negm = sb3("negm", [128, 2], F32)
        probs = [sb3(f"probs{i}", [128, TF], BF16) for i in range(4)]
        ptsb = [sb3(f"ptsb{i}", [128, 1024], BF16) for i in range(2)]
        lcols = sb3("lcols", [128, 2, 4], F32)
        o_sb = [sb3(f"o_sb{i}", [128, 512], BF16) for i in range(2)]
        maskt = sb3("maskt", [128, 4, 1024], BF16)

        a2_ld = n["ld"]
        mask_dv = [0]
        exp_sc = [0, 0, 0, 0]
        tpose_pe = [0, 0, 0, 0]
        dance_go = [0, 0]
        pv_mark = [0, 0]
        ts2_dv = {}
        osb_dv = [0, 0]
        ptsb_sc = [None, None]
        oev_dv = 0
        linv_dv = 0

        def U_A(h, s, c, kp, ld_h, ld_m):
            B = SLOT_BUD[s]
            par = c % 2
            coff = par * TF
            last = kp == B // 2 - 1

            @blk.tensor
            def _(e, wdv=ts2_dv.get(kp % 2, 0)):
                e.wait_ge(s_lh, ld_h)
                e.wait_ge(s_lm[s % 2], ld_m)
                if wdv:
                    e.wait_ge(s_dv, wdv)
                for half in range(2):
                    kt = 2 * kp + half
                    sl = slice(half * 512, (half + 1) * 512)
                    e.matmul(ps_a[kp % 2][:, sl],
                             qt2[s % 2][:, c * 128:(c + 1) * 128],
                             ktA[:, kt * 512:(kt + 1) * 512],
                             start=True, stop=False)
                for half in range(2):
                    kt = 2 * kp + half
                    sl = slice(half * 512, (half + 1) * 512)
                    ins = e.matmul(ps_a[kp % 2][:, sl], ones3[:],
                                   p3A[:, kt * 512:(kt + 1) * 512],
                                   start=False, stop=True)
                ins.then_inc(s_pe, 1)
            pe_qk = bump("pe")

            @blk.vector
            def _(e, pe_qk=pe_qk, base=n["dv"], wexp=exp_sc[(c + 2) % 4],
                  wgo=dance_go[s % 2]):
                e.wait_ge(s_pe, pe_qk)
                if kp == 0:
                    if wexp:
                        e.wait_ge(s_sc, wexp)
                    if wgo and c == 0:
                        e.wait_ge(s_gd[s % 2], wgo)
                if last:
                    e.wait_ge(s_lm[s % 2], ld_m)
                    e.tensor_add(ps_a[kp % 2][:], ps_a[kp % 2][:],
                                 maskt[:, c, :]).then_inc(s_dv, 1)
                    e.wait_ge(s_dv, base + 1)
                e.tensor_scalar(out=cache[:, coff + kp * 1024: coff + (kp + 1) * 1024],
                                in0=ps_a[kp % 2][:], scalar1=0.0, scalar2=seedc[:],
                                op0=ALU.add, op1=ALU.max,
                                accum_out=mloc[:, par, kp:kp + 1]).then_inc(s_dv, 1)
            bump("dv", 2 if last else 1)
            ts2_dv[kp % 2] = n["dv"]
            if last:
                mask_dv[0] = n["dv"]

        def U_MX(h, s, c):
            B = SLOT_BUD[s]
            par = c % 2

            @blk.vector
            def _(e, base=n["dv"], wexp=exp_sc[(c + 2) % 4]):
                e.wait_ge(s_dv, base)
                if wexp:
                    e.wait_ge(s_sc, wexp)
                e.reduce_max(mrun[:, par:par + 1], mloc[:, par, 0:B // 2],
                             axis=AX.X).then_inc(s_dv, 1)
                e.wait_ge(s_dv, base + 1)
                e.tensor_scalar_mul(negm[:, par:par + 1], mrun[:, par:par + 1],
                                    -1.0).then_inc(s_dv, 1)
            bump("dv", 2)

        def U_EXP(h, s, c):
            B = SLOT_BUD[s]
            par = c % 2
            coff = par * TF
            dv_exp = n["dv"]

            @blk.scalar
            def _(e, dv_exp=dv_exp, wpe=tpose_pe[c]):
                e.wait_ge(s_dv, dv_exp)
                if wpe:
                    e.wait_ge(s_pe, wpe)
                e.activation(out=probs[c][:, 0:B * 512],
                             in_=cache[:, coff: coff + B * 512],
                             func=AF.Exp, bias=negm[:, par:par + 1],
                             accum_out=lcols[:, s % 2, c:c + 1]).then_inc(s_sc, 1)
            exp_sc[c] = bump("sc")

        def mk_T(h, s, pair, kg):
            B = SLOT_BUD[s]
            wexp0 = exp_sc[2 * pair]
            wexp1 = exp_sc[2 * pair + 1]

            @blk.tensor
            def _(e, wexp=max(wexp0, wexp1), wsc=ptsb_sc[kg % 2], wdv=oev_box[0],
                  wpv=pv_mark[kg % 2]):
                e.wait_ge(s_sc, wexp)
                if wsc:
                    e.wait_ge(s_dv if wsc[0] == "dv" else s_sc, wsc[1])
                if wpv:
                    e.wait_ge(s_pe, wpv)
                if kg == 0 and pair == 0 and wdv:
                    e.wait_ge(s_dv, wdv)
                for kcl in range(4):
                    kc = kg * 4 + kcl
                    for par2 in range(2):
                        ins = e.transpose(
                            ps_b[kg % 2][:, kcl * 256 + par2 * 128: kcl * 256 + (par2 + 1) * 128],
                            probs[2 * pair + par2][:, kc * 128:(kc + 1) * 128], ident[:])
                ins.then_inc(s_pe, 1)
            pe_t = bump("pe")
            if kg == B - 1:
                tpose_pe[2 * pair] = tpose_pe[2 * pair + 1] = pe_t

            @blk.vector
            def _(e, pe_t=pe_t):
                e.wait_ge(s_pe, pe_t)
                e.tensor_copy(ptsb[kg % 2][:], ps_b[kg % 2][:]).then_inc(s_dv, 1)
            ptsb_sc[kg % 2] = ("dv", bump("dv"))

        def mk_PV(h, s, pair, kg):
            B = SLOT_BUD[s]

            @blk.tensor
            def _(e, wsc=ptsb_sc[kg % 2]):
                e.wait_ge(s_dv if wsc[0] == "dv" else s_sc, wsc[1])
                for kcl in range(4):
                    kc = kg * 4 + kcl
                    ins = e.matmul(ps_c[:, pair * 256:(pair + 1) * 256],
                                   vA[:, kc, :],
                                   ptsb[kg % 2][:, kcl * 256:(kcl + 1) * 256],
                                   start=(kg == 0 and kcl == 0),
                                   stop=(kg == B - 1 and kcl == 3))
                ins.then_inc(s_pe, 1)
            pv_mark[kg % 2] = bump("pe")

        def U_DANCE(h, s, ladd):
            pv_done = n["pe"]

            @blk.gpsimd
            def _(e, ladd=ladd):
                e.wait_ge(s_sc, ladd)
                for c4 in range(4):
                    e.dma_start(out=l_s[s, h, c4][:, None],
                                in_=lcols[:, s % 2, c4:c4 + 1]).then_inc(s_gd[s % 2], 16)
            bump(f"gd{s % 2}", 64)

            @blk.vector
            def _(e, pv_done=pv_done, wgo=osb_dv[s % 2]):
                e.wait_ge(s_pe, pv_done)
                if wgo:
                    e.wait_ge(s_sv[s % 2], wgo)
                e.tensor_copy(o_sb[s % 2][:], ps_c[:]).then_inc(s_dv, 1)
            oev = bump("dv")

            @blk.gpsimd
            def _(e, wdv=oev):
                e.wait_ge(s_dv, wdv)
                e.dma_start(out=o_s[h][:, s * 512:(s + 1) * 512],
                            in_=o_sb[s % 2][:]).then_inc(s_sv[s % 2], 16)
            osb_dv[s % 2] = bump(f"sv{s % 2}", 16)
            dance_go[s % 2] = n[f"gd{s % 2}"]
            return oev

        def interleave(primary, filler):
            fi = 0
            for i, u in enumerate(primary):
                u()
                take = 2 if len(filler) - fi > 2 * (len(primary) - i - 1) else 1
                for _ in range(take):
                    if fi < len(filler):
                        filler[fi](); fi += 1
            while fi < len(filler):
                filler[fi](); fi += 1

        oev_box = [oev_dv]
        linv_dv = [linv_dv]
        for h in range(NH):
            @blk.sync
            def _(e, h=h, wt=a2_tot, wpe=n["pe"]):
                if h == 0:
                    e.wait_ge(s_go, wt[0])
                    e.wait_ge(s_sk[0], wt[1])
                    e.wait_ge(s_sk[1], wt[2])
                    e.wait_ge(s_sv[0], wt[3])
                    e.wait_ge(s_sv[1], wt[4])
                else:
                    e.wait_ge(s_pe, wpe)
                e.dma_start(out=ktA[:], in_=kp_s[h]).then_inc(s_lh, 16)
                e.dma_start(out=vA[:], in_=v_s[h].rearrange("a p d -> p a d")).then_inc(s_lh, 16)
                e.dma_start(out=p3A[:], in_=pos3_s[h]).then_inc(s_lh, 16)
            ld_h = bump("lh", 48)

            pending = []
            for s in range(8):
                B = SLOT_BUD[s]

                @blk.sync
                def _(e, s=s, h=h, wdv=mask_dv[0], wt=a2_tot):
                    if h == 0:
                        e.wait_ge(s_go, wt[0])
                        e.wait_ge(s_sk[0], wt[1])
                        e.wait_ge(s_sk[1], wt[2])
                        e.wait_ge(s_sv[0], wt[3])
                        e.wait_ge(s_sv[1], wt[4])
                    if wdv:
                        e.wait_ge(s_dv, wdv)
                    e.dma_start(out=maskt[:],
                                in_=masks_d[s].rearrange("c p q -> p c q")).then_inc(s_lm[s % 2], 16)
                    e.dma_start(out=qt2[s % 2][:],
                                in_=q_s[h][:, s * 512:(s + 1) * 512]).then_inc(s_lm[s % 2], 16)
                ld_m = bump(f"lm{s % 2}", 32)

                A = {c: [(lambda h=h, s=s, c=c, kp=kp, ld_h=ld_h, ld_m=ld_m:
                          U_A(h, s, c, kp, ld_h, ld_m)) for kp in range(B // 2)]
                     for c in range(4)}
                interleave(A[0]
                           + [lambda h=h, s=s: (U_MX(h, s, 0), U_EXP(h, s, 0))]
                           + A[1]
                           + [lambda h=h, s=s: (U_MX(h, s, 1), U_EXP(h, s, 1))],
                           pending)
                pair0 = []
                for kg in range(B):
                    pair0.append(lambda h=h, s=s, kg=kg: mk_T(h, s, 0, kg))
                    if kg >= 1:
                        pair0.append(lambda h=h, s=s, kg=kg - 1: mk_PV(h, s, 0, kg))
                pair0.append(lambda h=h, s=s, kg=B - 1: mk_PV(h, s, 0, kg))
                interleave(A[2]
                           + [lambda h=h, s=s: (U_MX(h, s, 2), U_EXP(h, s, 2))]
                           + A[3]
                           + [lambda h=h, s=s: (U_MX(h, s, 3), U_EXP(h, s, 3))],
                           pair0)
                pending = []
                for kg in range(B):
                    pending.append(lambda h=h, s=s, kg=kg: mk_T(h, s, 1, kg))
                    if kg >= 1:
                        pending.append(lambda h=h, s=s, kg=kg - 1: mk_PV(h, s, 1, kg))
                pending.append(lambda h=h, s=s, kg=B - 1: mk_PV(h, s, 1, kg))
                pending.append(lambda h=h, s=s, la=exp_sc[3]:
                               oev_box.__setitem__(0, U_DANCE(h, s, la)))
            for u in pending:
                u()
            pending = []

        a3_sv = (n["sv0"], n["sv1"])
        a3_pe = n["pe"]

        @blk.tensor
        def _(e, w1=n["dv"], w2=n["sc"], w3=a3_sv, w4=n["gl"], w5=n["lh"],
              w6=(n["lm0"], n["lm1"]), w7=(n["gd0"], n["gd1"])):
            e.wait_ge(s_dv, w1)
            e.wait_ge(s_sc, w2)
            e.wait_ge(s_sv[0], w3[0])
            e.wait_ge(s_sv[1], w3[1])
            e.wait_ge(s_gl, w4)
            e.wait_ge(s_lh, w5)
            e.wait_ge(s_lm[0], w6[0])
            e.wait_ge(s_lm[1], w6[1])
            e.wait_ge(s_gd[0], w7[0])
            e.wait_ge(s_gd[1], w7[1])
            e.sem_inc(s_pe, 1)
        a4bar = bump("pe")
        p3s.close()

        p4 = ExitStack()
        sb4 = lambda nm, s, dt: p4.enter_context(nc.sbuf_tensor(nm, s, dt))
        o_ep = sb4("o_ep", [128, NH, 512], BF16)
        p_ep = sb4("p_ep", [128, NH, 512], BF16)
        sil_ep = sb4("sil_ep", [128, NH, 512], BF16)
        g_ep = sb4("g_ep", [128, NH, 512], BF16)
        woT = sb4("woT", [128, NH, 1024], BF16)
        out32 = sb4("out32", [128, D], F32)
        xo_ep = sb4("xo_ep", [128, D], F32)
        xmid_sb = sb4("xmid_sb", [128, D], F32)
        xng = sb4("xng", [128, D], BF16)
        x2t_a4 = sb4("x2t_a4", [128, D], F32)
        xngT = sb4("xngT", [128, D], BF16)
        wol_b = sb4("wol_b", [128, D], F32)
        bol_b = sb4("bol_b", [128, D], F32)
        gwT = sb4("gwT_sb", [128, DC, 8], BF16)
        lbb_b = sb4("lbb_b", [128, 8], F32)
        gsc = sb4("gsc", [128, 8], F32)
        max8 = sb4("max8", [128, 8], F32)
        idx2 = sb4("idx2", [128, 8], mybir.dt.uint32)
        gt = sb4("gt", [128, 8], F32)
        Wc = sb4("Wc", [128, 8], F32)
        Mc = sb4("Mc", [128, 8], F32)
        SCacc = sb4("SCacc", [8, 2], F32)
        sc2 = sb4("sc2", [8, 2], F32)

        @blk.sync
        def _(e, wbar=a4bar):
            e.wait_ge(s_pe, wbar)
            e.dma_start(out=wol_b[:], in_=olw_d[None, :].broadcast_to((128, D))).then_inc(s_ld, 16)
            e.dma_start(out=bol_b[:], in_=olb_d[None, :].broadcast_to((128, D))).then_inc(s_ld, 16)
            e.dma_start(out=lbb_b[:], in_=lbb_d.broadcast_to((128, 8))).then_inc(s_ld, 16)
        ld_a4 = bump("ld", 48)

        @blk.vector
        def _(e, wbar=a4bar):
            e.wait_ge(s_pe, wbar)
            e.memset(SCacc[:], 0.0).then_inc(s_dv, 1)
        scacc_dv = bump("dv")

        @blk.gpsimd
        def _(e, wbar=a4bar):
            e.wait_ge(s_pe, wbar)
            e.dma_start(out=woT[:],
                        in_=w_out_d.rearrange("(j p) q -> p j q", p=128)).then_inc(s_lw, 16)
            e.dma_start(out=gwT[:],
                        in_=gw_d.rearrange("(c p) e -> p c e", p=128)).then_inc(s_lw, 16)
        bump("lw", 32)
        a4_lw = n["lw"]

        @blk.scalar
        def _(e, w=a4_lw):
            e.wait_ge(s_lw, w)
            e.sem_inc(s_sc, 1)
        a4prep_sc = bump("sc")

        lrow_all = sb4("lrow_all", [1, NH * 512], F32)
        linv_ep = sb4("linv_ep", [128, NH, 512], F32)

        n_xm_go = [n["go"]]
        for s in range(8):
            @blk.sync
            def _(e, s=s, wpe=n["pe"], wgo=a3_sv, wld=ld_a4):
                e.wait_ge(s_ld, wld)
                e.wait_ge(s_sv[0], wgo[0])
                e.wait_ge(s_sv[1], wgo[1])
                if s >= 1:
                    e.wait_ge(s_pe, wpe)
                e.dma_start(out=o_ep[:],
                            in_=o_s[:, :, s * 512:(s + 1) * 512].rearrange("h p q -> p h q")).then_inc(s_ld, 16)
                e.dma_start(out=p_ep[:],
                            in_=p_s[:, :, s * 512:(s + 1) * 512].rearrange("h p q -> p h q")).then_inc(s_ld, 16)
            ld_ep = bump("ld", 32)

            @blk.gpsimd
            def _(e, s=s, wdv=n["dv"], wbar=a4bar):
                if s == 0:
                    e.wait_ge(s_pe, wbar)
                e.wait_ge(s_dv, wdv)
                e.dma_start(out=lrow_all[:],
                            in_=l_s[s].rearrange("h c p -> (h c p)")[None, :]).then_inc(s_gl, 16)
            gl_lr = bump("gl", 16)

            @blk.vector
            def _(e, gl_lr=gl_lr):
                e.wait_ge(s_gl, gl_lr)
                e.reciprocal(lrow_all[:], lrow_all[:]).then_inc(s_dv, 1)
            dv_lr = bump("dv")

            for hg in range(8):
                @blk.tensor
                def _(e, hg=hg, dv_lr=dv_lr, wsc=n["sc"] - (1 if hg >= 2 else 0)):
                    e.wait_ge(s_dv, dv_lr)
                    e.wait_ge(s_sc, wsc)
                    e.matmul(ps_a[hg % 2][:, 0:512], ones128f[:],
                             lrow_all[:, (2 * hg) * 512:(2 * hg + 1) * 512],
                             start=True, stop=True)
                    ins = e.matmul(ps_a[hg % 2][:, 512:1024], ones128f[:],
                                   lrow_all[:, (2 * hg + 1) * 512:(2 * hg + 2) * 512],
                                   start=True, stop=True)
                    ins.then_inc(s_pe, 1)
                pe_lb = bump("pe")

                @blk.scalar
                def _(e, hg=hg, pe_lb=pe_lb):
                    e.wait_ge(s_pe, pe_lb)
                    e.copy(linv_ep[:, 2 * hg:2 * hg + 2, :].rearrange("p h q -> p (h q)"),
                           ps_a[hg % 2][:]).then_inc(s_sc, 1)
                bump("sc")
            sc_linv = n["sc"]

            @blk.scalar
            def _(e, ld_ep=ld_ep, w=a4prep_sc, w2=sc_linv):
                e.wait_ge(s_ld, ld_ep)
                e.wait_ge(s_sc, max(w, w2))
                e.activation(out=sil_ep[:].rearrange("p h q -> p (h q)"),
                             in_=p_ep[:].rearrange("p h q -> p (h q)"),
                             func=AF.Silu).then_inc(s_sc, 1)
            sc_sil = bump("sc")

            @blk.vector
            def _(e, sc_sil=sc_sil, base=n["dv"]):
                e.wait_ge(s_sc, sc_sil)
                e.tensor_mul(g_ep[:].rearrange("p h q -> p (h q)"),
                             sil_ep[:].rearrange("p h q -> p (h q)"),
                             o_ep[:].rearrange("p h q -> p (h q)")).then_inc(s_dv, 1)
                e.wait_ge(s_dv, base + 1)
                e.tensor_mul(g_ep[:].rearrange("p h q -> p (h q)"),
                             g_ep[:].rearrange("p h q -> p (h q)"),
                             linv_ep[:].rearrange("p h q -> p (h q)")).then_inc(s_dv, 1)
            dv_g = bump("dv", 2)

            for tc in range(4):
                for oh in range(2):
                    psx = ps_c if oh == 0 else ps_d
                    @blk.tensor
                    def _(e, tc=tc, oh=oh, psx=psx, dv_g=dv_g, w=n["sc"],
                          wdv2=n["dv"]):
                        e.wait_ge(s_dv, max(dv_g, wdv2))
                        if w:
                            e.wait_ge(s_sc, w)
                        for j in range(NH):
                            ins = e.matmul(psx[:],
                                           g_ep[:, j, tc * 128:(tc + 1) * 128],
                                           woT[:, j, oh * 512:(oh + 1) * 512],
                                           start=(j == 0), stop=(j == NH - 1))
                        ins.then_inc(s_pe, 1)
                    pe_o = bump("pe")

                    @blk.scalar
                    def _(e, oh=oh, psx=psx, pe_o=pe_o):
                        e.wait_ge(s_pe, pe_o)
                        e.copy(out32[:, oh * 512:(oh + 1) * 512], psx[:]).then_inc(s_sc, 1)
                    bump("sc")
                sc_wout = n["sc"]

                @blk.sync
                def _(e, s=s, tc=tc, wdv=n["dv"]):
                    e.wait_ge(s_dv, wdv)
                    e.dma_start(out=xo_ep[:],
                                in_=xo_d[(s * 4 + tc) * 128:(s * 4 + tc + 1) * 128, :]).then_inc(s_ld, 16)
                ld_xo = bump("ld", 16)

                @blk.scalar
                def _(e, sc_wout=sc_wout):
                    e.wait_ge(s_sc, sc_wout)
                    e.activation(out=x2t_a4[:], in_=out32[:], func=AF.Square,
                                 accum_out=stats[:, 1:2]).then_inc(s_sc, 1)
                sc_sq = bump("sc")

                @blk.vector
                def _(e, sc_sq=sc_sq, base=n["dv"]):
                    e.wait_ge(s_sc, sc_sq)
                    e.reduce_sum(stats[:, 0:1], out32[:], axis=AX.X).then_inc(s_dv, 1)
                    e.wait_ge(s_dv, base + 1)
                    e.tensor_scalar_mul(stats[:, 2:3], stats[:, 0:1], 1.0 / D).then_inc(s_dv, 1)
                    e.wait_ge(s_dv, base + 2)
                    e.tensor_mul(stats[:, 3:4], stats[:, 2:3], stats[:, 2:3]).then_inc(s_dv, 1)
                    e.wait_ge(s_dv, base + 3)
                    e.scalar_tensor_tensor(out=stats[:, 3:4], in0=stats[:, 1:2],
                                           scalar=1.0 / D, in1=stats[:, 3:4],
                                           op0=ALU.mult, op1=ALU.subtract).then_inc(s_dv, 1)
                    e.wait_ge(s_dv, base + 4)
                    e.tensor_scalar_add(stats[:, 4:5], stats[:, 3:4], EPS).then_inc(s_dv, 1)
                bump("dv", 5)
                dv_v1 = n["dv"]

                @blk.scalar
                def _(e, dv_v1=dv_v1):
                    e.wait_ge(s_dv, dv_v1)
                    e.sqrt(stats[:, 5:6], stats[:, 4:5]).then_inc(s_sc, 1)
                sc_rt = bump("sc")

                @blk.vector
                def _(e, s=s, tc=tc, sc_rt=sc_rt, ld_xo=ld_xo, base=n["dv"]):
                    e.wait_ge(s_sc, sc_rt)
                    e.reciprocal(stats[:, 6:7], stats[:, 5:6]).then_inc(s_dv, 1)
                    e.wait_ge(s_dv, base + 1)
                    e.tensor_scalar(out=out32[:], in0=out32[:], scalar1=stats[:, 2:3],
                                    scalar2=stats[:, 6:7], op0=ALU.subtract,
                                    op1=ALU.mult).then_inc(s_dv, 1)
                    e.wait_ge(s_dv, base + 2)
                    e.tensor_mul(out32[:], out32[:], wol_b[:]).then_inc(s_dv, 1)
                    e.wait_ge(s_dv, base + 3)
                    e.tensor_add(out32[:], out32[:], bol_b[:]).then_inc(s_dv, 1)
                    e.wait_ge(s_dv, base + 4)
                    e.wait_ge(s_ld, ld_xo)
                    e.wait_ge(s_go, n_xm_go[0])
                    e.tensor_add(xmid_sb[:], out32[:], xo_ep[:]).then_inc(s_dv, 1)
                bump("dv", 5)
                dv_xm = n["dv"]

                @blk.gpsimd
                def _(e, s=s, tc=tc, dv_xm=dv_xm):
                    e.wait_ge(s_dv, dv_xm)
                    e.dma_start(out=xmid_d[(s * 4 + tc) * 128:(s * 4 + tc + 1) * 128, :],
                                in_=xmid_sb[:]).then_inc(s_go, 16)
                n_xm_go[0] = bump("go", 16)

                @blk.scalar
                def _(e, dv_xm=dv_xm):
                    e.wait_ge(s_dv, dv_xm)
                    e.activation(out=x2t_a4[:], in_=xmid_sb[:], func=AF.Square,
                                 accum_out=stats[:, 1:2]).then_inc(s_sc, 1)
                sc_sq2 = bump("sc")

                @blk.vector
                def _(e, sc_sq2=sc_sq2, base=n["dv"]):
                    e.wait_ge(s_sc, sc_sq2)
                    e.reduce_sum(stats[:, 0:1], xmid_sb[:], axis=AX.X).then_inc(s_dv, 1)
                    e.wait_ge(s_dv, base + 1)
                    e.tensor_scalar_mul(stats[:, 2:3], stats[:, 0:1], 1.0 / D).then_inc(s_dv, 1)
                    e.wait_ge(s_dv, base + 2)
                    e.tensor_mul(stats[:, 3:4], stats[:, 2:3], stats[:, 2:3]).then_inc(s_dv, 1)
                    e.wait_ge(s_dv, base + 3)
                    e.scalar_tensor_tensor(out=stats[:, 3:4], in0=stats[:, 1:2],
                                           scalar=1.0 / D, in1=stats[:, 3:4],
                                           op0=ALU.mult, op1=ALU.subtract).then_inc(s_dv, 1)
                    e.wait_ge(s_dv, base + 4)
                    e.tensor_scalar_add(stats[:, 4:5], stats[:, 3:4], EPS).then_inc(s_dv, 1)
                bump("dv", 5)
                dv_v2 = n["dv"]

                @blk.scalar
                def _(e, dv_v2=dv_v2):
                    e.wait_ge(s_dv, dv_v2)
                    e.sqrt(stats[:, 5:6], stats[:, 4:5]).then_inc(s_sc, 1)
                sc_rt2 = bump("sc")

                @blk.vector
                def _(e, sc_rt2=sc_rt2, base=n["dv"]):
                    e.wait_ge(s_sc, sc_rt2)
                    e.reciprocal(stats[:, 6:7], stats[:, 5:6]).then_inc(s_dv, 1)
                    e.wait_ge(s_dv, base + 1)
                    e.tensor_scalar(out=xng[:], in0=xmid_sb[:], scalar1=stats[:, 2:3],
                                    scalar2=stats[:, 6:7], op0=ALU.subtract,
                                    op1=ALU.mult).then_inc(s_dv, 1)
                bump("dv", 2)
                dv_xng = n["dv"]

                @blk.tensor
                def _(e, dv_xng=dv_xng, w=n["sc"]):
                    e.wait_ge(s_dv, dv_xng)
                    e.wait_ge(s_sc, w)
                    for k in range(DC):
                        ins = e.transpose(ps_b[0][:, k * 128:(k + 1) * 128],
                                          xng[:, k * 128:(k + 1) * 128], ident[:])
                    ins.then_inc(s_pe, 1)
                pe_xt = bump("pe")

                @blk.scalar
                def _(e, pe_xt=pe_xt):
                    e.wait_ge(s_pe, pe_xt)
                    e.copy(xngT[:], ps_b[0][:]).then_inc(s_sc, 1)
                sc_xt = bump("sc")

                @blk.tensor
                def _(e, sc_xt=sc_xt, w=n["dv"]):
                    e.wait_ge(s_sc, sc_xt)
                    e.wait_ge(s_dv, w)
                    for k in range(DC):
                        ins = e.matmul(ps_d[:, 0:8], xngT[:, k * 128:(k + 1) * 128],
                                       gwT[:, k, :], start=(k == 0), stop=(k == DC - 1))
                    ins.then_inc(s_pe, 1)
                pe_gt = bump("pe")

                @blk.vector
                def _(e, pe_gt=pe_gt):
                    e.wait_ge(s_pe, pe_gt)
                    e.tensor_add(gsc[:], ps_d[:, 0:8], lbb_b[:]).then_inc(s_dv, 1)
                dv_gl = bump("dv")

                @blk.scalar
                def _(e, dv_gl=dv_gl):
                    e.wait_ge(s_dv, dv_gl)
                    e.activation(out=gsc[:], in_=gsc[:], func=AF.Sigmoid).then_inc(s_sc, 1)
                sc_gs = bump("sc")

                @blk.vector
                def _(e, sc_gs=sc_gs, base=n["dv"]):
                    e.wait_ge(s_sc, sc_gs)
                    e.max(max8[:], gsc[:]).then_inc(s_dv, 1)
                    e.wait_ge(s_dv, base + 1)
                    e.max_index(idx2[:], max8[:], gsc[:]).then_inc(s_dv, 1)
                    e.wait_ge(s_dv, base + 2)
                    e.tensor_add(gt[:, 2:3], max8[:, 0:1], max8[:, 1:2]).then_inc(s_dv, 1)
                    e.wait_ge(s_dv, base + 3)
                    e.tensor_scalar_add(gt[:, 2:3], gt[:, 2:3], 1e-8).then_inc(s_dv, 1)
                    e.wait_ge(s_dv, base + 4)
                    e.reciprocal(gt[:, 3:4], gt[:, 2:3]).then_inc(s_dv, 1)
                    e.wait_ge(s_dv, base + 5)
                    e.tensor_mul(gt[:, 4:5], max8[:, 0:1], gt[:, 3:4]).then_inc(s_dv, 1)
                    e.tensor_mul(gt[:, 5:6], max8[:, 1:2], gt[:, 3:4]).then_inc(s_dv, 1)
                    e.wait_ge(s_dv, base + 7)
                    for e8 in range(8):
                        e.wait_ge(s_dv, base + 7 + e8 * 6)
                        e.tensor_single_scalar(gt[:, 0:1], idx2[:, 0:1], float(e8),
                                               ALU.is_equal).then_inc(s_dv, 1)
                        e.tensor_single_scalar(gt[:, 1:2], idx2[:, 1:2], float(e8),
                                               ALU.is_equal).then_inc(s_dv, 1)
                        e.wait_ge(s_dv, base + 9 + e8 * 6)
                        e.tensor_add(Mc[:, e8:e8 + 1], gt[:, 0:1], gt[:, 1:2]).then_inc(s_dv, 1)
                        e.tensor_scalar(out=gt[:, 0:1], in0=gt[:, 0:1],
                                        scalar1=gt[:, 4:5], scalar2=None,
                                        op0=ALU.mult).then_inc(s_dv, 1)
                        e.tensor_scalar(out=gt[:, 1:2], in0=gt[:, 1:2],
                                        scalar1=gt[:, 5:6], scalar2=None,
                                        op0=ALU.mult).then_inc(s_dv, 1)
                        e.wait_ge(s_dv, base + 12 + e8 * 6)
                        e.tensor_add(Wc[:, e8:e8 + 1], gt[:, 0:1], gt[:, 1:2]).then_inc(s_dv, 1)
                bump("dv", 7 + 8 * 6)
                dv_top2 = n["dv"]

                @blk.tensor
                def _(e, dv_top2=dv_top2):
                    e.wait_ge(s_dv, dv_top2)
                    e.matmul(ps_d[0:8, 16:17], Wc[:], onecol[:],
                             start=True, stop=True)
                    ins = e.matmul(ps_d[0:8, 17:18], Mc[:], onecol[:],
                                   start=True, stop=True)
                    ins.then_inc(s_pe, 1)
                pe_sc8 = bump("pe")

                @blk.vector
                def _(e, pe_sc8=pe_sc8, base=n["dv"]):
                    e.wait_ge(s_pe, pe_sc8)
                    e.tensor_add(SCacc[:, 0:1], SCacc[:, 0:1], ps_d[0:8, 16:17]).then_inc(s_dv, 1)
                    e.wait_ge(s_dv, base + 1)
                    e.tensor_add(SCacc[:, 1:2], SCacc[:, 1:2], ps_d[0:8, 17:18]).then_inc(s_dv, 1)
                bump("dv", 2)

        a4_dv = n["dv"]
        a4_go = n["go"]

        @blk.gpsimd
        def _(e, wdv=a4_dv, wgo=a4_go):
            e.wait_ge(s_dv, wdv)
            e.dma_start(out=part_d.rearrange("c p -> p c"), in_=SCacc[:],
                        allow_slow_non_contiguous=True).then_inc(s_go, 16)
        bump("go", 16)

        @blk.gpsimd
        def _(e, wgo=n["go"]):
            if DBG_A >= 1:
                e.dma_start(out=dbg["kp"], in_=kp_s).then_inc(s_go, 16)
                e.dma_start(out=dbg["v"], in_=v_s).then_inc(s_go, 16)
                e.dma_start(out=dbg["q"], in_=q_s).then_inc(s_go, 16)
                e.dma_start(out=dbg["p"], in_=p_s).then_inc(s_go, 16)
                e.dma_start(out=dbg["pos3"], in_=pos3_s).then_inc(s_go, 16)
                e.dma_start(out=dbg["xnf"], in_=xnf_s).then_inc(s_go, 16)
                bump("go", 96)
            if DBG_A >= 2:
                e.dma_start(out=dbg["o"], in_=o_s).then_inc(s_go, 16)
                bump("go", 16)
            e.wait_ge(s_go, n["go"])
        p4.close()

    return nc





TCH = 1024


def build_B():
    nc = bass.Bass("TRN2", target_bir_lowering=False)
    dr = lambda nm, s, dt, k="ExternalInput": nc.dram_tensor(nm, s, dt, kind=k).ap()

    xm_d = dr("xm", [TO, D], F32)
    pall_d = dr("pall", [8, 16], F32)
    gw_d = dr("gwT", [D, 8], F32)
    lbb_d = dr("lbb", [1, 8], F32)
    egw_d = dr("egwT", [8, D, 512], F32)
    euw_d = dr("euwT", [8, D, 512], F32)
    edw_d = dr("edwT", [8, 512, D], F32)
    sgw_d = dr("sgwT", [D, D], F32)
    suw_d = dr("suwT", [D, D], F32)
    sdw_d = dr("sdwT", [D, D], F32)
    ident_d = dr("ident", [128, 128], BF16)
    out_d = dr("out", [TO, D], F32, "ExternalOutput")

    scr = lambda nm, s, dt: nc.dram_tensor(nm, s, dt, kind="Internal").ap()
    wm_s = scr("wm_s", [1, 8], F32)

    gctx = ExitStack()
    gsb = lambda nm, s, dt: gctx.enter_context(nc.sbuf_tensor(nm, s, dt))
    pst = lambda nm, s, dt: gctx.enter_context(nc.psum_tensor(nm, s, dt))
    sem = lambda nm: gctx.enter_context(nc.semaphore(nm))

    ident = gsb("ident_s", [128, 128], BF16)
    part = gsb("part_s", [8, 16], F32)
    ones8 = gsb("ones8", [8, 1], F32)
    wvec = gsb("wvec", [8, 4], F32)
    wme_b = gsb("wme_b", [128, 8], F32)
    lbb_b = gsb("lbb_b", [128, 8], F32)
    gwT = gsb("gwT_sb", [128, DC, 8], BF16)
    stats = gsb("stats", [128, 8], F32)
    x2t = gsb("x2t", [128, D], F32)
    xin = [gsb(f"xin{i}", [128, D], F32) for i in range(2)]
    xng = [gsb(f"xng{i}", [128, D], BF16) for i in range(2)]
    xnfm = gsb("xnfm", [128, DC, TCH], BF16)
    lam = gsb("lam", [128, TO // 128, 8], F32)
    gsc = gsb("gsc", [128, 8], F32)
    max8 = gsb("max8", [128, 8], F32)
    idx2 = gsb("idx2", [128, 8], mybir.dt.uint32)
    gt = gsb("gt", [128, 8], F32)
    Mc = gsb("Mc", [128, 8], F32)
    wea = [gsb(f"wea{i}", [128, DC, 512], BF16) for i in range(2)]
    web = [gsb(f"web{i}", [128, DC, 512], BF16) for i in range(2)]
    wed = [gsb(f"wed{i}", [128, 4, D], BF16) for i in range(2)]
    wsa = gsb("wsa", [128, DC, D], BF16)
    wsb_ = gsb("wsb", [128, DC, D], BF16)
    wsd = gsb("wsd", [128, DC, D], BF16)
    hst = gsb("hst", [128, 512], BF16)
    hut = gsb("hut", [128, 512], BF16)
    hfm = gsb("hfm", [128, 8, TCH], BF16)
    acc = gsb("acc", [128, TCH // 128, D], F32)
    xmi = gsb("xmi", [128, D], F32)
    outt = gsb("outt", [128, D], F32)

    ps_a = [pst(f"ps_a{i}", [128, 1024], F32) for i in range(2)]
    ps_b = [pst(f"ps_b{i}", [128, 1024], BF16) for i in range(2)]
    ps_c = pst("ps_c", [128, 512], F32)
    ps_d = pst("ps_d", [128, 512], F32)

    s_lc = sem("s_lc")
    s_li = [sem("s_li0"), sem("s_li1")]
    s_lw = sem("s_lw")
    s_lwa = [sem("s_lwa0"), sem("s_lwa1")]
    s_lsh = sem("s_lsh")
    s_lx = sem("s_lx")
    s_pe = sem("s_pe")
    s_dv = sem("s_dv")
    s_sc = sem("s_sc")
    s_go = sem("s_go")
    s_gl = sem("s_gl")

    n = {"lc": 0, "li0": 0, "li1": 0, "lw": 0, "lwa0": 0, "lwa1": 0, "lx": 0,
         "lsh": 0, "pe": 0, "dv": 0, "sc": 0, "go": 0, "gl": 0}

    def bump(k, i=1):
        n[k] += i
        return n[k]

    with nc.Block() as blk:

        @blk.sync
        def _(e):
            e.dma_start(out=ident[:], in_=ident_d).then_inc(s_lc, 16)
            e.dma_start(out=part[:], in_=pall_d).then_inc(s_lc, 16)
            e.dma_start(out=lbb_b[:], in_=lbb_d.broadcast_to((128, 8))).then_inc(s_lc, 16)
        bump("lc", 48)

        @blk.vector
        def _(e, w=n["lc"]):
            e.wait_ge(s_lc, w)
            e.memset(ones8[:], 1.0).then_inc(s_dv, 1)
        dv0 = bump("dv")

        @blk.tensor
        def _(e, dv0=dv0):
            e.wait_ge(s_dv, dv0)
            e.matmul(ps_d[0:8, 0:1], part[:, 0:8], ones8[:], start=True, stop=True)
            ins = e.matmul(ps_d[0:8, 1:2], part[:, 8:16], ones8[:], start=True, stop=True)
            ins.then_inc(s_pe, 1)
        pe0 = bump("pe")

        @blk.vector
        def _(e, pe0=pe0, base=n["dv"]):
            e.wait_ge(s_pe, pe0)
            e.tensor_copy(wvec[:, 0:1], ps_d[0:8, 0:1]).then_inc(s_dv, 1)
            e.tensor_copy(wvec[:, 1:2], ps_d[0:8, 1:2]).then_inc(s_dv, 1)
            e.wait_ge(s_dv, base + 2)
            e.tensor_scalar_max(wvec[:, 2:3], wvec[:, 1:2], 1.0).then_inc(s_dv, 1)
            e.wait_ge(s_dv, base + 3)
            e.reciprocal(wvec[:, 2:3], wvec[:, 2:3]).then_inc(s_dv, 1)
            e.wait_ge(s_dv, base + 4)
            e.tensor_mul(wvec[:, 3:4], wvec[:, 0:1], wvec[:, 2:3]).then_inc(s_dv, 1)
        dv_wm = bump("dv", 5)

        @blk.gpsimd
        def _(e, dv_wm=dv_wm, base=n["gl"]):
            e.wait_ge(s_dv, dv_wm)
            e.dma_start(out=wm_s[0][:, None], in_=wvec[:, 3:4]).then_inc(s_gl, 16)
            e.wait_ge(s_gl, base + 16)
            e.dma_start(out=wme_b[:], in_=wm_s.broadcast_to((128, 8))).then_inc(s_gl, 16)
        gl_wm = bump("gl", 32)

        @blk.gpsimd
        def _(e):
            e.dma_start(out=gwT[:],
                        in_=gw_d.rearrange("(c p) e -> p c e", p=128)).then_inc(s_lw, 16)
        bump("lw", 16)

        @blk.scalar
        def _(e, w=n["lw"]):
            e.wait_ge(s_lw, w)
            e.sem_inc(s_sc, 1)
        bump("sc")

        b1_go = n["go"]

        @blk.gpsimd
        def _(e):
            e.dma_start(out=wsa[:],
                        in_=sgw_d.rearrange("(c p) q -> p c q", p=128)).then_inc(s_lsh, 16)
            e.dma_start(out=wsb_[:],
                        in_=suw_d.rearrange("(c p) q -> p c q", p=128)).then_inc(s_lsh, 16)
            e.dma_start(out=wsd[:],
                        in_=sdw_d.rearrange("(c p) q -> p c q", p=128)).then_inc(s_lsh, 16)
        lc_sh = bump("lsh", 48)

        psfree = {}
        ev_dv = [0, 0]
        ev_sc = [0, 0]
        wst_dv = {}
        out_go = 0
        for ch in range(TO // TCH):
            t0 = ch * TCH
            for tl in range(TCH // 128):
                t = ch * (TCH // 128) + tl
                b = t % 2

                @blk.sync
                def _(e, b=b, t=t, w=ev_dv[b]):
                    if w:
                        e.wait_ge(s_dv, w)
                    e.dma_start(out=xin[b][:], in_=xm_d[t * 128:(t + 1) * 128, :]).then_inc(s_li[b], 16)
                ld_t = bump(f"li{b}", 16)

                @blk.scalar
                def _(e, b=b, ld_t=ld_t):
                    e.wait_ge(s_li[b], ld_t)
                    e.activation(out=x2t[:], in_=xin[b][:], func=AF.Square,
                                 accum_out=stats[:, 1:2]).then_inc(s_sc, 1)
                sc_sq = bump("sc")

                @blk.vector
                def _(e, b=b, sc_sq=sc_sq, base=n["dv"]):
                    e.wait_ge(s_sc, sc_sq)
                    e.reduce_sum(stats[:, 0:1], xin[b][:], axis=AX.X).then_inc(s_dv, 1)
                    e.wait_ge(s_dv, base + 1)
                    e.tensor_scalar_mul(stats[:, 2:3], stats[:, 0:1], 1.0 / D).then_inc(s_dv, 1)
                    e.wait_ge(s_dv, base + 2)
                    e.tensor_mul(stats[:, 3:4], stats[:, 2:3], stats[:, 2:3]).then_inc(s_dv, 1)
                    e.wait_ge(s_dv, base + 3)
                    e.scalar_tensor_tensor(out=stats[:, 3:4], in0=stats[:, 1:2],
                                           scalar=1.0 / D, in1=stats[:, 3:4],
                                           op0=ALU.mult, op1=ALU.subtract).then_inc(s_dv, 1)
                    e.wait_ge(s_dv, base + 4)
                    e.tensor_scalar_add(stats[:, 4:5], stats[:, 3:4], EPS).then_inc(s_dv, 1)
                bump("dv", 5)
                dv_ab = n["dv"]

                @blk.scalar
                def _(e, dv_ab=dv_ab):
                    e.wait_ge(s_dv, dv_ab)
                    e.sqrt(stats[:, 5:6], stats[:, 4:5]).then_inc(s_sc, 1)
                sc_rt = bump("sc")

                @blk.vector
                def _(e, b=b, sc_rt=sc_rt, base=n["dv"]):
                    e.wait_ge(s_sc, sc_rt)
                    e.reciprocal(stats[:, 6:7], stats[:, 5:6]).then_inc(s_dv, 1)
                    e.wait_ge(s_dv, base + 1)
                    e.tensor_scalar(out=xng[b][:], in0=xin[b][:], scalar1=stats[:, 2:3],
                                    scalar2=stats[:, 6:7], op0=ALU.subtract,
                                    op1=ALU.mult).then_inc(s_dv, 1)
                bump("dv", 2)
                ev_dv[b] = n["dv"]
                dv_nrm = n["dv"]

                @blk.tensor
                def _(e, b=b, tl=tl, dv_nrm=dv_nrm, w=ev_sc[b], wpe=n["pe"]):
                    e.wait_ge(s_dv, dv_nrm)
                    if tl == 0:
                        e.wait_ge(s_pe, wpe)
                    if w:
                        e.wait_ge(s_sc, w)
                    for k in range(DC):
                        ins = e.transpose(ps_b[b][:, k * 128:(k + 1) * 128],
                                          xng[b][:, k * 128:(k + 1) * 128], ident[:])
                    ins.then_inc(s_pe, 1)
                pe_tr = bump("pe")

                @blk.scalar
                def _(e, b=b, tl=tl, pe_tr=pe_tr):
                    e.wait_ge(s_pe, pe_tr)
                    e.copy(xnfm[:, :, tl * 128:(tl + 1) * 128],
                           ps_b[b][:].rearrange("p (c q) -> p c q", c=DC)).then_inc(s_sc, 1)
                ev_sc[b] = bump("sc")
                sc_fm = n["sc"]

                @blk.tensor
                def _(e, tl=tl, sc_fm=sc_fm, wdv=n["dv"]):
                    e.wait_ge(s_sc, sc_fm)
                    e.wait_ge(s_dv, wdv)
                    for k in range(DC):
                        ins = e.matmul(ps_d[:, 0:8], xnfm[:, k, tl * 128:(tl + 1) * 128],
                                       gwT[:, k, :], start=(k == 0), stop=(k == DC - 1))
                    ins.then_inc(s_pe, 1)
                pe_gt = bump("pe")

                @blk.vector
                def _(e, pe_gt=pe_gt):
                    e.wait_ge(s_pe, pe_gt)
                    e.tensor_add(gsc[:], ps_d[:, 0:8], lbb_b[:]).then_inc(s_dv, 1)
                dv_gl = bump("dv")

                @blk.scalar
                def _(e, dv_gl=dv_gl):
                    e.wait_ge(s_dv, dv_gl)
                    e.activation(out=gsc[:], in_=gsc[:], func=AF.Sigmoid).then_inc(s_sc, 1)
                sc_gs = bump("sc")

                @blk.vector
                def _(e, t=t, sc_gs=sc_gs, wgl=gl_wm, base=n["dv"]):
                    e.wait_ge(s_sc, sc_gs)
                    e.wait_ge(s_gl, wgl)
                    e.max(max8[:], gsc[:]).then_inc(s_dv, 1)
                    e.wait_ge(s_dv, base + 1)
                    e.max_index(idx2[:], max8[:], gsc[:]).then_inc(s_dv, 1)
                    for e8 in range(8):
                        e.wait_ge(s_dv, base + 2 + e8 * 4)
                        e.tensor_single_scalar(gt[:, 0:1], idx2[:, 0:1], float(e8),
                                               ALU.is_equal).then_inc(s_dv, 1)
                        e.tensor_single_scalar(gt[:, 1:2], idx2[:, 1:2], float(e8),
                                               ALU.is_equal).then_inc(s_dv, 1)
                        e.wait_ge(s_dv, base + 4 + e8 * 4)
                        e.tensor_add(Mc[:, e8:e8 + 1], gt[:, 0:1], gt[:, 1:2]).then_inc(s_dv, 1)
                        e.wait_ge(s_dv, base + 5 + e8 * 4)
                        e.tensor_scalar(out=lam[:, t, e8:e8 + 1], in0=Mc[:, e8:e8 + 1],
                                        scalar1=wme_b[:, e8:e8 + 1], scalar2=None,
                                        op0=ALU.mult).then_inc(s_dv, 1)
                bump("dv", 2 + 8 * 4)

            @blk.vector
            def _(e, ch=ch, w=n["sc"], wo=out_go, base=n["dv"]):
                e.wait_ge(s_sc, w)
                if wo:
                    e.wait_ge(s_go, wo)
                e.memset(acc[:], 0.0).then_inc(s_dv, 1)
            acc0_dv = bump("dv")

            for ex in range(9):
                shared = ex == 8
                nhc = 8 if shared else 4
                if not shared:
                    bw = ex % 2

                    @blk.gpsimd
                    def _(e, ex=ex, bw=bw, w=wst_dv.get(bw, 0)):
                        if w:
                            e.wait_ge(s_pe, w)
                        e.dma_start(out=wea[bw][:],
                                    in_=egw_d[ex].rearrange("(c p) q -> p c q", p=128)).then_inc(s_lwa[bw], 16)
                        e.dma_start(out=web[bw][:],
                                    in_=euw_d[ex].rearrange("(c p) q -> p c q", p=128)).then_inc(s_lwa[bw], 16)
                        e.dma_start(out=wed[bw][:],
                                    in_=edw_d[ex].rearrange("(c p) q -> p c q", p=128)).then_inc(s_lwa[bw], 16)
                    ld_w = bump(f"lwa{bw}", 48)

                for hc in range(nhc):
                    for ts in range(TCH // 512):
                        hp = (hc * (TCH // 512) + ts) % 2
                        pslots = {0: ps_a[0][:, hp * 512:(hp + 1) * 512],
                                  1: ps_a[1][:, hp * 512:(hp + 1) * 512]}
                        ev_mark = {}
                        for which in range(2):
                            @blk.tensor
                            def _(e, ex=ex, hc=hc, ts=ts, which=which, ch=ch, hp=hp,
                                  shared=shared, ld_w=ld_w if not shared else lc_sh,
                                  wev=psfree.get((which, hp)), bw=(ex % 2) if not shared else 0):
                                if not shared:
                                    e.wait_ge(s_lwa[bw], ld_w)
                                else:
                                    e.wait_ge(s_lsh, ld_w)
                                if wev is not None:
                                    e.wait_ge(s_dv if wev[0] == "dv" else s_sc, wev[1])
                                wsrc = (wsa if shared else wea[bw]) if which == 0 else (wsb_ if shared else web[bw])
                                for k in range(DC):
                                    ins = e.matmul(
                                        pslots[which],
                                        wsrc[:, k, hc * 128:(hc + 1) * 128],
                                        xnfm[:, k, ts * 512:(ts + 1) * 512],
                                        start=(k == 0), stop=(k == DC - 1))
                                ins.then_inc(s_pe, 1)
                            pe_h = bump("pe")

                            if which == 0:
                                @blk.scalar
                                def _(e, pe_h=pe_h, hp=hp, wdv=n["dv"]):
                                    e.wait_ge(s_pe, pe_h)
                                    e.wait_ge(s_dv, wdv)
                                    e.activation(out=hst[:], in_=pslots[0],
                                                 func=AF.Silu).then_inc(s_sc, 1)
                                ev_mark[0] = ("sc", bump("sc"))
                                psfree[(0, hp)] = ev_mark[0]
                            else:
                                @blk.vector
                                def _(e, pe_h=pe_h, hp=hp, wdv=n["dv"]):
                                    e.wait_ge(s_pe, pe_h)
                                    e.wait_ge(s_dv, wdv)
                                    e.tensor_copy(hut[:], pslots[1]).then_inc(s_dv, 1)
                                ev_mark[1] = ("dv", bump("dv"))
                                psfree[(1, hp)] = ev_mark[1]

                        @blk.vector
                        def _(e, hc=hc, ts=ts, wsc=ev_mark[0], wdv=ev_mark[1]):
                            e.wait_ge(s_sc, wsc[1])
                            e.wait_ge(s_dv, wdv[1])
                            e.tensor_mul(hfm[:, hc, ts * 512:(ts + 1) * 512],
                                         hst[:], hut[:]).then_inc(s_dv, 1)
                        bump("dv")
                hdone_dv = n["dv"]

                for ts in range(TCH // 128):
                    for oh in range(2):
                        psx = ps_c if oh == 0 else ps_d

                        @blk.tensor
                        def _(e, ex=ex, ts=ts, oh=oh, psx=psx, shared=shared,
                              hdone_dv=hdone_dv, wdv=n["dv"], nhc=nhc,
                              bw=(ex % 2) if not shared else 0):
                            e.wait_ge(s_dv, max(hdone_dv, wdv))
                            wdsrc = wsd if shared else wed[bw]
                            for k in range(nhc):
                                ins = e.matmul(
                                    psx[:],
                                    hfm[:, k, ts * 128:(ts + 1) * 128],
                                    wdsrc[:, k if shared else k, oh * 512:(oh + 1) * 512],
                                    start=(k == 0), stop=(k == nhc - 1))
                            ins.then_inc(s_pe, 1)
                        pe_d = bump("pe")

                        @blk.vector
                        def _(e, ex=ex, ts=ts, oh=oh, psx=psx, shared=shared,
                              ch=ch, pe_d=pe_d, base=n["dv"]):
                            e.wait_ge(s_pe, pe_d)
                            e.wait_ge(s_dv, base)
                            if shared:
                                e.tensor_add(acc[:, ts, oh * 512:(oh + 1) * 512],
                                             acc[:, ts, oh * 512:(oh + 1) * 512],
                                             psx[:]).then_inc(s_dv, 1)
                            else:
                                e.scalar_tensor_tensor(
                                    out=acc[:, ts, oh * 512:(oh + 1) * 512],
                                    in0=psx[:],
                                    scalar=lam[:, ch * (TCH // 128) + ts, ex:ex + 1],
                                    in1=acc[:, ts, oh * 512:(oh + 1) * 512],
                                    op0=ALU.mult, op1=ALU.add).then_inc(s_dv, 1)
                        bump("dv")
                if not shared:
                    wst_dv[ex % 2] = n["pe"]

            for ts in range(TCH // 128):
                @blk.sync
                def _(e, ch=ch, ts=ts, w=n["dv"]):
                    e.wait_ge(s_dv, w)
                    e.dma_start(out=xmi[:],
                                in_=xm_d[(t0 + ts * 128):(t0 + (ts + 1) * 128), :]).then_inc(s_lx, 16)
                ld_xm = bump("lx", 16)

                @blk.vector
                def _(e, ts=ts, ld_xm=ld_xm, base=n["dv"], wo=out_go):
                    e.wait_ge(s_lx, ld_xm)
                    e.wait_ge(s_dv, base)
                    if wo:
                        e.wait_ge(s_go, wo)
                    e.tensor_add(outt[:], acc[:, ts, :], xmi[:]).then_inc(s_dv, 1)
                dv_out = bump("dv")

                @blk.gpsimd
                def _(e, ch=ch, ts=ts, dv_out=dv_out):
                    e.wait_ge(s_dv, dv_out)
                    e.dma_start(out=out_d[(t0 + ts * 128):(t0 + (ts + 1) * 128), :],
                                in_=outt[:]).then_inc(s_go, 16)
                out_go = bump("go", 16)

        @blk.gpsimd
        def _(e, w=n["go"]):
            e.wait_ge(s_go, w)

    return nc



def _core_masks(typ):
    needs = NEEDS[typ]
    m = np.zeros((8, 4, 128, 1024), dtype=np.float32)
    for s, bud in enumerate(SLOT_BUD):
        pos = needs[s] - 1
        k0 = (bud - 2) * 512
        kj = k0 + np.arange(1024)[None, :]
        for c in range(4):
            qpos = pos * 512 + c * 128 + np.arange(128)[:, None]
            m[s, c] = np.where(kj <= qpos, 0.0, NEG)
    return m.astype(BF16NP)


def _stage_inputs(inputs):
    x = np.ascontiguousarray(inputs["L0_attn_x"])
    common = {
        "w_inT": np.ascontiguousarray(inputs["L0_attn_W_in"].T),
        "b_in": np.ascontiguousarray(inputs["L0_attn_b_in"]),
        "ilw": np.ascontiguousarray(inputs["L0_attn_in_ln_w"]),
        "ilb": np.ascontiguousarray(inputs["L0_attn_in_ln_b"]),
        "w_outT": np.ascontiguousarray(inputs["L0_attn_W_out"].T),
        "olw": np.ascontiguousarray(inputs["L0_attn_out_ln_w"]),
        "olb": np.ascontiguousarray(inputs["L0_attn_out_ln_b"]),
        "smf": np.ascontiguousarray(inputs["L0_attn_smear_factor"]).reshape(1, NH),
        "lgs": np.ascontiguousarray(inputs["L0_attn_log_scale"]).reshape(1, NH),
        "gwT": np.ascontiguousarray(inputs["L0_ffn_gate_w"].T),
        "lbb": np.ascontiguousarray(inputs["L0_ffn_lb_bias"]).reshape(1, 8),
        "ident": np.eye(128, dtype=BF16NP),
        "ones3": np.ones((3, 128), dtype=BF16NP),
    }
    in_maps = []
    for c in range(8):
        bt, typ = c // 2, c % 2
        needs = NEEDS[typ]
        xo = np.concatenate([x[bt, (nn - 1) * 512:nn * 512] for nn in needs], 0)
        in_maps.append({"xf": np.ascontiguousarray(x[bt]),
                        "xo": np.ascontiguousarray(xo),
                        "masks": _core_masks(typ), **common})
    return x, in_maps


def kernel(**inputs):
    x, in_maps = _stage_inputs(inputs)
    ncA = build_A()
    resA = run_bass_kernel_spmd(ncA, in_maps, core_ids=list(range(8)))

    pall = np.stack([resA.results[c]["part"].reshape(16) for c in range(8)])
    ncB = build_B()
    common_b = {
        "pall": pall,
        "gwT": np.ascontiguousarray(inputs["L0_ffn_gate_w"].T),
        "lbb": np.ascontiguousarray(inputs["L0_ffn_lb_bias"]).reshape(1, 8),
        "egwT": np.ascontiguousarray(inputs["L0_ffn_expert_gate_w"].transpose(0, 2, 1)),
        "euwT": np.ascontiguousarray(inputs["L0_ffn_expert_up_w"].transpose(0, 2, 1)),
        "edwT": np.ascontiguousarray(inputs["L0_ffn_expert_down_w"].transpose(0, 2, 1)),
        "sgwT": np.ascontiguousarray(inputs["L0_ffn_shared_gate_w"].T),
        "suwT": np.ascontiguousarray(inputs["L0_ffn_shared_up_w"].T),
        "sdwT": np.ascontiguousarray(inputs["L0_ffn_shared_down_w"].T),
        "ident": np.eye(128, dtype=BF16NP),
    }
    in_maps_b = [{"xm": resA.results[c]["xmid"], **common_b} for c in range(8)]
    resB = run_bass_kernel_spmd(ncB, in_maps_b, core_ids=list(range(8)))

    out = np.zeros_like(x)
    for c in range(8):
        bt, typ = c // 2, c % 2
        needs = NEEDS[typ]
        ob = resB.results[c]["out"]
        for s, nn in enumerate(needs):
            out[bt, (nn - 1) * 512:nn * 512] = ob[s * 512:(s + 1) * 512]
    return out


# version 11
# speedup vs baseline: 1.0591x; 1.0591x over previous
import numpy as np
import ml_dtypes
from contextlib import ExitStack

import concourse.bass as bass
import concourse.mybir as mybir
from concourse.bass_utils import run_bass_kernel_spmd

F32 = mybir.dt.float32
BF16 = mybir.dt.bfloat16
AF = mybir.ActivationFunctionType
ALU = mybir.AluOpType
AX = mybir.AxisListType
BF16NP = ml_dtypes.bfloat16

D = 1024
DC = 8
NH = 16
DH = 128
TF = 8192
TO = 4096
EPS = 1e-5
NEG = -1e10

SLOT_BUD = [16, 14, 12, 10, 8, 6, 4, 2]
NEEDS = {0: [16, 13, 12, 9, 8, 5, 4, 1],
         1: [15, 14, 11, 10, 7, 6, 3, 2]}

DBG_A = 0


def build_A():
    nc = bass.Bass("TRN2", target_bir_lowering=False)
    dr = lambda nm, s, dt, k="ExternalInput": nc.dram_tensor(nm, s, dt, kind=k).ap()

    xf_d = dr("xf", [TF, D], F32)
    xo_d = dr("xo", [TO, D], F32)
    w_in_d = dr("w_inT", [D, 8208], F32)
    b_in_d = dr("b_in", [8208], F32)
    ilw_d = dr("ilw", [D], F32)
    ilb_d = dr("ilb", [D], F32)
    w_out_d = dr("w_outT", [2048, D], F32)
    olw_d = dr("olw", [D], F32)
    olb_d = dr("olb", [D], F32)
    smf_d = dr("smf", [1, NH], F32)
    lgs_d = dr("lgs", [1, NH], F32)
    gw_d = dr("gwT", [D, 8], F32)
    lbb_d = dr("lbb", [1, 8], F32)
    iota_d = dr("iota8", [1, 8], F32)
    masks_d = dr("masks", [8, 4, 128, 1024], BF16)
    ident_d = dr("ident", [128, 128], BF16)
    ones3_d = dr("ones3", [3, 128], BF16)

    xmid_d = dr("xmid", [TO, D], F32, "ExternalOutput")
    part_d = dr("part", [2, 8], F32, "ExternalOutput")

    scr = lambda nm, s, dt: nc.dram_tensor(nm, s, dt, kind="Internal").ap()
    xnf_s = scr("xnf_s", [DC, 128, TF], BF16)
    xno_s = scr("xno_s", [DC, 128, TO], BF16)
    kp_s = scr("kp_s", [NH, 128, TF], BF16)
    v_s = scr("v_s", [NH, TF // 128, 128, DH], BF16)
    q_s = scr("q_s", [NH, 128, TO], BF16)
    p_s = scr("p_s", [NH, 128, TO], BF16)
    pos3_s = scr("pos3_s", [NH, 3, TF], BF16)
    hs_s = scr("hs_s", [3, NH], F32)
    pos_s = scr("pos_s", [16, TF], F32)
    o_s = scr("o_s", [NH, 128, TO], BF16)
    l4_s = scr("l4_s", [4, 128], F32)
    l_s = scr("l_s", [8, NH, 4, 128], F32)
    sc8_s = scr("sc8_s", [8, 2], F32)

    dbg = {}
    if DBG_A >= 1:
        dbg["kp"] = dr("dbg_kp", [NH, 128, TF], BF16, "ExternalOutput")
        dbg["v"] = dr("dbg_v", [NH, TF // 128, 128, DH], BF16, "ExternalOutput")
        dbg["q"] = dr("dbg_q", [NH, 128, TO], BF16, "ExternalOutput")
        dbg["p"] = dr("dbg_p", [NH, 128, TO], BF16, "ExternalOutput")
        dbg["pos3"] = dr("dbg_pos3", [NH, 3, TF], BF16, "ExternalOutput")
        dbg["xnf"] = dr("dbg_xnf", [DC, 128, TF], BF16, "ExternalOutput")
    if DBG_A >= 2:
        dbg["o"] = dr("dbg_o", [NH, 128, TO], BF16, "ExternalOutput")

    gctx = ExitStack()
    gsb = lambda nm, s, dt: gctx.enter_context(nc.sbuf_tensor(nm, s, dt))
    pst = lambda nm, s, dt: gctx.enter_context(nc.psum_tensor(nm, s, dt))
    sem = lambda nm: gctx.enter_context(nc.semaphore(nm))

    ident = gsb("ident_s", [128, 128], BF16)
    hvec = gsb("hvec", [1, 3 * NH], F32)
    sig_b = gsb("sig_b", [128, NH], F32)
    oms_b = gsb("oms_b", [128, NH], F32)
    alp_b = gsb("alp_b", [128, NH], F32)
    bcol = gsb("bcol", [128, 65], F32)
    stats = gsb("stats", [128, 8], F32)
    seedc = gsb("seedc", [128, 1], F32)
    ones128f = gsb("ones128f", [1, 128], F32)
    onecol = gsb("onecol", [128, 1], F32)
    ones3 = gsb("ones3_s", [3, 128], BF16)
    carry = gsb("carry", [128, NH], F32)
    bvb = gsb("bvb", [128, 2048], F32)

    ps_a = [pst(f"ps_a{i}", [128, 1024], F32) for i in range(2)]
    ps_b = [pst(f"ps_b{i}", [128, 1024], BF16) for i in range(2)]
    ps_c = pst("ps_c", [128, 512], F32)
    ps_d = pst("ps_d", [128, 512], F32)

    s_ld = sem("s_ld")
    s_gl = sem("s_gl")
    s_pe = sem("s_pe")
    s_dv = sem("s_dv")
    s_sc = sem("s_sc")
    s_go = sem("s_go")
    s_lc = sem("s_lc")
    s_li = [sem("s_li0"), sem("s_li1")]
    s_lh = sem("s_lh")
    s_lm = [sem("s_lm0"), sem("s_lm1")]
    s_lw = sem("s_lw")
    s_sx = [sem("s_sx0"), sem("s_sx1")]
    s_sk = [sem("s_sk0"), sem("s_sk1")]
    s_sv = [sem("s_sv0"), sem("s_sv1")]
    s_gd = [sem("s_gd0"), sem("s_gd1")]
    s_xo = [sem("s_xo0"), sem("s_xo1")]
    s_xm = [sem("s_xm0"), sem("s_xm1")]

    n = {"ld": 0, "gl": 0, "pe": 0, "dv": 0, "sc": 0, "go": 0,
         "lc": 0, "li0": 0, "li1": 0, "lh": 0, "lm0": 0, "lm1": 0, "lw": 0,
         "sx0": 0, "sx1": 0, "sk0": 0, "sk1": 0, "sv0": 0, "sv1": 0,
         "gd0": 0, "gd1": 0, "xo0": 0, "xo1": 0, "xm0": 0, "xm1": 0}

    def bump(k, i=1):
        n[k] += i
        return n[k]

    with nc.Block() as blk:

        p1 = ExitStack()
        sb = lambda nm, s, dt: p1.enter_context(nc.sbuf_tensor(nm, s, dt))
        xin = [sb(f"xin{i}", [128, D], F32) for i in range(2)]
        x2t = sb("x2t", [128, D], F32)
        t2t = [sb(f"t2t{i}", [128, D], F32) for i in range(2)]
        xnbf = [sb(f"xnbf{i}", [128, D], BF16) for i in range(2)]
        xnT = [sb(f"xnT{i}", [128, D], BF16) for i in range(2)]
        wln_b = sb("wln_b", [128, D], F32)
        bln_b = sb("bln_b", [128, D], F32)


        @blk.sync
        def _(e):
            e.dma_start(out=ident[:], in_=ident_d).then_inc(s_lc, 16)
            e.dma_start(out=ones3[:], in_=ones3_d).then_inc(s_lc, 16)
            e.dma_start(out=wln_b[:], in_=ilw_d[None, :].broadcast_to((128, D))).then_inc(s_lc, 16)
            e.dma_start(out=bln_b[:], in_=ilb_d[None, :].broadcast_to((128, D))).then_inc(s_lc, 16)
            e.dma_start(out=hvec[:, 0:NH], in_=smf_d).then_inc(s_lc, 16)
            e.dma_start(out=hvec[:, NH:2 * NH], in_=lgs_d).then_inc(s_lc, 16)
            e.dma_start(out=bvb[:], in_=b_in_d[None, 4096:6144].broadcast_to((128, 2048))).then_inc(s_lc, 16)
        bump("lc", 112)

        @blk.gpsimd
        def _(e):
            for oc in range(64):
                e.dma_start(out=bcol[:, oc:oc + 1],
                            in_=b_in_d[oc * 128:(oc + 1) * 128][:, None]).then_inc(s_gl, 16)
            e.dma_start(out=bcol[0:16, 64:65],
                        in_=b_in_d[8192:8208][:, None]).then_inc(s_gl, 16)
        bump("gl", 65 * 16)
        ld0 = n["lc"]

        @blk.scalar
        def _(e, ld0=ld0):
            e.wait_ge(s_lc, ld0)
            e.activation(out=hvec[:, 2 * NH:3 * NH], in_=hvec[:, 0:NH],
                         func=AF.Sigmoid).then_inc(s_sc, 1)
        sc0 = bump("sc")

        @blk.vector
        def _(e, sc0=sc0):
            e.wait_ge(s_sc, sc0)
            e.tensor_scalar(out=hvec[:, 0:NH], in0=hvec[:, 2 * NH:3 * NH],
                            scalar1=-1.0, scalar2=1.0, op0=ALU.mult,
                            op1=ALU.add).then_inc(s_dv, 1)
            e.memset(carry[:], 0.0).then_inc(s_dv, 1)
            e.memset(seedc[:], -3.0e38).then_inc(s_dv, 1)
            e.memset(ones128f[:], 1.0).then_inc(s_dv, 1)
            e.memset(onecol[:], 1.0).then_inc(s_dv, 1)
        dv0 = bump("dv", 5)

        @blk.scalar
        def _(e, dv0=dv0):
            e.wait_ge(s_dv, dv0)
            e.activation(out=hvec[:, NH:2 * NH], in_=hvec[:, NH:2 * NH],
                         func=AF.Exp, scale=-2.0).then_inc(s_sc, 1)
        sc1 = bump("sc")

        @blk.vector
        def _(e, sc1=sc1):
            e.wait_ge(s_sc, sc1)
            e.tensor_scalar_mul(hvec[:, NH:2 * NH], hvec[:, NH:2 * NH],
                                float(1.0 / np.sqrt(128.0))).then_inc(s_dv, 1)
        dv1 = bump("dv")

        @blk.gpsimd
        def _(e, dv1=dv1):
            e.wait_ge(s_dv, dv1)
            e.dma_start(out=hs_s.rearrange("r h -> (r h)")[None, :],
                        in_=hvec[:, 0:3 * NH]).then_inc(s_gl, 16)
            e.wait_ge(s_gl, bump("gl", 16))
            for h in range(NH):
                e.dma_start(out=oms_b[:, h:h + 1],
                            in_=hs_s[0:1, h:h + 1].broadcast_to((128, 1))).then_inc(s_gl, 16)
                e.dma_start(out=alp_b[:, h:h + 1],
                            in_=hs_s[1:2, h:h + 1].broadcast_to((128, 1))).then_inc(s_gl, 16)
                e.dma_start(out=sig_b[:, h:h + 1],
                            in_=hs_s[2:3, h:h + 1].broadcast_to((128, 1))).then_inc(s_gl, 16)
        bump("gl", 48 * 16)

        ev_dv = [0, 0]
        ev_sc = [0, 0]
        ev_go = [0, 0]
        for t in range(96):
            b = t % 2
            own = t >= 64
            src = xo_d if own else xf_d
            row = (t - 64) * 128 if own else t * 128
            dstT = xno_s if own else xnf_s

            @blk.sync
            def _(e, b=b, src=src, row=row, w=ev_dv[b]):
                if w:
                    e.wait_ge(s_dv, w)
                e.dma_start(out=xin[b][:], in_=src[row:row + 128, :]).then_inc(s_li[b], 16)
            ld_t = bump(f"li{b}", 16)

            @blk.scalar
            def _(e, b=b, ld_t=ld_t):
                e.wait_ge(s_li[b], ld_t)
                e.activation(out=x2t[:], in_=xin[b][:], func=AF.Square,
                             accum_out=stats[:, 1:2]).then_inc(s_sc, 1)
            sc_sq = bump("sc")

            @blk.vector
            def _(e, b=b, sc_sq=sc_sq, base=n["dv"]):
                e.wait_ge(s_sc, sc_sq)
                e.reduce_sum(stats[:, 0:1], xin[b][:], axis=AX.X).then_inc(s_dv, 1)
                e.wait_ge(s_dv, base + 1)
                e.tensor_scalar_mul(stats[:, 2:3], stats[:, 0:1], 1.0 / D).then_inc(s_dv, 1)
                e.wait_ge(s_dv, base + 2)
                e.tensor_mul(stats[:, 3:4], stats[:, 2:3], stats[:, 2:3]).then_inc(s_dv, 1)
                e.wait_ge(s_dv, base + 3)
                e.scalar_tensor_tensor(out=stats[:, 3:4], in0=stats[:, 1:2],
                                       scalar=1.0 / D, in1=stats[:, 3:4],
                                       op0=ALU.mult, op1=ALU.subtract).then_inc(s_dv, 1)
                e.wait_ge(s_dv, base + 4)
                e.tensor_scalar(out=stats[:, 4:5], in0=stats[:, 3:4],
                                scalar1=float(1.0 + EPS), scalar2=float(EPS * EPS),
                                op0=ALU.mult, op1=ALU.add).then_inc(s_dv, 1)
            bump("dv", 5)
            dv_ab = n["dv"]

            @blk.scalar
            def _(e, dv_ab=dv_ab):
                e.wait_ge(s_dv, dv_ab)
                e.sqrt(stats[:, 5:6], stats[:, 4:5]).then_inc(s_sc, 1)
            sc_rt = bump("sc")

            @blk.vector
            def _(e, b=b, sc_rt=sc_rt, base=n["dv"]):
                e.wait_ge(s_sc, sc_rt)
                e.reciprocal(stats[:, 6:7], stats[:, 5:6]).then_inc(s_dv, 1)
                e.wait_ge(s_dv, base + 1)
                e.tensor_scalar(out=t2t[b][:], in0=xin[b][:], scalar1=stats[:, 2:3],
                                scalar2=stats[:, 6:7], op0=ALU.subtract,
                                op1=ALU.mult).then_inc(s_dv, 1)
                e.wait_ge(s_dv, base + 2)
                e.tensor_mul(t2t[b][:], t2t[b][:], wln_b[:]).then_inc(s_dv, 1)
                e.wait_ge(s_dv, base + 3)
                e.tensor_add(xnbf[b][:], t2t[b][:], bln_b[:]).then_inc(s_dv, 1)
            bump("dv", 4)
            ev_dv[b] = n["dv"]
            dv_nrm = n["dv"]

            @blk.tensor
            def _(e, b=b, dv_nrm=dv_nrm, w=ev_sc[b]):
                e.wait_ge(s_dv, dv_nrm)
                if w:
                    e.wait_ge(s_sc, w)
                for k in range(DC):
                    ins = e.transpose(ps_b[b][:, k * 128:(k + 1) * 128],
                                      xnbf[b][:, k * 128:(k + 1) * 128], ident[:])
                ins.then_inc(s_pe, 1)
            pe_tr = bump("pe")

            @blk.scalar
            def _(e, b=b, pe_tr=pe_tr, w=ev_go[b]):
                e.wait_ge(s_pe, pe_tr)
                if w:
                    e.wait_ge(s_sx[b], w)
                e.copy(xnT[b][:], ps_b[b][:]).then_inc(s_sc, 1)
            sc_ev = bump("sc")
            ev_sc[b] = sc_ev

            @blk.gpsimd
            def _(e, b=b, sc_ev=sc_ev, dstT=dstT, row=row):
                e.wait_ge(s_sc, sc_ev)
                e.dma_start(
                    out=dstT[:, :, row:row + 128].rearrange("c p q -> p c q"),
                    in_=xnT[b][:].rearrange("p (c q) -> p c q", c=DC),
                ).then_inc(s_sx[b], 16)
            ev_go[b] = bump(f"sx{b}", 16)

        a1_sx = (n["sx0"], n["sx1"])
        p1_keep = (wln_b, bln_b)
        del p1_keep
        p1.close()

        p2 = ExitStack()
        sb2 = lambda nm, s, dt: p2.enter_context(nc.sbuf_tensor(nm, s, dt))
        wt1 = sb2("wt1", [128, DC, 2048], BF16)
        wt2 = sb2("wt2", [128, DC, 2048], BF16)
        wyT = sb2("wyT", [128, DC, NH], BF16)
        xnt_in = [sb2(f"xnt_in{i}", [128, DC, 512], BF16) for i in range(2)]
        ktil = sb2("ktil", [128, 512], F32)
        khat = sb2("khat", [128, 512], F32)
        kout = [sb2(f"kout{i}", [128, 512], BF16) for i in range(2)]
        vout = [sb2(f"vout{i}", [128, 512], BF16) for i in range(2)]
        yrow = sb2("yrow", [16, TF], F32)
        posr = sb2("posr", [16, TF], F32)
        ppf = sb2("ppf", [128, 1024], F32)
        ppt = sb2("ppt", [128, 1024], F32)
        pph = sb2("pph", [128, 1024], BF16)
        ppm = sb2("ppm", [128, 1024], BF16)
        ppl = sb2("ppl", [128, 1024], BF16)
        qevac = [sb2(f"qevac{i}", [128, 512], BF16) for i in range(2)]
        @blk.tensor
        def _(e, w=n["gl"], w2=n["lc"], w3=a1_sx, w4=(n["li0"], n["li1"])):
            e.wait_ge(s_gl, w)
            e.wait_ge(s_lc, w2)
            e.wait_ge(s_sx[0], w3[0])
            e.wait_ge(s_sx[1], w3[1])
            e.wait_ge(s_li[0], w4[0])
            e.wait_ge(s_li[1], w4[1])
            e.sem_inc(s_pe, 1)
        a2bar = bump("pe")

        @blk.gpsimd
        def _(e, wbar=a2bar):
            e.wait_ge(s_pe, wbar)
            e.dma_start(out=wt1[:],
                        in_=w_in_d[:, 2048:4096].rearrange("(c p) j -> p c j", p=128)).then_inc(s_lw, 16)
            e.dma_start(out=wt2[:],
                        in_=w_in_d[:, 4096:6144].rearrange("(c p) j -> p c j", p=128)).then_inc(s_lw, 16)
            e.dma_start(out=wyT[:],
                        in_=w_in_d[:, 8192:8208].rearrange("(c p) j -> p c j", p=128)).then_inc(s_lw, 16)
        bump("lw", 48)
        a2a_lw = n["lw"]

        @blk.scalar
        def _(e, w=a2a_lw):
            e.wait_ge(s_lw, w)
            e.sem_inc(s_sc, 1)
        a2a_sc = bump("sc")

        xnt_free = [0, 0]
        kevac_dv = [0, 0]
        kout_go = [0, 0]
        vout_go = [0, 0]
        vevac_dv = [0, 0]
        y_dv = 0
        for tt in range(16):
            b = tt % 2
            tok0 = tt * 512

            @blk.sync
            def _(e, b=b, tok0=tok0, w=xnt_free[b], wg=a1_sx):
                e.wait_ge(s_sx[0], wg[0])
                e.wait_ge(s_sx[1], wg[1])
                if w:
                    e.wait_ge(s_pe, w)
                e.dma_start(out=xnt_in[b][:],
                            in_=xnf_s[:, :, tok0:tok0 + 512].rearrange("c p q -> p c q")).then_inc(s_li[b], 16)
            ld_kv = bump(f"li{b}", 16)

            for h in range(NH):
                @blk.tensor
                def _(e, b=b, h=h, ld_kv=ld_kv, wdv=kevac_dv[h % 2], wsc=a2a_sc):
                    e.wait_ge(s_li[b], ld_kv)
                    e.wait_ge(s_sc, wsc)
                    if wdv:
                        e.wait_ge(s_dv, wdv)
                    for c in range(DC):
                        ins = e.matmul(ps_a[h % 2][:, 0:512],
                                       wt1[:, c, h * 128:(h + 1) * 128],
                                       xnt_in[b][:, c, :],
                                       start=(c == 0), stop=(c == DC - 1))
                    ins.then_inc(s_pe, 1)
                pe_k = bump("pe")

                @blk.vector
                def _(e, h=h, pe_k=pe_k, base=n["dv"], wgo=kout_go[h % 2]):
                    e.wait_ge(s_pe, pe_k)
                    e.wait_ge(s_dv, base)
                    if wgo:
                        e.wait_ge(s_sk[h % 2], wgo)
                    e.tensor_scalar(out=ktil[:], in0=ps_a[h % 2][:, 0:512],
                                    scalar1=bcol[:, 16 + h:17 + h],
                                    scalar2=oms_b[:, h:h + 1],
                                    op0=ALU.add, op1=ALU.mult).then_inc(s_dv, 1)
                    e.tensor_scalar(out=khat[:], in0=ps_a[h % 2][:, 0:512],
                                    scalar1=bcol[:, 16 + h:17 + h],
                                    scalar2=sig_b[:, h:h + 1],
                                    op0=ALU.add, op1=ALU.mult).then_inc(s_dv, 1)
                    e.wait_ge(s_dv, base + 2)
                    e.tensor_add(kout[h % 2][:, 0:1], ktil[:, 0:1], carry[:, h:h + 1]).then_inc(s_dv, 1)
                    e.tensor_add(kout[h % 2][:, 1:512], ktil[:, 1:512], khat[:, 0:511]).then_inc(s_dv, 1)
                    e.wait_ge(s_dv, base + 4)
                    e.tensor_copy(carry[:, h:h + 1], khat[:, 511:512]).then_inc(s_dv, 1)
                bump("dv", 5)
                kevac_dv[h % 2] = n["dv"] - 3
                dv_k = n["dv"]

                @blk.gpsimd
                def _(e, h=h, tok0=tok0, dv_k=dv_k):
                    e.wait_ge(s_dv, dv_k)
                    e.dma_start(out=kp_s[h][:, tok0:tok0 + 512],
                                in_=kout[h % 2][:]).then_inc(s_sk[h % 2], 16)
                kout_go[h % 2] = bump(f"sk{h % 2}", 16)

            vi = 0
            for g in range(4):
                for ts4 in range(4):
                    @blk.tensor
                    def _(e, b=b, g=g, ts4=ts4, vi=vi, wdv=vevac_dv[vi % 2]):
                        if wdv:
                            e.wait_ge(s_dv, wdv)
                        for c in range(DC):
                            ins = e.matmul(ps_a[vi % 2][:, 512:1024],
                                           xnt_in[b][:, c, ts4 * 128:(ts4 + 1) * 128],
                                           wt2[:, c, g * 512:(g + 1) * 512],
                                           start=(c == 0), stop=(c == DC - 1))
                        ins.then_inc(s_pe, 1)
                    pe_v = bump("pe")

                    @blk.vector
                    def _(e, g=g, vi=vi, pe_v=pe_v, wgo=vout_go[vi % 2]):
                        e.wait_ge(s_pe, pe_v)
                        if wgo:
                            e.wait_ge(s_sv[vi % 2], wgo)
                        e.tensor_add(vout[vi % 2][:], ps_a[vi % 2][:, 512:1024],
                                     bvb[:, g * 512:(g + 1) * 512]).then_inc(s_dv, 1)
                    vevac_dv[vi % 2] = bump("dv")
                    dv_v = n["dv"]

                    @blk.gpsimd
                    def _(e, g=g, ts4=ts4, tok0=tok0, vi=vi, dv_v=dv_v):
                        e.wait_ge(s_dv, dv_v)
                        kc = tok0 // 128 + ts4
                        e.dma_start(
                            out=v_s[4 * g:4 * (g + 1), kc, :, :].rearrange("h p d -> p h d"),
                            in_=vout[vi % 2][:].rearrange("p (h d) -> p h d", h=4),
                        ).then_inc(s_sv[vi % 2], 16)
                    vout_go[vi % 2] = bump(f"sv{vi % 2}", 16)
                    vi += 1

            @blk.tensor
            def _(e, b=b, ld_kv=ld_kv, wdv=y_dv):
                if wdv:
                    e.wait_ge(s_dv, wdv)
                for c in range(DC):
                    ins = e.matmul(ps_d[0:16, :], wyT[:, c, :], xnt_in[b][:, c, :],
                                   start=(c == 0), stop=(c == DC - 1))
                ins.then_inc(s_pe, 1)
            pe_yt = bump("pe")
            xnt_free[b] = pe_yt

            @blk.vector
            def _(e, tok0=tok0, pe_yt=pe_yt):
                e.wait_ge(s_pe, pe_yt)
                e.tensor_scalar(out=yrow[:, tok0:tok0 + 512], in0=ps_d[0:16, :],
                                scalar1=bcol[0:16, 64:65], scalar2=None,
                                op0=ALU.add).then_inc(s_dv, 1)
            y_dv = bump("dv")

        a2b_dv = n["dv"]

        @blk.scalar
        def _(e, w=a2b_dv):
            e.wait_ge(s_dv, w)
            e.activation(out=yrow[:], in_=yrow[:], func=AF.Sigmoid).then_inc(s_sc, 1)
        sc_sig = bump("sc")

        @blk.vector
        def _(e, sc_sig=sc_sig):
            e.wait_ge(s_sc, sc_sig)
            e.tensor_tensor_scan(out=posr[:], data0=yrow[:], data1=yrow[:],
                                 initial=0.0, op0=ALU.add, op1=ALU.max).then_inc(s_dv, 1)
        dv_scan = bump("dv")

        @blk.gpsimd
        def _(e, dv_scan=dv_scan):
            e.wait_ge(s_dv, dv_scan)
            e.dma_start(out=pos_s, in_=posr[:]).then_inc(s_go, 16)
            e.wait_ge(s_go, bump("go", 16))
            e.dma_start(out=ppf[:],
                        in_=pos_s.rearrange("h (g q) -> (h g) q", g=8)).then_inc(s_gl, 16)
        gl_pp = bump("gl", 16)

        @blk.vector
        def _(e, gl_pp=gl_pp, base=n["dv"]):
            e.wait_ge(s_gl, gl_pp)
            e.tensor_copy(pph[:], ppf[:]).then_inc(s_dv, 1)
            e.wait_ge(s_dv, base + 1)
            e.tensor_sub(ppt[:], ppf[:], pph[:]).then_inc(s_dv, 1)
            e.wait_ge(s_dv, base + 2)
            e.tensor_copy(ppm[:], ppt[:]).then_inc(s_dv, 1)
            e.wait_ge(s_dv, base + 3)
            e.tensor_sub(ppt[:], ppt[:], ppm[:]).then_inc(s_dv, 1)
            e.wait_ge(s_dv, base + 4)
            e.tensor_copy(ppl[:], ppt[:]).then_inc(s_dv, 1)
        bump("dv", 5)
        dv_p3 = n["dv"]

        @blk.gpsimd
        def _(e, dv_p3=dv_p3):
            e.wait_ge(s_dv, dv_p3)
            for h in range(NH):
                for r, t_ in ((0, pph), (1, ppm), (2, ppl)):
                    e.dma_start(out=pos3_s[h, r].rearrange("(g q) -> g q", g=8),
                                in_=t_[h * 8:(h + 1) * 8, :]).then_inc(s_go, 16)
        bump("go", 48 * 16)

        @blk.gpsimd
        def _(e, wpe=n["pe"]):
            e.wait_ge(s_pe, wpe)
            e.dma_start(out=wt1[:],
                        in_=w_in_d[:, 0:2048].rearrange("(c p) j -> p c j", p=128)).then_inc(s_lw, 16)
            e.dma_start(out=wt2[:],
                        in_=w_in_d[:, 6144:8192].rearrange("(c p) j -> p c j", p=128)).then_inc(s_lw, 16)
        bump("lw", 32)
        a2d_lw = n["lw"]

        @blk.scalar
        def _(e, w=a2d_lw):
            e.wait_ge(s_lw, w)
            e.sem_inc(s_sc, 1)
        a2d_sc = bump("sc")

        qp_free = [0, 0]
        qev_go = [0, 0]
        qevac_dv = [0, 0]
        for tt in range(8):
            b = tt % 2
            tok0 = tt * 512

            @blk.sync
            def _(e, b=b, tok0=tok0, w=qp_free[b], wsc=a2d_sc):
                e.wait_ge(s_sc, wsc)
                if w:
                    e.wait_ge(s_pe, w)
                e.dma_start(out=xnt_in[b][:],
                            in_=xno_s[:, :, tok0:tok0 + 512].rearrange("c p q -> p c q")).then_inc(s_li[b], 16)
            ld_qp = bump(f"li{b}", 16)

            for which, wsrc, dst_s in ((0, wt1, q_s), (1, wt2, p_s)):
                for h in range(NH):
                    @blk.tensor
                    def _(e, b=b, h=h, wsrc=wsrc, ld_qp=ld_qp, wdv=qevac_dv[h % 2]):
                        e.wait_ge(s_li[b], ld_qp)
                        if wdv:
                            e.wait_ge(s_dv, wdv)
                        for c in range(DC):
                            ins = e.matmul(ps_a[h % 2][:, 0:512],
                                           wsrc[:, c, h * 128:(h + 1) * 128],
                                           xnt_in[b][:, c, :],
                                           start=(c == 0), stop=(c == DC - 1))
                        ins.then_inc(s_pe, 1)
                    pe_q = bump("pe")

                    @blk.vector
                    def _(e, h=h, which=which, pe_q=pe_q, wgo=qev_go[h % 2]):
                        e.wait_ge(s_pe, pe_q)
                        if wgo:
                            e.wait_ge(s_sk[h % 2], wgo)
                        if which == 0:
                            e.tensor_scalar(out=qevac[h % 2][:], in0=ps_a[h % 2][:, 0:512],
                                            scalar1=bcol[:, h:h + 1],
                                            scalar2=alp_b[:, h:h + 1],
                                            op0=ALU.add, op1=ALU.mult).then_inc(s_dv, 1)
                        else:
                            e.tensor_scalar(out=qevac[h % 2][:], in0=ps_a[h % 2][:, 0:512],
                                            scalar1=bcol[:, 48 + h:49 + h],
                                            scalar2=None, op0=ALU.add).then_inc(s_dv, 1)
                    qevac_dv[h % 2] = bump("dv")
                    dv_q = n["dv"]

                    @blk.gpsimd
                    def _(e, h=h, tok0=tok0, dst_s=dst_s, dv_q=dv_q):
                        e.wait_ge(s_dv, dv_q)
                        e.dma_start(out=dst_s[h][:, tok0:tok0 + 512],
                                    in_=qevac[h % 2][:]).then_inc(s_sk[h % 2], 16)
                    qev_go[h % 2] = bump(f"sk{h % 2}", 16)
                qp_free[b] = n["pe"]

        a2_tot = (n["go"], n["sk0"], n["sk1"], n["sv0"], n["sv1"])

        p2.close()

        p3s = ExitStack()
        sb3 = lambda nm, s, dt: p3s.enter_context(nc.sbuf_tensor(nm, s, dt))
        ktA = sb3("ktA", [128, TF], BF16)
        vA = sb3("vA", [128, 64, DH], BF16)
        qt2 = [sb3(f"qt2{i}", [128, 512], BF16) for i in range(2)]
        p3A = sb3("p3A", [3, TF], BF16)
        cache = sb3("cache", [128, 2 * TF], F32)
        mloc = sb3("mloc", [128, 2, 8], F32)
        mrun = sb3("mrun", [128, 2], F32)
        negm = sb3("negm", [128, 2], F32)
        probs = [sb3(f"probs{i}", [128, TF], BF16) for i in range(4)]
        ptsb = [sb3(f"ptsb{i}", [128, 1024], BF16) for i in range(2)]
        lcols = sb3("lcols", [128, 2, 4], F32)
        o_sb = [sb3(f"o_sb{i}", [128, 512], BF16) for i in range(2)]
        maskt = sb3("maskt", [128, 4, 1024], BF16)

        a2_ld = n["ld"]
        mask_dv = [0]
        exp_sc = [0, 0, 0, 0]
        tpose_pe = [0, 0, 0, 0]
        dance_go = [0, 0]
        pv_mark = [0, 0]
        ts2_dv = {}
        osb_dv = [0, 0]
        ptsb_sc = [None, None]
        oev_dv = 0
        linv_dv = 0

        def U_A(h, s, c, kp, ld_h, ld_m):
            B = SLOT_BUD[s]
            par = c % 2
            coff = par * TF
            last = kp == B // 2 - 1

            @blk.tensor
            def _(e, wdv=ts2_dv.get(kp % 2, 0)):
                e.wait_ge(s_lh, ld_h)
                e.wait_ge(s_lm[s % 2], ld_m)
                if wdv:
                    e.wait_ge(s_dv, wdv)
                for half in range(2):
                    kt = 2 * kp + half
                    sl = slice(half * 512, (half + 1) * 512)
                    e.matmul(ps_a[kp % 2][:, sl],
                             qt2[s % 2][:, c * 128:(c + 1) * 128],
                             ktA[:, kt * 512:(kt + 1) * 512],
                             start=True, stop=False)
                for half in range(2):
                    kt = 2 * kp + half
                    sl = slice(half * 512, (half + 1) * 512)
                    ins = e.matmul(ps_a[kp % 2][:, sl], ones3[:],
                                   p3A[:, kt * 512:(kt + 1) * 512],
                                   start=False, stop=True)
                ins.then_inc(s_pe, 1)
            pe_qk = bump("pe")

            @blk.vector
            def _(e, pe_qk=pe_qk, base=n["dv"], wexp=exp_sc[(c + 2) % 4],
                  wgo=dance_go[s % 2]):
                e.wait_ge(s_pe, pe_qk)
                if kp == 0:
                    if wexp:
                        e.wait_ge(s_sc, wexp)
                    if wgo and c == 0:
                        e.wait_ge(s_gd[s % 2], wgo)
                if last:
                    e.wait_ge(s_lm[s % 2], ld_m)
                    e.tensor_add(ps_a[kp % 2][:], ps_a[kp % 2][:],
                                 maskt[:, c, :]).then_inc(s_dv, 1)
                    e.wait_ge(s_dv, base + 1)
                e.tensor_scalar(out=cache[:, coff + kp * 1024: coff + (kp + 1) * 1024],
                                in0=ps_a[kp % 2][:], scalar1=0.0, scalar2=seedc[:],
                                op0=ALU.add, op1=ALU.max,
                                accum_out=mloc[:, par, kp:kp + 1]).then_inc(s_dv, 1)
            bump("dv", 2 if last else 1)
            ts2_dv[kp % 2] = n["dv"]
            if last:
                mask_dv[0] = n["dv"]

        def U_MX(h, s, c):
            B = SLOT_BUD[s]
            par = c % 2

            @blk.vector
            def _(e, base=n["dv"], wexp=exp_sc[(c + 2) % 4]):
                e.wait_ge(s_dv, base)
                if wexp:
                    e.wait_ge(s_sc, wexp)
                e.reduce_max(mrun[:, par:par + 1], mloc[:, par, 0:B // 2],
                             axis=AX.X).then_inc(s_dv, 1)
                e.wait_ge(s_dv, base + 1)
                e.tensor_scalar_mul(negm[:, par:par + 1], mrun[:, par:par + 1],
                                    -1.0).then_inc(s_dv, 1)
            bump("dv", 2)

        def U_EXP(h, s, c):
            B = SLOT_BUD[s]
            par = c % 2
            coff = par * TF
            dv_exp = n["dv"]

            @blk.scalar
            def _(e, dv_exp=dv_exp, wpe=tpose_pe[c]):
                e.wait_ge(s_dv, dv_exp)
                if wpe:
                    e.wait_ge(s_pe, wpe)
                e.activation(out=probs[c][:, 0:B * 512],
                             in_=cache[:, coff: coff + B * 512],
                             func=AF.Exp, bias=negm[:, par:par + 1],
                             accum_out=lcols[:, s % 2, c:c + 1]).then_inc(s_sc, 1)
            exp_sc[c] = bump("sc")

        def mk_T(h, s, pair, kg):
            B = SLOT_BUD[s]
            wexp0 = exp_sc[2 * pair]
            wexp1 = exp_sc[2 * pair + 1]

            @blk.tensor
            def _(e, wexp=max(wexp0, wexp1), wsc=ptsb_sc[kg % 2], wdv=oev_box[0],
                  wpv=pv_mark[kg % 2]):
                e.wait_ge(s_sc, wexp)
                if wsc:
                    e.wait_ge(s_dv if wsc[0] == "dv" else s_sc, wsc[1])
                if wpv:
                    e.wait_ge(s_pe, wpv)
                if kg == 0 and pair == 0 and wdv:
                    e.wait_ge(s_dv, wdv)
                for kcl in range(4):
                    kc = kg * 4 + kcl
                    for par2 in range(2):
                        ins = e.transpose(
                            ps_b[kg % 2][:, kcl * 256 + par2 * 128: kcl * 256 + (par2 + 1) * 128],
                            probs[2 * pair + par2][:, kc * 128:(kc + 1) * 128], ident[:])
                ins.then_inc(s_pe, 1)
            pe_t = bump("pe")
            if kg == B - 1:
                tpose_pe[2 * pair] = tpose_pe[2 * pair + 1] = pe_t

            @blk.vector
            def _(e, pe_t=pe_t):
                e.wait_ge(s_pe, pe_t)
                e.tensor_copy(ptsb[kg % 2][:], ps_b[kg % 2][:]).then_inc(s_dv, 1)
            ptsb_sc[kg % 2] = ("dv", bump("dv"))

        def mk_PV(h, s, pair, kg):
            B = SLOT_BUD[s]

            @blk.tensor
            def _(e, wsc=ptsb_sc[kg % 2]):
                e.wait_ge(s_dv if wsc[0] == "dv" else s_sc, wsc[1])
                for kcl in range(4):
                    kc = kg * 4 + kcl
                    ins = e.matmul(ps_c[:, pair * 256:(pair + 1) * 256],
                                   vA[:, kc, :],
                                   ptsb[kg % 2][:, kcl * 256:(kcl + 1) * 256],
                                   start=(kg == 0 and kcl == 0),
                                   stop=(kg == B - 1 and kcl == 3))
                ins.then_inc(s_pe, 1)
            pv_mark[kg % 2] = bump("pe")

        def U_DANCE(h, s, ladd):
            pv_done = n["pe"]

            @blk.gpsimd
            def _(e, ladd=ladd):
                e.wait_ge(s_sc, ladd)
                for c4 in range(4):
                    e.dma_start(out=l_s[s, h, c4][:, None],
                                in_=lcols[:, s % 2, c4:c4 + 1]).then_inc(s_gd[s % 2], 16)
            bump(f"gd{s % 2}", 64)

            @blk.vector
            def _(e, pv_done=pv_done, wgo=osb_dv[s % 2]):
                e.wait_ge(s_pe, pv_done)
                if wgo:
                    e.wait_ge(s_sv[s % 2], wgo)
                e.tensor_copy(o_sb[s % 2][:], ps_c[:]).then_inc(s_dv, 1)
            oev = bump("dv")

            @blk.gpsimd
            def _(e, wdv=oev):
                e.wait_ge(s_dv, wdv)
                e.dma_start(out=o_s[h][:, s * 512:(s + 1) * 512],
                            in_=o_sb[s % 2][:]).then_inc(s_sv[s % 2], 16)
            osb_dv[s % 2] = bump(f"sv{s % 2}", 16)
            dance_go[s % 2] = n[f"gd{s % 2}"]
            return oev

        def interleave(primary, filler):
            fi = 0
            for i, u in enumerate(primary):
                u()
                take = 2 if len(filler) - fi > 2 * (len(primary) - i - 1) else 1
                for _ in range(take):
                    if fi < len(filler):
                        filler[fi](); fi += 1
            while fi < len(filler):
                filler[fi](); fi += 1

        oev_box = [oev_dv]
        linv_dv = [linv_dv]
        for h in range(NH):
            @blk.sync
            def _(e, h=h, wt=a2_tot, wpe=n["pe"]):
                if h == 0:
                    e.wait_ge(s_go, wt[0])
                    e.wait_ge(s_sk[0], wt[1])
                    e.wait_ge(s_sk[1], wt[2])
                    e.wait_ge(s_sv[0], wt[3])
                    e.wait_ge(s_sv[1], wt[4])
                else:
                    e.wait_ge(s_pe, wpe)
                e.dma_start(out=ktA[:], in_=kp_s[h]).then_inc(s_lh, 16)
                e.dma_start(out=vA[:], in_=v_s[h].rearrange("a p d -> p a d")).then_inc(s_lh, 16)
                e.dma_start(out=p3A[:], in_=pos3_s[h]).then_inc(s_lh, 16)
            ld_h = bump("lh", 48)

            pending = []
            for s in range(8):
                B = SLOT_BUD[s]

                @blk.sync
                def _(e, s=s, h=h, wdv=mask_dv[0], wt=a2_tot):
                    if h == 0:
                        e.wait_ge(s_go, wt[0])
                        e.wait_ge(s_sk[0], wt[1])
                        e.wait_ge(s_sk[1], wt[2])
                        e.wait_ge(s_sv[0], wt[3])
                        e.wait_ge(s_sv[1], wt[4])
                    if wdv:
                        e.wait_ge(s_dv, wdv)
                    e.dma_start(out=maskt[:],
                                in_=masks_d[s].rearrange("c p q -> p c q")).then_inc(s_lm[s % 2], 16)
                    e.dma_start(out=qt2[s % 2][:],
                                in_=q_s[h][:, s * 512:(s + 1) * 512]).then_inc(s_lm[s % 2], 16)
                ld_m = bump(f"lm{s % 2}", 32)

                A = {c: [(lambda h=h, s=s, c=c, kp=kp, ld_h=ld_h, ld_m=ld_m:
                          U_A(h, s, c, kp, ld_h, ld_m)) for kp in range(B // 2)]
                     for c in range(4)}
                interleave(A[0]
                           + [lambda h=h, s=s: (U_MX(h, s, 0), U_EXP(h, s, 0))]
                           + A[1]
                           + [lambda h=h, s=s: (U_MX(h, s, 1), U_EXP(h, s, 1))],
                           pending)
                pair0 = []
                for kg in range(B):
                    pair0.append(lambda h=h, s=s, kg=kg: mk_T(h, s, 0, kg))
                    if kg >= 1:
                        pair0.append(lambda h=h, s=s, kg=kg - 1: mk_PV(h, s, 0, kg))
                pair0.append(lambda h=h, s=s, kg=B - 1: mk_PV(h, s, 0, kg))
                interleave(A[2]
                           + [lambda h=h, s=s: (U_MX(h, s, 2), U_EXP(h, s, 2))]
                           + A[3]
                           + [lambda h=h, s=s: (U_MX(h, s, 3), U_EXP(h, s, 3))],
                           pair0)
                pending = []
                for kg in range(B):
                    pending.append(lambda h=h, s=s, kg=kg: mk_T(h, s, 1, kg))
                    if kg >= 1:
                        pending.append(lambda h=h, s=s, kg=kg - 1: mk_PV(h, s, 1, kg))
                pending.append(lambda h=h, s=s, kg=B - 1: mk_PV(h, s, 1, kg))
                pending.append(lambda h=h, s=s, la=exp_sc[3]:
                               oev_box.__setitem__(0, U_DANCE(h, s, la)))
            for u in pending:
                u()
            pending = []

        a3_sv = (n["sv0"], n["sv1"])
        a3_pe = n["pe"]

        @blk.tensor
        def _(e, w1=n["dv"], w2=n["sc"], w3=a3_sv, w4=n["gl"], w5=n["lh"],
              w6=(n["lm0"], n["lm1"]), w7=(n["gd0"], n["gd1"])):
            e.wait_ge(s_dv, w1)
            e.wait_ge(s_sc, w2)
            e.wait_ge(s_sv[0], w3[0])
            e.wait_ge(s_sv[1], w3[1])
            e.wait_ge(s_gl, w4)
            e.wait_ge(s_lh, w5)
            e.wait_ge(s_lm[0], w6[0])
            e.wait_ge(s_lm[1], w6[1])
            e.wait_ge(s_gd[0], w7[0])
            e.wait_ge(s_gd[1], w7[1])
            e.sem_inc(s_pe, 1)
        a4bar = bump("pe")
        p3s.close()

        p4 = ExitStack()
        sb4 = lambda nm, s, dt: p4.enter_context(nc.sbuf_tensor(nm, s, dt))
        o_ep = sb4("o_ep", [128, NH, 512], BF16)
        p_ep = sb4("p_ep", [128, NH, 512], BF16)
        g_ep = sb4("g_ep", [128, NH, 512], BF16)
        woT = sb4("woT", [128, NH, 1024], BF16)
        out32b = [sb4(f"out32b{i}", [128, D], F32) for i in range(2)]
        xo_ep2 = [sb4(f"xo_ep2{i}", [128, D], F32) for i in range(2)]
        xmid_sb2 = [sb4(f"xmid_sb2{i}", [128, D], F32) for i in range(2)]
        xm4 = xo_ep2
        xng2 = [sb4(f"xng2_{i}", [128, D], BF16) for i in range(2)]
        xngT2 = [sb4(f"xngT2_{i}", [128, D], BF16) for i in range(2)]
        x2t_a4 = sb4("x2t_a4", [128, D], F32)
        stats4 = sb4("stats4", [128, 2, 8], F32)
        wol_b = sb4("wol_b", [128, D], F32)
        bol_b = sb4("bol_b", [128, D], F32)
        gwT = sb4("gwT_sb", [128, DC, 8], BF16)
        lbb_b = sb4("lbb_b", [128, 8], F32)
        iota_sb = sb4("iota_sb", [128, 8], F32)
        gsc2 = [sb4(f"gsc2_{i}", [128, 8], F32) for i in range(2)]
        max8 = sb4("max8", [128, 8], F32)
        idx2 = sb4("idx2", [128, 8], mybir.dt.uint32)
        idxf = sb4("idxf", [128, 2], F32)
        eqA = sb4("eqA", [128, 8], F32)
        eqB = sb4("eqB", [128, 8], F32)
        gt = sb4("gt", [128, 8], F32)
        Wc2 = [sb4(f"Wc2_{i}", [128, 8], F32) for i in range(2)]
        Mc2 = [sb4(f"Mc2_{i}", [128, 8], F32) for i in range(2)]
        SCacc = sb4("SCacc", [8, 2], F32)

        @blk.sync
        def _(e, wbar=a4bar):
            e.wait_ge(s_pe, wbar)
            e.dma_start(out=wol_b[:], in_=olw_d[None, :].broadcast_to((128, D))).then_inc(s_ld, 16)
            e.dma_start(out=bol_b[:], in_=olb_d[None, :].broadcast_to((128, D))).then_inc(s_ld, 16)
            e.dma_start(out=lbb_b[:], in_=lbb_d.broadcast_to((128, 8))).then_inc(s_ld, 16)
            e.dma_start(out=iota_sb[:], in_=iota_d.broadcast_to((128, 8))).then_inc(s_ld, 16)
        ld_a4 = bump("ld", 64)

        @blk.vector
        def _(e, wbar=a4bar):
            e.wait_ge(s_pe, wbar)
            e.memset(SCacc[:], 0.0).then_inc(s_dv, 1)
        scacc_dv = bump("dv")

        @blk.gpsimd
        def _(e, wbar=a4bar):
            e.wait_ge(s_pe, wbar)
            e.dma_start(out=woT[:],
                        in_=w_out_d.rearrange("(j p) q -> p j q", p=128)).then_inc(s_lw, 16)
            e.dma_start(out=gwT[:],
                        in_=gw_d.rearrange("(c p) e -> p c e", p=128)).then_inc(s_lw, 16)
        bump("lw", 32)
        a4_lw = n["lw"]

        @blk.scalar
        def _(e, w=a4_lw):
            e.wait_ge(s_lw, w)
            e.sem_inc(s_sc, 1)
        a4prep_sc = bump("sc")

        lrow_all = sb4("lrow_all", [1, NH * 512], F32)
        linv_ep = sb4("linv_ep", [128, NH, 512], F32)

        g_mark = [0]
        sil_mark = [0]
        linv_pe_mark = [0]
        linv_ev_mark = [0, 0]
        wmm_mark_par = [0]
        evac_mark = [0, 0]
        st_mark = [0, 0]
        go_mark = [0, 0]
        dvg_slot = [0, 0]

        def mk_prep(s):
            st = {}

            def u_load(s=s):
                @blk.sync
                def _(e, s=s, wld=ld_a4, wgo=a3_sv, wdv=g_mark[0]):
                    e.wait_ge(s_ld, wld)
                    e.wait_ge(s_sv[0], wgo[0])
                    e.wait_ge(s_sv[1], wgo[1])
                    if wdv:
                        e.wait_ge(s_dv, wdv)
                    e.dma_start(out=o_ep[:],
                                in_=o_s[:, :, s * 512:(s + 1) * 512]
                                .rearrange("h p q -> p h q")).then_inc(s_ld, 16)
                    e.dma_start(out=p_ep[:],
                                in_=p_s[:, :, s * 512:(s + 1) * 512]
                                .rearrange("h p q -> p h q")).then_inc(s_ld, 16)
                st["ld_ep"] = bump("ld", 32)

                @blk.gpsimd
                def _(e, s=s, wpe=linv_pe_mark[0], wbar=a4bar):
                    if s == 0:
                        e.wait_ge(s_pe, wbar)
                    if wpe:
                        e.wait_ge(s_pe, wpe)
                    e.dma_start(out=lrow_all[:],
                                in_=l_s[s].rearrange("h c p -> (h c p)")[None, :]).then_inc(s_gl, 16)
                st["gl"] = bump("gl", 16)

            def u_recip():
                @blk.vector
                def _(e, w=st["gl"]):
                    e.wait_ge(s_gl, w)
                    e.reciprocal(lrow_all[:], lrow_all[:]).then_inc(s_dv, 1)
                st["dv_lr"] = bump("dv")

            def u_linv(hg):
                @blk.tensor
                def _(e, hg=hg, wdv=st["dv_lr"], wsc=linv_ev_mark[hg % 2]):
                    e.wait_ge(s_dv, wdv)
                    if wsc:
                        e.wait_ge(s_sc, wsc)
                    e.matmul(ps_a[hg % 2][:, 0:512], ones128f[:],
                             lrow_all[:, (2 * hg) * 512:(2 * hg + 1) * 512],
                             start=True, stop=True)
                    ins = e.matmul(ps_a[hg % 2][:, 512:1024], ones128f[:],
                                   lrow_all[:, (2 * hg + 1) * 512:(2 * hg + 2) * 512],
                                   start=True, stop=True)
                    ins.then_inc(s_pe, 1)
                pe_lb = bump("pe")
                linv_pe_mark[0] = pe_lb

                @blk.scalar
                def _(e, hg=hg, pe_lb=pe_lb, wdv=g_mark[0]):
                    e.wait_ge(s_pe, pe_lb)
                    if hg == 0 and wdv:
                        e.wait_ge(s_dv, wdv)
                    e.copy(linv_ep[:, 2 * hg:2 * hg + 2, :].rearrange("p h q -> p (h q)"),
                           ps_a[hg % 2][:]).then_inc(s_sc, 1)
                linv_ev_mark[hg % 2] = bump("sc")

            def u_sil(s=s):
                @blk.scalar
                def _(e, wld=st["ld_ep"], w=a4prep_sc, wpe=wmm_mark_par[0]):
                    e.wait_ge(s_ld, wld)
                    e.wait_ge(s_sc, w)
                    if wpe:
                        e.wait_ge(s_pe, wpe)
                    e.activation(out=g_ep[:].rearrange("p h q -> p (h q)"),
                                 in_=p_ep[:].rearrange("p h q -> p (h q)"),
                                 func=AF.Silu).then_inc(s_sc, 1)
                st["sil"] = bump("sc")
                sil_mark[0] = st["sil"]

            def u_gmul(s=s):
                @blk.vector
                def _(e, s=s, wsil=st["sil"], wlv=linv_ev_mark[1], base=n["dv"]):
                    e.wait_ge(s_sc, max(wsil, wlv))
                    e.tensor_mul(g_ep[:].rearrange("p h q -> p (h q)"),
                                 g_ep[:].rearrange("p h q -> p (h q)"),
                                 o_ep[:].rearrange("p h q -> p (h q)")).then_inc(s_dv, 1)
                    e.wait_ge(s_dv, base + 1)
                    e.tensor_mul(g_ep[:].rearrange("p h q -> p (h q)"),
                                 g_ep[:].rearrange("p h q -> p (h q)"),
                                 linv_ep[:].rearrange("p h q -> p (h q)")).then_inc(s_dv, 1)
                g_mark[0] = bump("dv", 2)
                dvg_slot[s % 2] = g_mark[0]

            units = [u_load, u_recip]
            units += [(lambda hg=hg: u_linv(hg)) for hg in range(8)]
            return units, [u_sil, u_gmul]

        def mk_wchain(s):
            units = []
            for tc in range(4):
                def u_wmm(tc=tc, oh=0, s=s):
                    psx = ps_c if oh == 0 else ps_d

                    @blk.tensor
                    def _(e, s=s, tc=tc, psx=psx, wdv=dvg_slot[s % 2],
                          wsc=max(evac_mark[oh], a4prep_sc)):
                        e.wait_ge(s_dv, wdv)
                        e.wait_ge(s_sc, wsc)
                        for j in range(NH):
                            ins = e.matmul(psx[:],
                                           g_ep[:, j, tc * 128:(tc + 1) * 128],
                                           woT[:, j, oh * 512:(oh + 1) * 512],
                                           start=(j == 0), stop=(j == NH - 1))
                        ins.then_inc(s_pe, 1)
                    st_pe = bump("pe")
                    wmm_mark_par[0] = st_pe

                    @blk.scalar
                    def _(e, tc=tc, oh=oh, psx=psx, wpe=st_pe, wdv=st_mark[tc % 2]):
                        e.wait_ge(s_pe, wpe)
                        if oh == 0 and wdv:
                            e.wait_ge(s_dv, wdv)
                        e.copy(out32b[tc % 2][:, oh * 512:(oh + 1) * 512],
                               psx[:]).then_inc(s_sc, 1)
                    evac_mark[oh] = bump("sc")

                def u_wmm1(tc=tc, s=s):
                    u_wmm(tc, 1, s)

                def u_xo(tc=tc, s=s):
                    @blk.sync
                    def _(e, s=s, tc=tc, wdv=st_mark[tc % 2], wbar=a4bar):
                        if wdv:
                            e.wait_ge(s_dv, wdv)
                        else:
                            e.wait_ge(s_pe, wbar)
                        e.dma_start(out=xo_ep2[tc % 2][:],
                                    in_=xo_d[(s * 4 + tc) * 128:(s * 4 + tc + 1) * 128, :]).then_inc(s_xo[tc % 2], 16)
                    st_ld[tc % 2] = bump(f"xo{tc % 2}", 16)

                def u_sq(tc=tc):
                    @blk.scalar
                    def _(e, tc=tc, wsc=evac_mark[1], wdv=st_mark[tc % 2]):
                        e.wait_ge(s_sc, wsc)
                        if wdv:
                            e.wait_ge(s_dv, wdv)
                        e.activation(out=x2t_a4[:], in_=out32b[tc % 2][:], func=AF.Square,
                                     accum_out=stats4[:, tc % 2, 1:2]).then_inc(s_sc, 1)
                    st_sq[tc % 2] = bump("sc")

                def u_stats(tc=tc):
                    @blk.vector
                    def _(e, tc=tc, wsc=st_sq[tc % 2], base=n["dv"]):
                        p = tc % 2
                        e.wait_ge(s_sc, wsc)
                        e.reduce_sum(stats4[:, p, 0:1], out32b[p][:], axis=AX.X).then_inc(s_dv, 1)
                        e.wait_ge(s_dv, base + 1)
                        e.tensor_scalar_mul(stats4[:, p, 2:3], stats4[:, p, 0:1], 1.0 / D).then_inc(s_dv, 1)
                        e.wait_ge(s_dv, base + 2)
                        e.tensor_mul(stats4[:, p, 3:4], stats4[:, p, 2:3], stats4[:, p, 2:3]).then_inc(s_dv, 1)
                        e.wait_ge(s_dv, base + 3)
                        e.scalar_tensor_tensor(out=stats4[:, p, 3:4], in0=stats4[:, p, 1:2],
                                               scalar=1.0 / D, in1=stats4[:, p, 3:4],
                                               op0=ALU.mult, op1=ALU.subtract).then_inc(s_dv, 1)
                        e.wait_ge(s_dv, base + 4)
                        e.tensor_scalar_add(stats4[:, p, 4:5], stats4[:, p, 3:4], EPS).then_inc(s_dv, 1)
                    bump("dv", 5)
                    st_dvs[tc % 2] = n["dv"]

                def u_sqrt(tc=tc):
                    @blk.scalar
                    def _(e, tc=tc, wdv=st_dvs[tc % 2]):
                        e.wait_ge(s_dv, wdv)
                        e.sqrt(stats4[:, tc % 2, 5:6], stats4[:, tc % 2, 4:5]).then_inc(s_sc, 1)
                    st_rt[tc % 2] = bump("sc")

                def u_apply(tc=tc, s=s):
                    @blk.vector
                    def _(e, s=s, tc=tc, wsc=st_rt[tc % 2], wld=st_ld[tc % 2],
                          wgo=go_mark[tc % 2], base=n["dv"]):
                        p = tc % 2
                        e.wait_ge(s_sc, wsc)
                        e.reciprocal(stats4[:, p, 6:7], stats4[:, p, 5:6]).then_inc(s_dv, 1)
                        e.wait_ge(s_dv, base + 1)
                        e.tensor_scalar(out=out32b[p][:], in0=out32b[p][:],
                                        scalar1=stats4[:, p, 2:3],
                                        scalar2=stats4[:, p, 6:7], op0=ALU.subtract,
                                        op1=ALU.mult).then_inc(s_dv, 1)
                        e.wait_ge(s_dv, base + 2)
                        e.tensor_mul(out32b[p][:], out32b[p][:], wol_b[:]).then_inc(s_dv, 1)
                        e.wait_ge(s_dv, base + 3)
                        e.tensor_add(out32b[p][:], out32b[p][:], bol_b[:]).then_inc(s_dv, 1)
                        e.wait_ge(s_dv, base + 4)
                        e.wait_ge(s_xo[tc % 2], wld)
                        if wgo:
                            e.wait_ge(s_xm[tc % 2], wgo)
                        e.tensor_add(xmid_sb2[p][:], out32b[p][:], xo_ep2[p][:]).then_inc(s_dv, 1)
                    bump("dv", 5)
                    st_mark[tc % 2] = n["dv"]

                def u_store(tc=tc, s=s):
                    @blk.gpsimd
                    def _(e, s=s, tc=tc, wdv=st_mark[tc % 2]):
                        e.wait_ge(s_dv, wdv)
                        e.dma_start(out=xmid_d[(s * 4 + tc) * 128:(s * 4 + tc + 1) * 128, :],
                                    in_=xmid_sb2[tc % 2][:]).then_inc(s_xm[tc % 2], 16)
                    go_mark[tc % 2] = bump(f"xm{tc % 2}", 16)

                units += [lambda tc=tc: u_wmm(tc, 0), u_wmm1, u_xo, u_sq,
                          u_stats, u_sqrt, u_apply, u_store]
            return units

        st_ld = [0, 0]
        st_sq = [0, 0]
        st_dvs = [0, 0]
        st_rt = [0, 0]

        pe0, pl0 = mk_prep(0)
        for u in pe0 + pl0:
            u()
        for s in range(8):
            pe_n, pl_n = mk_prep(s + 1) if s < 7 else ([], [])
            interleave(mk_wchain(s), pe_n)
            for u in pl_n:
                u()
        a4a_go = (n["xm0"], n["xm1"])

        bT_ev = [0, 0]
        bgmm = [0, 0]
        bgsc = [0]
        bsig = [0, 0]
        btop = [0, 0]
        bsca = [0]
        for t in range(32):
            p = t % 2

            @blk.sync
            def _(e, t=t, p=p, wgo=a4a_go, wdv=st_mark[p]):
                e.wait_ge(s_xm[0], wgo[0])
                e.wait_ge(s_xm[1], wgo[1])
                if wdv:
                    e.wait_ge(s_dv, wdv)
                e.dma_start(out=xm4[p][:],
                            in_=xmid_d[t * 128:(t + 1) * 128, :]).then_inc(s_xo[p], 16)
            b_ld = bump(f"xo{p}", 16)

            @blk.scalar
            def _(e, p=p, wld=b_ld, wdv=st_mark[p]):
                e.wait_ge(s_xo[p], wld)
                if wdv:
                    e.wait_ge(s_dv, wdv)
                e.activation(out=x2t_a4[:], in_=xm4[p][:], func=AF.Square,
                             accum_out=stats4[:, p, 1:2]).then_inc(s_sc, 1)
            b_sq = bump("sc")

            @blk.vector
            def _(e, p=p, wsc=b_sq, base=n["dv"]):
                e.wait_ge(s_sc, wsc)
                e.reduce_sum(stats4[:, p, 0:1], xm4[p][:], axis=AX.X).then_inc(s_dv, 1)
                e.wait_ge(s_dv, base + 1)
                e.tensor_scalar_mul(stats4[:, p, 2:3], stats4[:, p, 0:1], 1.0 / D).then_inc(s_dv, 1)
                e.wait_ge(s_dv, base + 2)
                e.tensor_mul(stats4[:, p, 3:4], stats4[:, p, 2:3], stats4[:, p, 2:3]).then_inc(s_dv, 1)
                e.wait_ge(s_dv, base + 3)
                e.scalar_tensor_tensor(out=stats4[:, p, 3:4], in0=stats4[:, p, 1:2],
                                       scalar=1.0 / D, in1=stats4[:, p, 3:4],
                                       op0=ALU.mult, op1=ALU.subtract).then_inc(s_dv, 1)
                e.wait_ge(s_dv, base + 4)
                e.tensor_scalar_add(stats4[:, p, 4:5], stats4[:, p, 3:4], EPS).then_inc(s_dv, 1)
            bump("dv", 5)
            b_dvs = n["dv"]

            @blk.scalar
            def _(e, p=p, wdv=b_dvs):
                e.wait_ge(s_dv, wdv)
                e.sqrt(stats4[:, p, 5:6], stats4[:, p, 4:5]).then_inc(s_sc, 1)
            b_rt = bump("sc")

            @blk.vector
            def _(e, p=p, wsc=b_rt, base=n["dv"]):
                e.wait_ge(s_sc, wsc)
                e.reciprocal(stats4[:, p, 6:7], stats4[:, p, 5:6]).then_inc(s_dv, 1)
                e.wait_ge(s_dv, base + 1)
                e.tensor_scalar(out=xng2[p][:], in0=xm4[p][:], scalar1=stats4[:, p, 2:3],
                                scalar2=stats4[:, p, 6:7], op0=ALU.subtract,
                                op1=ALU.mult).then_inc(s_dv, 1)
            bump("dv", 2)
            st_mark[p] = n["dv"]
            b_xng = n["dv"]

            @blk.tensor
            def _(e, p=p, wdv=b_xng, wsc=bT_ev[p]):
                e.wait_ge(s_dv, wdv)
                if wsc:
                    e.wait_ge(s_sc, wsc)
                for k in range(DC):
                    ins = e.transpose(ps_b[p][:, k * 128:(k + 1) * 128],
                                      xng2[p][:, k * 128:(k + 1) * 128], ident[:])
                ins.then_inc(s_pe, 1)
            b_pet = bump("pe")

            @blk.scalar
            def _(e, p=p, wpe=b_pet, wp2=bgmm[p]):
                e.wait_ge(s_pe, max(wpe, wp2))
                e.copy(xngT2[p][:], ps_b[p][:]).then_inc(s_sc, 1)
            bT_ev[p] = bump("sc")

            @blk.tensor
            def _(e, p=p, wsc=bT_ev[p], wdv=bgsc[0]):
                e.wait_ge(s_sc, wsc)
                if wdv:
                    e.wait_ge(s_dv, wdv)
                for k in range(DC):
                    ins = e.matmul(ps_d[:, 0:8], xngT2[p][:, k * 128:(k + 1) * 128],
                                   gwT[:, k, :], start=(k == 0), stop=(k == DC - 1))
                ins.then_inc(s_pe, 1)
            bgmm[p] = bump("pe")

            @blk.vector
            def _(e, p=p, wpe=bgmm[p], wsc=bsig[p]):
                e.wait_ge(s_pe, wpe)
                if wsc:
                    e.wait_ge(s_sc, wsc)
                e.tensor_add(gsc2[p][:], ps_d[:, 0:8], lbb_b[:]).then_inc(s_dv, 1)
            bgsc[0] = bump("dv")

            @blk.scalar
            def _(e, p=p, wdv=bgsc[0]):
                e.wait_ge(s_dv, wdv)
                e.activation(out=gsc2[p][:], in_=gsc2[p][:], func=AF.Sigmoid).then_inc(s_sc, 1)
            bsig[p] = bump("sc")

            @blk.vector
            def _(e, p=p, wsc=bsig[p], wpe=btop[p], base=n["dv"]):
                e.wait_ge(s_sc, wsc)
                if wpe:
                    e.wait_ge(s_pe, wpe)
                e.max(max8[:], gsc2[p][:]).then_inc(s_dv, 1)
                e.wait_ge(s_dv, base + 1)
                e.max_index(idx2[:], max8[:], gsc2[p][:]).then_inc(s_dv, 1)
                e.wait_ge(s_dv, base + 2)
                e.tensor_copy(idxf[:], idx2[:, 0:2]).then_inc(s_dv, 1)
                e.tensor_add(gt[:, 2:3], max8[:, 0:1], max8[:, 1:2]).then_inc(s_dv, 1)
                e.wait_ge(s_dv, base + 4)
                e.tensor_scalar_add(gt[:, 2:3], gt[:, 2:3], 1e-8).then_inc(s_dv, 1)
                e.wait_ge(s_dv, base + 5)
                e.reciprocal(gt[:, 3:4], gt[:, 2:3]).then_inc(s_dv, 1)
                e.wait_ge(s_dv, base + 6)
                e.tensor_mul(gt[:, 4:5], max8[:, 0:1], gt[:, 3:4]).then_inc(s_dv, 1)
                e.tensor_mul(gt[:, 5:6], max8[:, 1:2], gt[:, 3:4]).then_inc(s_dv, 1)
                e.wait_ge(s_dv, base + 8)
                e.tensor_scalar(out=eqA[:], in0=iota_sb[:], scalar1=idxf[:, 0:1],
                                scalar2=None, op0=ALU.is_equal).then_inc(s_dv, 1)
                e.tensor_scalar(out=eqB[:], in0=iota_sb[:], scalar1=idxf[:, 1:2],
                                scalar2=None, op0=ALU.is_equal).then_inc(s_dv, 1)
                e.wait_ge(s_dv, base + 10)
                e.tensor_add(Mc2[p][:], eqA[:], eqB[:]).then_inc(s_dv, 1)
                e.wait_ge(s_dv, base + 11)
                e.tensor_scalar(out=eqA[:], in0=eqA[:], scalar1=gt[:, 4:5],
                                scalar2=None, op0=ALU.mult).then_inc(s_dv, 1)
                e.tensor_scalar(out=eqB[:], in0=eqB[:], scalar1=gt[:, 5:6],
                                scalar2=None, op0=ALU.mult).then_inc(s_dv, 1)
                e.wait_ge(s_dv, base + 13)
                e.tensor_add(Wc2[p][:], eqA[:], eqB[:]).then_inc(s_dv, 1)
            bump("dv", 14)
            b_top2 = n["dv"]

            @blk.tensor
            def _(e, p=p, wdv=b_top2, wd2=bsca[0]):
                e.wait_ge(s_dv, wdv)
                if wd2:
                    e.wait_ge(s_dv, wd2)
                e.matmul(ps_c[0:8, 0:1], Wc2[p][:], onecol[:],
                         start=True, stop=True)
                ins = e.matmul(ps_c[0:8, 1:2], Mc2[p][:], onecol[:],
                               start=True, stop=True)
                ins.then_inc(s_pe, 1)
            btop[p] = bump("pe")

            @blk.vector
            def _(e, p=p, wpe=btop[p], base=n["dv"]):
                e.wait_ge(s_pe, wpe)
                e.tensor_add(SCacc[:, 0:1], SCacc[:, 0:1], ps_c[0:8, 0:1]).then_inc(s_dv, 1)
                e.wait_ge(s_dv, base + 1)
                e.tensor_add(SCacc[:, 1:2], SCacc[:, 1:2], ps_c[0:8, 1:2]).then_inc(s_dv, 1)
            bsca[0] = bump("dv", 2)

        a4_dv = n["dv"]
        a4_go = n["go"]

        @blk.gpsimd
        def _(e, wdv=a4_dv, wgo=a4_go):
            e.wait_ge(s_dv, wdv)
            e.dma_start(out=part_d.rearrange("c p -> p c"), in_=SCacc[:],
                        allow_slow_non_contiguous=True).then_inc(s_go, 16)
        bump("go", 16)

        @blk.gpsimd
        def _(e, wgo=n["go"]):
            if DBG_A >= 1:
                e.dma_start(out=dbg["kp"], in_=kp_s).then_inc(s_go, 16)
                e.dma_start(out=dbg["v"], in_=v_s).then_inc(s_go, 16)
                e.dma_start(out=dbg["q"], in_=q_s).then_inc(s_go, 16)
                e.dma_start(out=dbg["p"], in_=p_s).then_inc(s_go, 16)
                e.dma_start(out=dbg["pos3"], in_=pos3_s).then_inc(s_go, 16)
                e.dma_start(out=dbg["xnf"], in_=xnf_s).then_inc(s_go, 16)
                bump("go", 96)
            if DBG_A >= 2:
                e.dma_start(out=dbg["o"], in_=o_s).then_inc(s_go, 16)
                bump("go", 16)
            e.wait_ge(s_go, n["go"])
            e.wait_ge(s_xm[0], n["xm0"])
            e.wait_ge(s_xm[1], n["xm1"])
            e.wait_ge(s_xo[0], n["xo0"])
            e.wait_ge(s_xo[1], n["xo1"])
        p4.close()

    return nc





TCH = 1024


def build_B():
    nc = bass.Bass("TRN2", target_bir_lowering=False)
    dr = lambda nm, s, dt, k="ExternalInput": nc.dram_tensor(nm, s, dt, kind=k).ap()

    xm_d = dr("xm", [TO, D], F32)
    pall_d = dr("pall", [8, 16], F32)
    gw_d = dr("gwT", [D, 8], F32)
    lbb_d = dr("lbb", [1, 8], F32)
    egw_d = dr("egwT", [8, D, 512], F32)
    euw_d = dr("euwT", [8, D, 512], F32)
    edw_d = dr("edwT", [8, 512, D], F32)
    sgw_d = dr("sgwT", [D, D], F32)
    suw_d = dr("suwT", [D, D], F32)
    sdw_d = dr("sdwT", [D, D], F32)
    ident_d = dr("ident", [128, 128], BF16)
    out_d = dr("out", [TO, D], F32, "ExternalOutput")

    scr = lambda nm, s, dt: nc.dram_tensor(nm, s, dt, kind="Internal").ap()
    wm_s = scr("wm_s", [1, 8], F32)

    gctx = ExitStack()
    gsb = lambda nm, s, dt: gctx.enter_context(nc.sbuf_tensor(nm, s, dt))
    pst = lambda nm, s, dt: gctx.enter_context(nc.psum_tensor(nm, s, dt))
    sem = lambda nm: gctx.enter_context(nc.semaphore(nm))

    ident = gsb("ident_s", [128, 128], BF16)
    part = gsb("part_s", [8, 16], F32)
    ones8 = gsb("ones8", [8, 1], F32)
    wvec = gsb("wvec", [8, 4], F32)
    wme_b = gsb("wme_b", [128, 8], F32)
    lbb_b = gsb("lbb_b", [128, 8], F32)
    gwT = gsb("gwT_sb", [128, DC, 8], BF16)
    stats = gsb("stats", [128, 8], F32)
    x2t = gsb("x2t", [128, D], F32)
    xin = [gsb(f"xin{i}", [128, D], F32) for i in range(2)]
    xng = [gsb(f"xng{i}", [128, D], BF16) for i in range(2)]
    xnfm = gsb("xnfm", [128, DC, TCH], BF16)
    lam = gsb("lam", [128, TO // 128, 8], F32)
    gsc = gsb("gsc", [128, 8], F32)
    max8 = gsb("max8", [128, 8], F32)
    idx2 = gsb("idx2", [128, 8], mybir.dt.uint32)
    gt = gsb("gt", [128, 8], F32)
    Mc = gsb("Mc", [128, 8], F32)
    wea = [gsb(f"wea{i}", [128, DC, 512], BF16) for i in range(2)]
    web = [gsb(f"web{i}", [128, DC, 512], BF16) for i in range(2)]
    wed = [gsb(f"wed{i}", [128, 4, D], BF16) for i in range(2)]
    wsa = gsb("wsa", [128, DC, D], BF16)
    wsb_ = gsb("wsb", [128, DC, D], BF16)
    wsd = gsb("wsd", [128, DC, D], BF16)
    hst = gsb("hst", [128, 512], BF16)
    hut = gsb("hut", [128, 512], BF16)
    hfm = gsb("hfm", [128, 8, TCH], BF16)
    acc = gsb("acc", [128, TCH // 128, D], F32)
    xmi = gsb("xmi", [128, D], F32)
    outt = gsb("outt", [128, D], F32)

    ps_a = [pst(f"ps_a{i}", [128, 1024], F32) for i in range(2)]
    ps_b = [pst(f"ps_b{i}", [128, 1024], BF16) for i in range(2)]
    ps_c = pst("ps_c", [128, 512], F32)
    ps_d = pst("ps_d", [128, 512], F32)

    s_lc = sem("s_lc")
    s_li = [sem("s_li0"), sem("s_li1")]
    s_lw = sem("s_lw")
    s_lwa = [sem("s_lwa0"), sem("s_lwa1")]
    s_lsh = sem("s_lsh")
    s_lx = sem("s_lx")
    s_pe = sem("s_pe")
    s_dv = sem("s_dv")
    s_sc = sem("s_sc")
    s_go = sem("s_go")
    s_gl = sem("s_gl")

    n = {"lc": 0, "li0": 0, "li1": 0, "lw": 0, "lwa0": 0, "lwa1": 0, "lx": 0,
         "lsh": 0, "pe": 0, "dv": 0, "sc": 0, "go": 0, "gl": 0}

    def bump(k, i=1):
        n[k] += i
        return n[k]

    with nc.Block() as blk:

        @blk.sync
        def _(e):
            e.dma_start(out=ident[:], in_=ident_d).then_inc(s_lc, 16)
            e.dma_start(out=part[:], in_=pall_d).then_inc(s_lc, 16)
            e.dma_start(out=lbb_b[:], in_=lbb_d.broadcast_to((128, 8))).then_inc(s_lc, 16)
        bump("lc", 48)

        @blk.vector
        def _(e, w=n["lc"]):
            e.wait_ge(s_lc, w)
            e.memset(ones8[:], 1.0).then_inc(s_dv, 1)
        dv0 = bump("dv")

        @blk.tensor
        def _(e, dv0=dv0):
            e.wait_ge(s_dv, dv0)
            e.matmul(ps_d[0:8, 0:1], part[:, 0:8], ones8[:], start=True, stop=True)
            ins = e.matmul(ps_d[0:8, 1:2], part[:, 8:16], ones8[:], start=True, stop=True)
            ins.then_inc(s_pe, 1)
        pe0 = bump("pe")

        @blk.vector
        def _(e, pe0=pe0, base=n["dv"]):
            e.wait_ge(s_pe, pe0)
            e.tensor_copy(wvec[:, 0:1], ps_d[0:8, 0:1]).then_inc(s_dv, 1)
            e.tensor_copy(wvec[:, 1:2], ps_d[0:8, 1:2]).then_inc(s_dv, 1)
            e.wait_ge(s_dv, base + 2)
            e.tensor_scalar_max(wvec[:, 2:3], wvec[:, 1:2], 1.0).then_inc(s_dv, 1)
            e.wait_ge(s_dv, base + 3)
            e.reciprocal(wvec[:, 2:3], wvec[:, 2:3]).then_inc(s_dv, 1)
            e.wait_ge(s_dv, base + 4)
            e.tensor_mul(wvec[:, 3:4], wvec[:, 0:1], wvec[:, 2:3]).then_inc(s_dv, 1)
        dv_wm = bump("dv", 5)

        @blk.gpsimd
        def _(e, dv_wm=dv_wm, base=n["gl"]):
            e.wait_ge(s_dv, dv_wm)
            e.dma_start(out=wm_s[0][:, None], in_=wvec[:, 3:4]).then_inc(s_gl, 16)
            e.wait_ge(s_gl, base + 16)
            e.dma_start(out=wme_b[:], in_=wm_s.broadcast_to((128, 8))).then_inc(s_gl, 16)
        gl_wm = bump("gl", 32)

        @blk.gpsimd
        def _(e):
            e.dma_start(out=gwT[:],
                        in_=gw_d.rearrange("(c p) e -> p c e", p=128)).then_inc(s_lw, 16)
        bump("lw", 16)

        @blk.scalar
        def _(e, w=n["lw"]):
            e.wait_ge(s_lw, w)
            e.sem_inc(s_sc, 1)
        bump("sc")

        b1_go = n["go"]

        @blk.gpsimd
        def _(e):
            e.dma_start(out=wsa[:],
                        in_=sgw_d.rearrange("(c p) q -> p c q", p=128)).then_inc(s_lsh, 16)
            e.dma_start(out=wsb_[:],
                        in_=suw_d.rearrange("(c p) q -> p c q", p=128)).then_inc(s_lsh, 16)
            e.dma_start(out=wsd[:],
                        in_=sdw_d.rearrange("(c p) q -> p c q", p=128)).then_inc(s_lsh, 16)
        lc_sh = bump("lsh", 48)

        psfree = {}
        ev_dv = [0, 0]
        ev_sc = [0, 0]
        wst_dv = {}
        out_go = 0
        for ch in range(TO // TCH):
            t0 = ch * TCH
            for tl in range(TCH // 128):
                t = ch * (TCH // 128) + tl
                b = t % 2

                @blk.sync
                def _(e, b=b, t=t, w=ev_dv[b]):
                    if w:
                        e.wait_ge(s_dv, w)
                    e.dma_start(out=xin[b][:], in_=xm_d[t * 128:(t + 1) * 128, :]).then_inc(s_li[b], 16)
                ld_t = bump(f"li{b}", 16)

                @blk.scalar
                def _(e, b=b, ld_t=ld_t):
                    e.wait_ge(s_li[b], ld_t)
                    e.activation(out=x2t[:], in_=xin[b][:], func=AF.Square,
                                 accum_out=stats[:, 1:2]).then_inc(s_sc, 1)
                sc_sq = bump("sc")

                @blk.vector
                def _(e, b=b, sc_sq=sc_sq, base=n["dv"]):
                    e.wait_ge(s_sc, sc_sq)
                    e.reduce_sum(stats[:, 0:1], xin[b][:], axis=AX.X).then_inc(s_dv, 1)
                    e.wait_ge(s_dv, base + 1)
                    e.tensor_scalar_mul(stats[:, 2:3], stats[:, 0:1], 1.0 / D).then_inc(s_dv, 1)
                    e.wait_ge(s_dv, base + 2)
                    e.tensor_mul(stats[:, 3:4], stats[:, 2:3], stats[:, 2:3]).then_inc(s_dv, 1)
                    e.wait_ge(s_dv, base + 3)
                    e.scalar_tensor_tensor(out=stats[:, 3:4], in0=stats[:, 1:2],
                                           scalar=1.0 / D, in1=stats[:, 3:4],
                                           op0=ALU.mult, op1=ALU.subtract).then_inc(s_dv, 1)
                    e.wait_ge(s_dv, base + 4)
                    e.tensor_scalar_add(stats[:, 4:5], stats[:, 3:4], EPS).then_inc(s_dv, 1)
                bump("dv", 5)
                dv_ab = n["dv"]

                @blk.scalar
                def _(e, dv_ab=dv_ab):
                    e.wait_ge(s_dv, dv_ab)
                    e.sqrt(stats[:, 5:6], stats[:, 4:5]).then_inc(s_sc, 1)
                sc_rt = bump("sc")

                @blk.vector
                def _(e, b=b, sc_rt=sc_rt, base=n["dv"]):
                    e.wait_ge(s_sc, sc_rt)
                    e.reciprocal(stats[:, 6:7], stats[:, 5:6]).then_inc(s_dv, 1)
                    e.wait_ge(s_dv, base + 1)
                    e.tensor_scalar(out=xng[b][:], in0=xin[b][:], scalar1=stats[:, 2:3],
                                    scalar2=stats[:, 6:7], op0=ALU.subtract,
                                    op1=ALU.mult).then_inc(s_dv, 1)
                bump("dv", 2)
                ev_dv[b] = n["dv"]
                dv_nrm = n["dv"]

                @blk.tensor
                def _(e, b=b, tl=tl, dv_nrm=dv_nrm, w=ev_sc[b], wpe=n["pe"]):
                    e.wait_ge(s_dv, dv_nrm)
                    if tl == 0:
                        e.wait_ge(s_pe, wpe)
                    if w:
                        e.wait_ge(s_sc, w)
                    for k in range(DC):
                        ins = e.transpose(ps_b[b][:, k * 128:(k + 1) * 128],
                                          xng[b][:, k * 128:(k + 1) * 128], ident[:])
                    ins.then_inc(s_pe, 1)
                pe_tr = bump("pe")

                @blk.scalar
                def _(e, b=b, tl=tl, pe_tr=pe_tr):
                    e.wait_ge(s_pe, pe_tr)
                    e.copy(xnfm[:, :, tl * 128:(tl + 1) * 128],
                           ps_b[b][:].rearrange("p (c q) -> p c q", c=DC)).then_inc(s_sc, 1)
                ev_sc[b] = bump("sc")
                sc_fm = n["sc"]

                @blk.tensor
                def _(e, tl=tl, sc_fm=sc_fm, wdv=n["dv"]):
                    e.wait_ge(s_sc, sc_fm)
                    e.wait_ge(s_dv, wdv)
                    for k in range(DC):
                        ins = e.matmul(ps_d[:, 0:8], xnfm[:, k, tl * 128:(tl + 1) * 128],
                                       gwT[:, k, :], start=(k == 0), stop=(k == DC - 1))
                    ins.then_inc(s_pe, 1)
                pe_gt = bump("pe")

                @blk.vector
                def _(e, pe_gt=pe_gt):
                    e.wait_ge(s_pe, pe_gt)
                    e.tensor_add(gsc[:], ps_d[:, 0:8], lbb_b[:]).then_inc(s_dv, 1)
                dv_gl = bump("dv")

                @blk.scalar
                def _(e, dv_gl=dv_gl):
                    e.wait_ge(s_dv, dv_gl)
                    e.activation(out=gsc[:], in_=gsc[:], func=AF.Sigmoid).then_inc(s_sc, 1)
                sc_gs = bump("sc")

                @blk.vector
                def _(e, t=t, sc_gs=sc_gs, wgl=gl_wm, base=n["dv"]):
                    e.wait_ge(s_sc, sc_gs)
                    e.wait_ge(s_gl, wgl)
                    e.max(max8[:], gsc[:]).then_inc(s_dv, 1)
                    e.wait_ge(s_dv, base + 1)
                    e.max_index(idx2[:], max8[:], gsc[:]).then_inc(s_dv, 1)
                    for e8 in range(8):
                        e.wait_ge(s_dv, base + 2 + e8 * 4)
                        e.tensor_single_scalar(gt[:, 0:1], idx2[:, 0:1], float(e8),
                                               ALU.is_equal).then_inc(s_dv, 1)
                        e.tensor_single_scalar(gt[:, 1:2], idx2[:, 1:2], float(e8),
                                               ALU.is_equal).then_inc(s_dv, 1)
                        e.wait_ge(s_dv, base + 4 + e8 * 4)
                        e.tensor_add(Mc[:, e8:e8 + 1], gt[:, 0:1], gt[:, 1:2]).then_inc(s_dv, 1)
                        e.wait_ge(s_dv, base + 5 + e8 * 4)
                        e.tensor_scalar(out=lam[:, t, e8:e8 + 1], in0=Mc[:, e8:e8 + 1],
                                        scalar1=wme_b[:, e8:e8 + 1], scalar2=None,
                                        op0=ALU.mult).then_inc(s_dv, 1)
                bump("dv", 2 + 8 * 4)

            @blk.vector
            def _(e, ch=ch, w=n["sc"], wo=out_go, base=n["dv"]):
                e.wait_ge(s_sc, w)
                if wo:
                    e.wait_ge(s_go, wo)
                e.memset(acc[:], 0.0).then_inc(s_dv, 1)
            acc0_dv = bump("dv")

            for ex in range(9):
                shared = ex == 8
                nhc = 8 if shared else 4
                if not shared:
                    bw = ex % 2

                    @blk.gpsimd
                    def _(e, ex=ex, bw=bw, w=wst_dv.get(bw, 0)):
                        if w:
                            e.wait_ge(s_pe, w)
                        e.dma_start(out=wea[bw][:],
                                    in_=egw_d[ex].rearrange("(c p) q -> p c q", p=128)).then_inc(s_lwa[bw], 16)
                        e.dma_start(out=web[bw][:],
                                    in_=euw_d[ex].rearrange("(c p) q -> p c q", p=128)).then_inc(s_lwa[bw], 16)
                        e.dma_start(out=wed[bw][:],
                                    in_=edw_d[ex].rearrange("(c p) q -> p c q", p=128)).then_inc(s_lwa[bw], 16)
                    ld_w = bump(f"lwa{bw}", 48)

                for hc in range(nhc):
                    for ts in range(TCH // 512):
                        hp = (hc * (TCH // 512) + ts) % 2
                        pslots = {0: ps_a[0][:, hp * 512:(hp + 1) * 512],
                                  1: ps_a[1][:, hp * 512:(hp + 1) * 512]}
                        ev_mark = {}
                        for which in range(2):
                            @blk.tensor
                            def _(e, ex=ex, hc=hc, ts=ts, which=which, ch=ch, hp=hp,
                                  shared=shared, ld_w=ld_w if not shared else lc_sh,
                                  wev=psfree.get((which, hp)), bw=(ex % 2) if not shared else 0):
                                if not shared:
                                    e.wait_ge(s_lwa[bw], ld_w)
                                else:
                                    e.wait_ge(s_lsh, ld_w)
                                if wev is not None:
                                    e.wait_ge(s_dv if wev[0] == "dv" else s_sc, wev[1])
                                wsrc = (wsa if shared else wea[bw]) if which == 0 else (wsb_ if shared else web[bw])
                                for k in range(DC):
                                    ins = e.matmul(
                                        pslots[which],
                                        wsrc[:, k, hc * 128:(hc + 1) * 128],
                                        xnfm[:, k, ts * 512:(ts + 1) * 512],
                                        start=(k == 0), stop=(k == DC - 1))
                                ins.then_inc(s_pe, 1)
                            pe_h = bump("pe")

                            if which == 0:
                                @blk.scalar
                                def _(e, pe_h=pe_h, hp=hp, wdv=n["dv"]):
                                    e.wait_ge(s_pe, pe_h)
                                    e.wait_ge(s_dv, wdv)
                                    e.activation(out=hst[:], in_=pslots[0],
                                                 func=AF.Silu).then_inc(s_sc, 1)
                                ev_mark[0] = ("sc", bump("sc"))
                                psfree[(0, hp)] = ev_mark[0]
                            else:
                                @blk.vector
                                def _(e, pe_h=pe_h, hp=hp, wdv=n["dv"]):
                                    e.wait_ge(s_pe, pe_h)
                                    e.wait_ge(s_dv, wdv)
                                    e.tensor_copy(hut[:], pslots[1]).then_inc(s_dv, 1)
                                ev_mark[1] = ("dv", bump("dv"))
                                psfree[(1, hp)] = ev_mark[1]

                        @blk.vector
                        def _(e, hc=hc, ts=ts, wsc=ev_mark[0], wdv=ev_mark[1]):
                            e.wait_ge(s_sc, wsc[1])
                            e.wait_ge(s_dv, wdv[1])
                            e.tensor_mul(hfm[:, hc, ts * 512:(ts + 1) * 512],
                                         hst[:], hut[:]).then_inc(s_dv, 1)
                        bump("dv")
                hdone_dv = n["dv"]

                for ts in range(TCH // 128):
                    for oh in range(2):
                        psx = ps_c if oh == 0 else ps_d

                        @blk.tensor
                        def _(e, ex=ex, ts=ts, oh=oh, psx=psx, shared=shared,
                              hdone_dv=hdone_dv, wdv=n["dv"], nhc=nhc,
                              bw=(ex % 2) if not shared else 0):
                            e.wait_ge(s_dv, max(hdone_dv, wdv))
                            wdsrc = wsd if shared else wed[bw]
                            for k in range(nhc):
                                ins = e.matmul(
                                    psx[:],
                                    hfm[:, k, ts * 128:(ts + 1) * 128],
                                    wdsrc[:, k if shared else k, oh * 512:(oh + 1) * 512],
                                    start=(k == 0), stop=(k == nhc - 1))
                            ins.then_inc(s_pe, 1)
                        pe_d = bump("pe")

                        @blk.vector
                        def _(e, ex=ex, ts=ts, oh=oh, psx=psx, shared=shared,
                              ch=ch, pe_d=pe_d, base=n["dv"]):
                            e.wait_ge(s_pe, pe_d)
                            e.wait_ge(s_dv, base)
                            if shared:
                                e.tensor_add(acc[:, ts, oh * 512:(oh + 1) * 512],
                                             acc[:, ts, oh * 512:(oh + 1) * 512],
                                             psx[:]).then_inc(s_dv, 1)
                            else:
                                e.scalar_tensor_tensor(
                                    out=acc[:, ts, oh * 512:(oh + 1) * 512],
                                    in0=psx[:],
                                    scalar=lam[:, ch * (TCH // 128) + ts, ex:ex + 1],
                                    in1=acc[:, ts, oh * 512:(oh + 1) * 512],
                                    op0=ALU.mult, op1=ALU.add).then_inc(s_dv, 1)
                        bump("dv")
                if not shared:
                    wst_dv[ex % 2] = n["pe"]

            for ts in range(TCH // 128):
                @blk.sync
                def _(e, ch=ch, ts=ts, w=n["dv"]):
                    e.wait_ge(s_dv, w)
                    e.dma_start(out=xmi[:],
                                in_=xm_d[(t0 + ts * 128):(t0 + (ts + 1) * 128), :]).then_inc(s_lx, 16)
                ld_xm = bump("lx", 16)

                @blk.vector
                def _(e, ts=ts, ld_xm=ld_xm, base=n["dv"], wo=out_go):
                    e.wait_ge(s_lx, ld_xm)
                    e.wait_ge(s_dv, base)
                    if wo:
                        e.wait_ge(s_go, wo)
                    e.tensor_add(outt[:], acc[:, ts, :], xmi[:]).then_inc(s_dv, 1)
                dv_out = bump("dv")

                @blk.gpsimd
                def _(e, ch=ch, ts=ts, dv_out=dv_out):
                    e.wait_ge(s_dv, dv_out)
                    e.dma_start(out=out_d[(t0 + ts * 128):(t0 + (ts + 1) * 128), :],
                                in_=outt[:]).then_inc(s_go, 16)
                out_go = bump("go", 16)

        @blk.gpsimd
        def _(e, w=n["go"]):
            e.wait_ge(s_go, w)

    return nc



def _core_masks(typ):
    needs = NEEDS[typ]
    m = np.zeros((8, 4, 128, 1024), dtype=np.float32)
    for s, bud in enumerate(SLOT_BUD):
        pos = needs[s] - 1
        k0 = (bud - 2) * 512
        kj = k0 + np.arange(1024)[None, :]
        for c in range(4):
            qpos = pos * 512 + c * 128 + np.arange(128)[:, None]
            m[s, c] = np.where(kj <= qpos, 0.0, NEG)
    return m.astype(BF16NP)


def _stage_inputs(inputs):
    x = np.ascontiguousarray(inputs["L0_attn_x"])
    common = {
        "w_inT": np.ascontiguousarray(inputs["L0_attn_W_in"].T),
        "b_in": np.ascontiguousarray(inputs["L0_attn_b_in"]),
        "ilw": np.ascontiguousarray(inputs["L0_attn_in_ln_w"]),
        "ilb": np.ascontiguousarray(inputs["L0_attn_in_ln_b"]),
        "w_outT": np.ascontiguousarray(inputs["L0_attn_W_out"].T),
        "olw": np.ascontiguousarray(inputs["L0_attn_out_ln_w"]),
        "olb": np.ascontiguousarray(inputs["L0_attn_out_ln_b"]),
        "smf": np.ascontiguousarray(inputs["L0_attn_smear_factor"]).reshape(1, NH),
        "lgs": np.ascontiguousarray(inputs["L0_attn_log_scale"]).reshape(1, NH),
        "gwT": np.ascontiguousarray(inputs["L0_ffn_gate_w"].T),
        "lbb": np.ascontiguousarray(inputs["L0_ffn_lb_bias"]).reshape(1, 8),
        "ident": np.eye(128, dtype=BF16NP),
        "ones3": np.ones((3, 128), dtype=BF16NP),
        "iota8": np.arange(8, dtype=np.float32).reshape(1, 8),
    }
    in_maps = []
    for c in range(8):
        bt, typ = c // 2, c % 2
        needs = NEEDS[typ]
        xo = np.concatenate([x[bt, (nn - 1) * 512:nn * 512] for nn in needs], 0)
        in_maps.append({"xf": np.ascontiguousarray(x[bt]),
                        "xo": np.ascontiguousarray(xo),
                        "masks": _core_masks(typ), **common})
    return x, in_maps


def kernel(**inputs):
    x, in_maps = _stage_inputs(inputs)
    ncA = build_A()
    resA = run_bass_kernel_spmd(ncA, in_maps, core_ids=list(range(8)))

    pall = np.stack([resA.results[c]["part"].reshape(16) for c in range(8)])
    ncB = build_B()
    common_b = {
        "pall": pall,
        "gwT": np.ascontiguousarray(inputs["L0_ffn_gate_w"].T),
        "lbb": np.ascontiguousarray(inputs["L0_ffn_lb_bias"]).reshape(1, 8),
        "egwT": np.ascontiguousarray(inputs["L0_ffn_expert_gate_w"].transpose(0, 2, 1)),
        "euwT": np.ascontiguousarray(inputs["L0_ffn_expert_up_w"].transpose(0, 2, 1)),
        "edwT": np.ascontiguousarray(inputs["L0_ffn_expert_down_w"].transpose(0, 2, 1)),
        "sgwT": np.ascontiguousarray(inputs["L0_ffn_shared_gate_w"].T),
        "suwT": np.ascontiguousarray(inputs["L0_ffn_shared_up_w"].T),
        "sdwT": np.ascontiguousarray(inputs["L0_ffn_shared_down_w"].T),
        "ident": np.eye(128, dtype=BF16NP),
    }
    in_maps_b = [{"xm": resA.results[c]["xmid"], **common_b} for c in range(8)]
    resB = run_bass_kernel_spmd(ncB, in_maps_b, core_ids=list(range(8)))

    out = np.zeros_like(x)
    for c in range(8):
        bt, typ = c // 2, c % 2
        needs = NEEDS[typ]
        ob = resB.results[c]["out"]
        for s, nn in enumerate(needs):
            out[bt, (nn - 1) * 512:nn * 512] = ob[s * 512:(s + 1) * 512]
    return out


# version 12 (final)
# speedup vs baseline: 1.0636x; 1.0043x over previous
import numpy as np
import ml_dtypes
from contextlib import ExitStack

import concourse.bass as bass
import concourse.mybir as mybir
from concourse.bass_utils import run_bass_kernel_spmd

F32 = mybir.dt.float32
BF16 = mybir.dt.bfloat16
AF = mybir.ActivationFunctionType
ALU = mybir.AluOpType
AX = mybir.AxisListType
BF16NP = ml_dtypes.bfloat16

D = 1024
DC = 8
NH = 16
DH = 128
TF = 8192
TO = 4096
EPS = 1e-5
NEG = -1e10

SLOT_BUD = [16, 14, 12, 10, 8, 6, 4, 2]
NEEDS = {0: [16, 13, 12, 9, 8, 5, 4, 1],
         1: [15, 14, 11, 10, 7, 6, 3, 2]}

DBG_A = 0


def build_A():
    nc = bass.Bass("TRN2", target_bir_lowering=False)
    dr = lambda nm, s, dt, k="ExternalInput": nc.dram_tensor(nm, s, dt, kind=k).ap()

    xf_d = dr("xf", [TF, D], F32)
    xo_d = dr("xo", [TO, D], F32)
    w_in_d = dr("w_inT", [D, 8208], F32)
    b_in_d = dr("b_in", [8208], F32)
    ilw_d = dr("ilw", [D], F32)
    ilb_d = dr("ilb", [D], F32)
    w_out_d = dr("w_outT", [2048, D], F32)
    olw_d = dr("olw", [D], F32)
    olb_d = dr("olb", [D], F32)
    smf_d = dr("smf", [1, NH], F32)
    lgs_d = dr("lgs", [1, NH], F32)
    gw_d = dr("gwT", [D, 8], F32)
    lbb_d = dr("lbb", [1, 8], F32)
    iota_d = dr("iota8", [1, 8], F32)
    masks_d = dr("masks", [8, 4, 128, 1024], BF16)
    ident_d = dr("ident", [128, 128], BF16)
    ones3_d = dr("ones3", [3, 128], BF16)

    xmid_d = dr("xmid", [TO, D], F32, "ExternalOutput")
    part_d = dr("part", [2, 8], F32, "ExternalOutput")

    scr = lambda nm, s, dt: nc.dram_tensor(nm, s, dt, kind="Internal").ap()
    xnf_s = scr("xnf_s", [DC, 128, TF], BF16)
    xno_s = scr("xno_s", [DC, 128, TO], BF16)
    kp_s = scr("kp_s", [NH, 128, TF], BF16)
    v_s = scr("v_s", [NH, TF // 128, 128, DH], BF16)
    q_s = scr("q_s", [NH, 128, TO], BF16)
    p_s = scr("p_s", [NH, 128, TO], BF16)
    pos3_s = scr("pos3_s", [NH, 3, TF], BF16)
    hs_s = scr("hs_s", [3, NH], F32)
    pos_s = scr("pos_s", [16, TF], F32)
    o_s = scr("o_s", [NH, 128, TO], BF16)
    l4_s = scr("l4_s", [4, 128], F32)
    l_s = scr("l_s", [8, NH, 4, 128], F32)
    sc8_s = scr("sc8_s", [8, 2], F32)

    dbg = {}
    if DBG_A >= 1:
        dbg["kp"] = dr("dbg_kp", [NH, 128, TF], BF16, "ExternalOutput")
        dbg["v"] = dr("dbg_v", [NH, TF // 128, 128, DH], BF16, "ExternalOutput")
        dbg["q"] = dr("dbg_q", [NH, 128, TO], BF16, "ExternalOutput")
        dbg["p"] = dr("dbg_p", [NH, 128, TO], BF16, "ExternalOutput")
        dbg["pos3"] = dr("dbg_pos3", [NH, 3, TF], BF16, "ExternalOutput")
        dbg["xnf"] = dr("dbg_xnf", [DC, 128, TF], BF16, "ExternalOutput")
    if DBG_A >= 2:
        dbg["o"] = dr("dbg_o", [NH, 128, TO], BF16, "ExternalOutput")

    gctx = ExitStack()
    gsb = lambda nm, s, dt: gctx.enter_context(nc.sbuf_tensor(nm, s, dt))
    pst = lambda nm, s, dt: gctx.enter_context(nc.psum_tensor(nm, s, dt))
    sem = lambda nm: gctx.enter_context(nc.semaphore(nm))

    ident = gsb("ident_s", [128, 128], BF16)
    hvec = gsb("hvec", [1, 3 * NH], F32)
    sig_b = gsb("sig_b", [128, NH], F32)
    oms_b = gsb("oms_b", [128, NH], F32)
    alp_b = gsb("alp_b", [128, NH], F32)
    bcol = gsb("bcol", [128, 65], F32)
    statsA = gsb("statsA", [128, 2, 8], F32)
    stats = gsb("stats", [128, 8], F32)
    seedc = gsb("seedc", [128, 1], F32)
    ones128f = gsb("ones128f", [1, 128], F32)
    onecol = gsb("onecol", [128, 1], F32)
    ones3 = gsb("ones3_s", [3, 128], BF16)
    carry = gsb("carry", [128, NH], F32)
    bvb = gsb("bvb", [128, 2048], F32)

    ps_a = [pst(f"ps_a{i}", [128, 1024], F32) for i in range(2)]
    ps_b = [pst(f"ps_b{i}", [128, 1024], BF16) for i in range(2)]
    ps_c = pst("ps_c", [128, 512], F32)
    ps_d = pst("ps_d", [128, 512], F32)

    s_ld = sem("s_ld")
    s_gl = sem("s_gl")
    s_pe = sem("s_pe")
    s_dv = sem("s_dv")
    s_sc = sem("s_sc")
    s_go = sem("s_go")
    s_lc = sem("s_lc")
    s_li = [sem("s_li0"), sem("s_li1")]
    s_lh = sem("s_lh")
    s_lm = [sem("s_lm0"), sem("s_lm1")]
    s_lw = sem("s_lw")
    s_sx = [sem("s_sx0"), sem("s_sx1")]
    s_sk = [sem("s_sk0"), sem("s_sk1")]
    s_sv = [sem("s_sv0"), sem("s_sv1")]
    s_gd = [sem("s_gd0"), sem("s_gd1")]
    s_xo = [sem("s_xo0"), sem("s_xo1")]
    s_xm = [sem("s_xm0"), sem("s_xm1")]

    n = {"ld": 0, "gl": 0, "pe": 0, "dv": 0, "sc": 0, "go": 0,
         "lc": 0, "li0": 0, "li1": 0, "lh": 0, "lm0": 0, "lm1": 0, "lw": 0,
         "sx0": 0, "sx1": 0, "sk0": 0, "sk1": 0, "sv0": 0, "sv1": 0,
         "gd0": 0, "gd1": 0, "xo0": 0, "xo1": 0, "xm0": 0, "xm1": 0}

    def bump(k, i=1):
        n[k] += i
        return n[k]

    with nc.Block() as blk:

        p1 = ExitStack()
        sb = lambda nm, s, dt: p1.enter_context(nc.sbuf_tensor(nm, s, dt))
        xin = [sb(f"xin{i}", [128, D], F32) for i in range(2)]
        x2t = [sb(f"x2t{i}", [128, D], F32) for i in range(2)]
        t2t = [sb(f"t2t{i}", [128, D], F32) for i in range(2)]
        xnbf = [sb(f"xnbf{i}", [128, D], BF16) for i in range(2)]
        xnT = [sb(f"xnT{i}", [128, D], BF16) for i in range(2)]
        wln_b = sb("wln_b", [128, D], F32)
        bln_b = sb("bln_b", [128, D], F32)


        @blk.sync
        def _(e):
            e.dma_start(out=ident[:], in_=ident_d).then_inc(s_lc, 16)
            e.dma_start(out=ones3[:], in_=ones3_d).then_inc(s_lc, 16)
            e.dma_start(out=wln_b[:], in_=ilw_d[None, :].broadcast_to((128, D))).then_inc(s_lc, 16)
            e.dma_start(out=bln_b[:], in_=ilb_d[None, :].broadcast_to((128, D))).then_inc(s_lc, 16)
            e.dma_start(out=hvec[:, 0:NH], in_=smf_d).then_inc(s_lc, 16)
            e.dma_start(out=hvec[:, NH:2 * NH], in_=lgs_d).then_inc(s_lc, 16)
            e.dma_start(out=bvb[:], in_=b_in_d[None, 4096:6144].broadcast_to((128, 2048))).then_inc(s_lc, 16)
        bump("lc", 112)

        @blk.gpsimd
        def _(e):
            for oc in range(64):
                e.dma_start(out=bcol[:, oc:oc + 1],
                            in_=b_in_d[oc * 128:(oc + 1) * 128][:, None]).then_inc(s_gl, 16)
            e.dma_start(out=bcol[0:16, 64:65],
                        in_=b_in_d[8192:8208][:, None]).then_inc(s_gl, 16)
        bump("gl", 65 * 16)
        ld0 = n["lc"]

        @blk.scalar
        def _(e, ld0=ld0):
            e.wait_ge(s_lc, ld0)
            e.activation(out=hvec[:, 2 * NH:3 * NH], in_=hvec[:, 0:NH],
                         func=AF.Sigmoid).then_inc(s_sc, 1)
        sc0 = bump("sc")

        @blk.vector
        def _(e, sc0=sc0):
            e.wait_ge(s_sc, sc0)
            e.tensor_scalar(out=hvec[:, 0:NH], in0=hvec[:, 2 * NH:3 * NH],
                            scalar1=-1.0, scalar2=1.0, op0=ALU.mult,
                            op1=ALU.add).then_inc(s_dv, 1)
            e.memset(carry[:], 0.0).then_inc(s_dv, 1)
            e.memset(seedc[:], -3.0e38).then_inc(s_dv, 1)
            e.memset(ones128f[:], 1.0).then_inc(s_dv, 1)
            e.memset(onecol[:], 1.0).then_inc(s_dv, 1)
        dv0 = bump("dv", 5)

        @blk.scalar
        def _(e, dv0=dv0):
            e.wait_ge(s_dv, dv0)
            e.activation(out=hvec[:, NH:2 * NH], in_=hvec[:, NH:2 * NH],
                         func=AF.Exp, scale=-2.0).then_inc(s_sc, 1)
        sc1 = bump("sc")

        @blk.vector
        def _(e, sc1=sc1):
            e.wait_ge(s_sc, sc1)
            e.tensor_scalar_mul(hvec[:, NH:2 * NH], hvec[:, NH:2 * NH],
                                float(1.0 / np.sqrt(128.0))).then_inc(s_dv, 1)
        dv1 = bump("dv")

        @blk.gpsimd
        def _(e, dv1=dv1):
            e.wait_ge(s_dv, dv1)
            e.dma_start(out=hs_s.rearrange("r h -> (r h)")[None, :],
                        in_=hvec[:, 0:3 * NH]).then_inc(s_gl, 16)
            e.wait_ge(s_gl, bump("gl", 16))
            for h in range(NH):
                e.dma_start(out=oms_b[:, h:h + 1],
                            in_=hs_s[0:1, h:h + 1].broadcast_to((128, 1))).then_inc(s_gl, 16)
                e.dma_start(out=alp_b[:, h:h + 1],
                            in_=hs_s[1:2, h:h + 1].broadcast_to((128, 1))).then_inc(s_gl, 16)
                e.dma_start(out=sig_b[:, h:h + 1],
                            in_=hs_s[2:3, h:h + 1].broadcast_to((128, 1))).then_inc(s_gl, 16)
        bump("gl", 48 * 16)

        ev_dv = [0, 0]
        ev_sc = [0, 0]
        ev_go = [0, 0]
        a1m = {}

        def a1_src(t):
            own = t >= 64
            return (xo_d if own else xf_d, ((t - 64) if own else t) * 128,
                    xno_s if own else xnf_s)

        def a1_load(t):
            b = t % 2
            src, row, _ = a1_src(t)

            @blk.sync
            def _(e, b=b, src=src, row=row, w=ev_dv[b]):
                if w:
                    e.wait_ge(s_dv, w)
                e.dma_start(out=xin[b][:], in_=src[row:row + 128, :]).then_inc(s_li[b], 16)
            a1m[("ld", t)] = bump(f"li{b}", 16)

        def a1_sq(t):
            b = t % 2

            @blk.scalar
            def _(e, b=b, ld_t=a1m[("ld", t)]):
                e.wait_ge(s_li[b], ld_t)
                e.activation(out=x2t[b][:], in_=xin[b][:], func=AF.Square,
                             accum_out=statsA[:, b, 1:2]).then_inc(s_sc, 1)
            a1m[("sq", t)] = bump("sc")

        def a1_st5(t):
            b = t % 2

            @blk.vector
            def _(e, b=b, sc_sq=a1m[("sq", t)], base=n["dv"]):
                e.wait_ge(s_sc, sc_sq)
                e.reduce_sum(statsA[:, b, 0:1], xin[b][:], axis=AX.X).then_inc(s_dv, 1)
                e.wait_ge(s_dv, base + 1)
                e.tensor_scalar_mul(statsA[:, b, 2:3], statsA[:, b, 0:1], 1.0 / D).then_inc(s_dv, 1)
                e.wait_ge(s_dv, base + 2)
                e.tensor_mul(statsA[:, b, 3:4], statsA[:, b, 2:3], statsA[:, b, 2:3]).then_inc(s_dv, 1)
                e.wait_ge(s_dv, base + 3)
                e.scalar_tensor_tensor(out=statsA[:, b, 3:4], in0=statsA[:, b, 1:2],
                                       scalar=1.0 / D, in1=statsA[:, b, 3:4],
                                       op0=ALU.mult, op1=ALU.subtract).then_inc(s_dv, 1)
                e.wait_ge(s_dv, base + 4)
                e.tensor_scalar(out=statsA[:, b, 4:5], in0=statsA[:, b, 3:4],
                                scalar1=float(1.0 + EPS), scalar2=float(EPS * EPS),
                                op0=ALU.mult, op1=ALU.add).then_inc(s_dv, 1)
            bump("dv", 5)
            a1m[("st5", t)] = n["dv"]

        def a1_rt(t):
            b = t % 2

            @blk.scalar
            def _(e, b=b, w=a1m[("st5", t)]):
                e.wait_ge(s_dv, w)
                e.sqrt(statsA[:, b, 5:6], statsA[:, b, 4:5]).then_inc(s_sc, 1)
            a1m[("rt", t)] = bump("sc")

        def a1_tail(t):
            b = t % 2
            _, row, dstT = a1_src(t)

            @blk.vector
            def _(e, b=b, sc_rt=a1m[("rt", t)], base=n["dv"]):
                e.wait_ge(s_sc, sc_rt)
                e.reciprocal(statsA[:, b, 6:7], statsA[:, b, 5:6]).then_inc(s_dv, 1)
                e.wait_ge(s_dv, base + 1)
                e.tensor_scalar(out=t2t[b][:], in0=xin[b][:], scalar1=statsA[:, b, 2:3],
                                scalar2=statsA[:, b, 6:7], op0=ALU.subtract,
                                op1=ALU.mult).then_inc(s_dv, 1)
                e.wait_ge(s_dv, base + 2)
                e.tensor_mul(t2t[b][:], t2t[b][:], wln_b[:]).then_inc(s_dv, 1)
                e.wait_ge(s_dv, base + 3)
                e.tensor_add(xnbf[b][:], t2t[b][:], bln_b[:]).then_inc(s_dv, 1)
            bump("dv", 4)
            ev_dv[b] = n["dv"]
            dv_nrm = n["dv"]

            @blk.tensor
            def _(e, b=b, dv_nrm=dv_nrm, w=ev_sc[b]):
                e.wait_ge(s_dv, dv_nrm)
                if w:
                    e.wait_ge(s_sc, w)
                for k in range(DC):
                    ins = e.transpose(ps_b[b][:, k * 128:(k + 1) * 128],
                                      xnbf[b][:, k * 128:(k + 1) * 128], ident[:])
                ins.then_inc(s_pe, 1)
            pe_tr = bump("pe")

            @blk.scalar
            def _(e, b=b, pe_tr=pe_tr, w=ev_go[b]):
                e.wait_ge(s_pe, pe_tr)
                if w:
                    e.wait_ge(s_sx[b], w)
                e.copy(xnT[b][:], ps_b[b][:]).then_inc(s_sc, 1)
            sc_ev = bump("sc")
            ev_sc[b] = sc_ev

            @blk.gpsimd
            def _(e, b=b, sc_ev=sc_ev, dstT=dstT, row=row):
                e.wait_ge(s_sc, sc_ev)
                e.dma_start(
                    out=dstT[:, :, row:row + 128].rearrange("c p q -> p c q"),
                    in_=xnT[b][:].rearrange("p (c q) -> p c q", c=DC),
                ).then_inc(s_sx[b], 16)
            ev_go[b] = bump(f"sx{b}", 16)

        a1_load(0)
        a1_sq(0)
        a1_st5(0)
        for t in range(96):
            if t + 1 < 96:
                a1_load(t + 1)
                a1_sq(t + 1)
            a1_rt(t)
            if t + 1 < 96:
                a1_st5(t + 1)
            a1_tail(t)

        a1_sx = (n["sx0"], n["sx1"])
        p1_keep = (wln_b, bln_b)
        del p1_keep
        p1.close()

        p2 = ExitStack()
        sb2 = lambda nm, s, dt: p2.enter_context(nc.sbuf_tensor(nm, s, dt))
        wt1 = sb2("wt1", [128, DC, 2048], BF16)
        wt2 = sb2("wt2", [128, DC, 2048], BF16)
        wyT = sb2("wyT", [128, DC, NH], BF16)
        xnt_in = [sb2(f"xnt_in{i}", [128, DC, 512], BF16) for i in range(2)]
        ktil = sb2("ktil", [128, 512], F32)
        khat = sb2("khat", [128, 512], F32)
        kout = [sb2(f"kout{i}", [128, 512], BF16) for i in range(2)]
        vout = [sb2(f"vout{i}", [128, 512], BF16) for i in range(2)]
        yrow = sb2("yrow", [16, TF], F32)
        posr = sb2("posr", [16, TF], F32)
        ppf = sb2("ppf", [128, 1024], F32)
        ppt = sb2("ppt", [128, 1024], F32)
        pph = sb2("pph", [128, 1024], BF16)
        ppm = sb2("ppm", [128, 1024], BF16)
        ppl = sb2("ppl", [128, 1024], BF16)
        qevac = [sb2(f"qevac{i}", [128, 512], BF16) for i in range(2)]
        @blk.tensor
        def _(e, w=n["gl"], w2=n["lc"], w3=a1_sx, w4=(n["li0"], n["li1"])):
            e.wait_ge(s_gl, w)
            e.wait_ge(s_lc, w2)
            e.wait_ge(s_sx[0], w3[0])
            e.wait_ge(s_sx[1], w3[1])
            e.wait_ge(s_li[0], w4[0])
            e.wait_ge(s_li[1], w4[1])
            e.sem_inc(s_pe, 1)
        a2bar = bump("pe")

        @blk.gpsimd
        def _(e, wbar=a2bar):
            e.wait_ge(s_pe, wbar)
            e.dma_start(out=wt1[:],
                        in_=w_in_d[:, 2048:4096].rearrange("(c p) j -> p c j", p=128)).then_inc(s_lw, 16)
            e.dma_start(out=wt2[:],
                        in_=w_in_d[:, 4096:6144].rearrange("(c p) j -> p c j", p=128)).then_inc(s_lw, 16)
            e.dma_start(out=wyT[:],
                        in_=w_in_d[:, 8192:8208].rearrange("(c p) j -> p c j", p=128)).then_inc(s_lw, 16)
        bump("lw", 48)
        a2a_lw = n["lw"]

        @blk.scalar
        def _(e, w=a2a_lw):
            e.wait_ge(s_lw, w)
            e.sem_inc(s_sc, 1)
        a2a_sc = bump("sc")

        xnt_free = [0, 0]
        kevac_dv = [0, 0]
        kout_go = [0, 0]
        vout_go = [0, 0]
        vevac_dv = [0, 0]
        y_dv = 0
        for tt in range(16):
            b = tt % 2
            tok0 = tt * 512

            @blk.sync
            def _(e, b=b, tok0=tok0, w=xnt_free[b], wg=a1_sx):
                e.wait_ge(s_sx[0], wg[0])
                e.wait_ge(s_sx[1], wg[1])
                if w:
                    e.wait_ge(s_pe, w)
                e.dma_start(out=xnt_in[b][:],
                            in_=xnf_s[:, :, tok0:tok0 + 512].rearrange("c p q -> p c q")).then_inc(s_li[b], 16)
            ld_kv = bump(f"li{b}", 16)

            for h in range(NH):
                @blk.tensor
                def _(e, b=b, h=h, ld_kv=ld_kv, wdv=kevac_dv[h % 2], wsc=a2a_sc):
                    e.wait_ge(s_li[b], ld_kv)
                    e.wait_ge(s_sc, wsc)
                    if wdv:
                        e.wait_ge(s_dv, wdv)
                    for c in range(DC):
                        ins = e.matmul(ps_a[h % 2][:, 0:512],
                                       wt1[:, c, h * 128:(h + 1) * 128],
                                       xnt_in[b][:, c, :],
                                       start=(c == 0), stop=(c == DC - 1))
                    ins.then_inc(s_pe, 1)
                pe_k = bump("pe")

                @blk.vector
                def _(e, h=h, pe_k=pe_k, base=n["dv"], wgo=kout_go[h % 2]):
                    e.wait_ge(s_pe, pe_k)
                    e.wait_ge(s_dv, base)
                    if wgo:
                        e.wait_ge(s_sk[h % 2], wgo)
                    e.tensor_scalar(out=ktil[:], in0=ps_a[h % 2][:, 0:512],
                                    scalar1=bcol[:, 16 + h:17 + h],
                                    scalar2=oms_b[:, h:h + 1],
                                    op0=ALU.add, op1=ALU.mult).then_inc(s_dv, 1)
                    e.tensor_scalar(out=khat[:], in0=ps_a[h % 2][:, 0:512],
                                    scalar1=bcol[:, 16 + h:17 + h],
                                    scalar2=sig_b[:, h:h + 1],
                                    op0=ALU.add, op1=ALU.mult).then_inc(s_dv, 1)
                    e.wait_ge(s_dv, base + 2)
                    e.tensor_add(kout[h % 2][:, 0:1], ktil[:, 0:1], carry[:, h:h + 1]).then_inc(s_dv, 1)
                    e.tensor_add(kout[h % 2][:, 1:512], ktil[:, 1:512], khat[:, 0:511]).then_inc(s_dv, 1)
                    e.wait_ge(s_dv, base + 4)
                    e.tensor_copy(carry[:, h:h + 1], khat[:, 511:512]).then_inc(s_dv, 1)
                bump("dv", 5)
                kevac_dv[h % 2] = n["dv"] - 3
                dv_k = n["dv"]

                @blk.gpsimd
                def _(e, h=h, tok0=tok0, dv_k=dv_k):
                    e.wait_ge(s_dv, dv_k)
                    e.dma_start(out=kp_s[h][:, tok0:tok0 + 512],
                                in_=kout[h % 2][:]).then_inc(s_sk[h % 2], 16)
                kout_go[h % 2] = bump(f"sk{h % 2}", 16)

            vi = 0
            for g in range(4):
                for ts4 in range(4):
                    @blk.tensor
                    def _(e, b=b, g=g, ts4=ts4, vi=vi, wdv=vevac_dv[vi % 2]):
                        if wdv:
                            e.wait_ge(s_dv, wdv)
                        for c in range(DC):
                            ins = e.matmul(ps_a[vi % 2][:, 512:1024],
                                           xnt_in[b][:, c, ts4 * 128:(ts4 + 1) * 128],
                                           wt2[:, c, g * 512:(g + 1) * 512],
                                           start=(c == 0), stop=(c == DC - 1))
                        ins.then_inc(s_pe, 1)
                    pe_v = bump("pe")

                    @blk.vector
                    def _(e, g=g, vi=vi, pe_v=pe_v, wgo=vout_go[vi % 2]):
                        e.wait_ge(s_pe, pe_v)
                        if wgo:
                            e.wait_ge(s_sv[vi % 2], wgo)
                        e.tensor_add(vout[vi % 2][:], ps_a[vi % 2][:, 512:1024],
                                     bvb[:, g * 512:(g + 1) * 512]).then_inc(s_dv, 1)
                    vevac_dv[vi % 2] = bump("dv")
                    dv_v = n["dv"]

                    @blk.gpsimd
                    def _(e, g=g, ts4=ts4, tok0=tok0, vi=vi, dv_v=dv_v):
                        e.wait_ge(s_dv, dv_v)
                        kc = tok0 // 128 + ts4
                        e.dma_start(
                            out=v_s[4 * g:4 * (g + 1), kc, :, :].rearrange("h p d -> p h d"),
                            in_=vout[vi % 2][:].rearrange("p (h d) -> p h d", h=4),
                        ).then_inc(s_sv[vi % 2], 16)
                    vout_go[vi % 2] = bump(f"sv{vi % 2}", 16)
                    vi += 1

            @blk.tensor
            def _(e, b=b, ld_kv=ld_kv, wdv=y_dv):
                if wdv:
                    e.wait_ge(s_dv, wdv)
                for c in range(DC):
                    ins = e.matmul(ps_d[0:16, :], wyT[:, c, :], xnt_in[b][:, c, :],
                                   start=(c == 0), stop=(c == DC - 1))
                ins.then_inc(s_pe, 1)
            pe_yt = bump("pe")
            xnt_free[b] = pe_yt

            @blk.vector
            def _(e, tok0=tok0, pe_yt=pe_yt):
                e.wait_ge(s_pe, pe_yt)
                e.tensor_scalar(out=yrow[:, tok0:tok0 + 512], in0=ps_d[0:16, :],
                                scalar1=bcol[0:16, 64:65], scalar2=None,
                                op0=ALU.add).then_inc(s_dv, 1)
            y_dv = bump("dv")

        a2b_dv = n["dv"]

        @blk.scalar
        def _(e, w=a2b_dv):
            e.wait_ge(s_dv, w)
            e.activation(out=yrow[:], in_=yrow[:], func=AF.Sigmoid).then_inc(s_sc, 1)
        sc_sig = bump("sc")

        @blk.vector
        def _(e, sc_sig=sc_sig):
            e.wait_ge(s_sc, sc_sig)
            e.tensor_tensor_scan(out=posr[:], data0=yrow[:], data1=yrow[:],
                                 initial=0.0, op0=ALU.add, op1=ALU.max).then_inc(s_dv, 1)
        dv_scan = bump("dv")

        @blk.gpsimd
        def _(e, dv_scan=dv_scan):
            e.wait_ge(s_dv, dv_scan)
            e.dma_start(out=pos_s, in_=posr[:]).then_inc(s_go, 16)
            e.wait_ge(s_go, bump("go", 16))
            e.dma_start(out=ppf[:],
                        in_=pos_s.rearrange("h (g q) -> (h g) q", g=8)).then_inc(s_gl, 16)
        gl_pp = bump("gl", 16)

        @blk.vector
        def _(e, gl_pp=gl_pp, base=n["dv"]):
            e.wait_ge(s_gl, gl_pp)
            e.tensor_copy(pph[:], ppf[:]).then_inc(s_dv, 1)
            e.wait_ge(s_dv, base + 1)
            e.tensor_sub(ppt[:], ppf[:], pph[:]).then_inc(s_dv, 1)
            e.wait_ge(s_dv, base + 2)
            e.tensor_copy(ppm[:], ppt[:]).then_inc(s_dv, 1)
            e.wait_ge(s_dv, base + 3)
            e.tensor_sub(ppt[:], ppt[:], ppm[:]).then_inc(s_dv, 1)
            e.wait_ge(s_dv, base + 4)
            e.tensor_copy(ppl[:], ppt[:]).then_inc(s_dv, 1)
        bump("dv", 5)
        dv_p3 = n["dv"]

        @blk.gpsimd
        def _(e, dv_p3=dv_p3):
            e.wait_ge(s_dv, dv_p3)
            for h in range(NH):
                for r, t_ in ((0, pph), (1, ppm), (2, ppl)):
                    e.dma_start(out=pos3_s[h, r].rearrange("(g q) -> g q", g=8),
                                in_=t_[h * 8:(h + 1) * 8, :]).then_inc(s_go, 16)
        bump("go", 48 * 16)

        @blk.gpsimd
        def _(e, wpe=n["pe"]):
            e.wait_ge(s_pe, wpe)
            e.dma_start(out=wt1[:],
                        in_=w_in_d[:, 0:2048].rearrange("(c p) j -> p c j", p=128)).then_inc(s_lw, 16)
            e.dma_start(out=wt2[:],
                        in_=w_in_d[:, 6144:8192].rearrange("(c p) j -> p c j", p=128)).then_inc(s_lw, 16)
        bump("lw", 32)
        a2d_lw = n["lw"]

        @blk.scalar
        def _(e, w=a2d_lw):
            e.wait_ge(s_lw, w)
            e.sem_inc(s_sc, 1)
        a2d_sc = bump("sc")

        qp_free = [0, 0]
        qev_go = [0, 0]
        qevac_dv = [0, 0]
        for tt in range(8):
            b = tt % 2
            tok0 = tt * 512

            @blk.sync
            def _(e, b=b, tok0=tok0, w=qp_free[b], wsc=a2d_sc):
                e.wait_ge(s_sc, wsc)
                if w:
                    e.wait_ge(s_pe, w)
                e.dma_start(out=xnt_in[b][:],
                            in_=xno_s[:, :, tok0:tok0 + 512].rearrange("c p q -> p c q")).then_inc(s_li[b], 16)
            ld_qp = bump(f"li{b}", 16)

            for which, wsrc, dst_s in ((0, wt1, q_s), (1, wt2, p_s)):
                for h in range(NH):
                    @blk.tensor
                    def _(e, b=b, h=h, wsrc=wsrc, ld_qp=ld_qp, wdv=qevac_dv[h % 2]):
                        e.wait_ge(s_li[b], ld_qp)
                        if wdv:
                            e.wait_ge(s_dv, wdv)
                        for c in range(DC):
                            ins = e.matmul(ps_a[h % 2][:, 0:512],
                                           wsrc[:, c, h * 128:(h + 1) * 128],
                                           xnt_in[b][:, c, :],
                                           start=(c == 0), stop=(c == DC - 1))
                        ins.then_inc(s_pe, 1)
                    pe_q = bump("pe")

                    @blk.vector
                    def _(e, h=h, which=which, pe_q=pe_q, wgo=qev_go[h % 2]):
                        e.wait_ge(s_pe, pe_q)
                        if wgo:
                            e.wait_ge(s_sk[h % 2], wgo)
                        if which == 0:
                            e.tensor_scalar(out=qevac[h % 2][:], in0=ps_a[h % 2][:, 0:512],
                                            scalar1=bcol[:, h:h + 1],
                                            scalar2=alp_b[:, h:h + 1],
                                            op0=ALU.add, op1=ALU.mult).then_inc(s_dv, 1)
                        else:
                            e.tensor_scalar(out=qevac[h % 2][:], in0=ps_a[h % 2][:, 0:512],
                                            scalar1=bcol[:, 48 + h:49 + h],
                                            scalar2=None, op0=ALU.add).then_inc(s_dv, 1)
                    qevac_dv[h % 2] = bump("dv")
                    dv_q = n["dv"]

                    @blk.gpsimd
                    def _(e, h=h, tok0=tok0, dst_s=dst_s, dv_q=dv_q):
                        e.wait_ge(s_dv, dv_q)
                        e.dma_start(out=dst_s[h][:, tok0:tok0 + 512],
                                    in_=qevac[h % 2][:]).then_inc(s_sk[h % 2], 16)
                    qev_go[h % 2] = bump(f"sk{h % 2}", 16)
                qp_free[b] = n["pe"]

        a2_tot = (n["go"], n["sk0"], n["sk1"], n["sv0"], n["sv1"])

        p2.close()

        p3s = ExitStack()
        sb3 = lambda nm, s, dt: p3s.enter_context(nc.sbuf_tensor(nm, s, dt))
        ktA = sb3("ktA", [128, TF], BF16)
        vA = sb3("vA", [128, 64, DH], BF16)
        qt2 = [sb3(f"qt2{i}", [128, 512], BF16) for i in range(2)]
        p3A = sb3("p3A", [3, TF], BF16)
        cache = sb3("cache", [128, 2 * TF], F32)
        mloc = sb3("mloc", [128, 2, 8], F32)
        mrun = sb3("mrun", [128, 2], F32)
        negm = sb3("negm", [128, 2], F32)
        probs = [sb3(f"probs{i}", [128, TF], BF16) for i in range(4)]
        ptsb = [sb3(f"ptsb{i}", [128, 1024], BF16) for i in range(2)]
        lcols = sb3("lcols", [128, 2, 4], F32)
        o_sb = [sb3(f"o_sb{i}", [128, 512], BF16) for i in range(2)]
        maskt = sb3("maskt", [128, 4, 1024], BF16)

        a2_ld = n["ld"]
        mask_dv = [0]
        exp_sc = [0, 0, 0, 0]
        tpose_pe = [0, 0, 0, 0]
        dance_go = [0, 0]
        pv_mark = [0, 0]
        ts2_dv = {}
        osb_dv = [0, 0]
        ptsb_sc = [None, None]
        oev_dv = 0
        linv_dv = 0

        def U_A(h, s, c, kp, ld_h, ld_m):
            B = SLOT_BUD[s]
            par = c % 2
            coff = par * TF
            last = kp == B // 2 - 1

            @blk.tensor
            def _(e, wdv=ts2_dv.get(kp % 2, 0)):
                e.wait_ge(s_lh, ld_h)
                e.wait_ge(s_lm[s % 2], ld_m)
                if wdv:
                    e.wait_ge(s_dv, wdv)
                for half in range(2):
                    kt = 2 * kp + half
                    sl = slice(half * 512, (half + 1) * 512)
                    e.matmul(ps_a[kp % 2][:, sl],
                             qt2[s % 2][:, c * 128:(c + 1) * 128],
                             ktA[:, kt * 512:(kt + 1) * 512],
                             start=True, stop=False)
                for half in range(2):
                    kt = 2 * kp + half
                    sl = slice(half * 512, (half + 1) * 512)
                    ins = e.matmul(ps_a[kp % 2][:, sl], ones3[:],
                                   p3A[:, kt * 512:(kt + 1) * 512],
                                   start=False, stop=True)
                ins.then_inc(s_pe, 1)
            pe_qk = bump("pe")

            @blk.vector
            def _(e, pe_qk=pe_qk, base=n["dv"], wexp=exp_sc[(c + 2) % 4],
                  wgo=dance_go[s % 2]):
                e.wait_ge(s_pe, pe_qk)
                if kp == 0:
                    if wexp:
                        e.wait_ge(s_sc, wexp)
                    if wgo and c == 0:
                        e.wait_ge(s_gd[s % 2], wgo)
                if last:
                    e.wait_ge(s_lm[s % 2], ld_m)
                    e.tensor_add(ps_a[kp % 2][:], ps_a[kp % 2][:],
                                 maskt[:, c, :]).then_inc(s_dv, 1)
                    e.wait_ge(s_dv, base + 1)
                e.tensor_scalar(out=cache[:, coff + kp * 1024: coff + (kp + 1) * 1024],
                                in0=ps_a[kp % 2][:], scalar1=0.0, scalar2=seedc[:],
                                op0=ALU.add, op1=ALU.max,
                                accum_out=mloc[:, par, kp:kp + 1]).then_inc(s_dv, 1)
            bump("dv", 2 if last else 1)
            ts2_dv[kp % 2] = n["dv"]
            if last:
                mask_dv[0] = n["dv"]

        def U_MX(h, s, c):
            B = SLOT_BUD[s]
            par = c % 2

            @blk.vector
            def _(e, base=n["dv"], wexp=exp_sc[(c + 2) % 4]):
                e.wait_ge(s_dv, base)
                if wexp:
                    e.wait_ge(s_sc, wexp)
                e.reduce_max(mrun[:, par:par + 1], mloc[:, par, 0:B // 2],
                             axis=AX.X).then_inc(s_dv, 1)
                e.wait_ge(s_dv, base + 1)
                e.tensor_scalar_mul(negm[:, par:par + 1], mrun[:, par:par + 1],
                                    -1.0).then_inc(s_dv, 1)
            bump("dv", 2)

        def U_EXP(h, s, c):
            B = SLOT_BUD[s]
            par = c % 2
            coff = par * TF
            dv_exp = n["dv"]

            @blk.scalar
            def _(e, dv_exp=dv_exp, wpe=tpose_pe[c]):
                e.wait_ge(s_dv, dv_exp)
                if wpe:
                    e.wait_ge(s_pe, wpe)
                e.activation(out=probs[c][:, 0:B * 512],
                             in_=cache[:, coff: coff + B * 512],
                             func=AF.Exp, bias=negm[:, par:par + 1],
                             accum_out=lcols[:, s % 2, c:c + 1]).then_inc(s_sc, 1)
            exp_sc[c] = bump("sc")

        def mk_T(h, s, pair, kg):
            B = SLOT_BUD[s]
            wexp0 = exp_sc[2 * pair]
            wexp1 = exp_sc[2 * pair + 1]

            @blk.tensor
            def _(e, wexp=max(wexp0, wexp1), wsc=ptsb_sc[kg % 2], wdv=oev_box[0],
                  wpv=pv_mark[kg % 2]):
                e.wait_ge(s_sc, wexp)
                if wsc:
                    e.wait_ge(s_dv if wsc[0] == "dv" else s_sc, wsc[1])
                if wpv:
                    e.wait_ge(s_pe, wpv)
                if kg == 0 and pair == 0 and wdv:
                    e.wait_ge(s_dv, wdv)
                for kcl in range(4):
                    kc = kg * 4 + kcl
                    for par2 in range(2):
                        ins = e.transpose(
                            ps_b[kg % 2][:, kcl * 256 + par2 * 128: kcl * 256 + (par2 + 1) * 128],
                            probs[2 * pair + par2][:, kc * 128:(kc + 1) * 128], ident[:])
                ins.then_inc(s_pe, 1)
            pe_t = bump("pe")
            if kg == B - 1:
                tpose_pe[2 * pair] = tpose_pe[2 * pair + 1] = pe_t

            @blk.vector
            def _(e, pe_t=pe_t):
                e.wait_ge(s_pe, pe_t)
                e.tensor_copy(ptsb[kg % 2][:], ps_b[kg % 2][:]).then_inc(s_dv, 1)
            ptsb_sc[kg % 2] = ("dv", bump("dv"))

        def mk_PV(h, s, pair, kg):
            B = SLOT_BUD[s]

            @blk.tensor
            def _(e, wsc=ptsb_sc[kg % 2]):
                e.wait_ge(s_dv if wsc[0] == "dv" else s_sc, wsc[1])
                for kcl in range(4):
                    kc = kg * 4 + kcl
                    ins = e.matmul(ps_c[:, pair * 256:(pair + 1) * 256],
                                   vA[:, kc, :],
                                   ptsb[kg % 2][:, kcl * 256:(kcl + 1) * 256],
                                   start=(kg == 0 and kcl == 0),
                                   stop=(kg == B - 1 and kcl == 3))
                ins.then_inc(s_pe, 1)
            pv_mark[kg % 2] = bump("pe")

        def U_DANCE(h, s, ladd):
            pv_done = n["pe"]

            @blk.gpsimd
            def _(e, ladd=ladd):
                e.wait_ge(s_sc, ladd)
                for c4 in range(4):
                    e.dma_start(out=l_s[s, h, c4][:, None],
                                in_=lcols[:, s % 2, c4:c4 + 1]).then_inc(s_gd[s % 2], 16)
            bump(f"gd{s % 2}", 64)

            @blk.vector
            def _(e, pv_done=pv_done, wgo=osb_dv[s % 2]):
                e.wait_ge(s_pe, pv_done)
                if wgo:
                    e.wait_ge(s_sv[s % 2], wgo)
                e.tensor_copy(o_sb[s % 2][:], ps_c[:]).then_inc(s_dv, 1)
            oev = bump("dv")

            @blk.gpsimd
            def _(e, wdv=oev):
                e.wait_ge(s_dv, wdv)
                e.dma_start(out=o_s[h][:, s * 512:(s + 1) * 512],
                            in_=o_sb[s % 2][:]).then_inc(s_sv[s % 2], 16)
            osb_dv[s % 2] = bump(f"sv{s % 2}", 16)
            dance_go[s % 2] = n[f"gd{s % 2}"]
            return oev

        def interleave(primary, filler):
            fi = 0
            for i, u in enumerate(primary):
                u()
                take = 2 if len(filler) - fi > 2 * (len(primary) - i - 1) else 1
                for _ in range(take):
                    if fi < len(filler):
                        filler[fi](); fi += 1
            while fi < len(filler):
                filler[fi](); fi += 1

        oev_box = [oev_dv]
        linv_dv = [linv_dv]
        for h in range(NH):
            @blk.sync
            def _(e, h=h, wt=a2_tot, wpe=n["pe"]):
                if h == 0:
                    e.wait_ge(s_go, wt[0])
                    e.wait_ge(s_sk[0], wt[1])
                    e.wait_ge(s_sk[1], wt[2])
                    e.wait_ge(s_sv[0], wt[3])
                    e.wait_ge(s_sv[1], wt[4])
                else:
                    e.wait_ge(s_pe, wpe)
                e.dma_start(out=ktA[:], in_=kp_s[h]).then_inc(s_lh, 16)
                e.dma_start(out=vA[:], in_=v_s[h].rearrange("a p d -> p a d")).then_inc(s_lh, 16)
                e.dma_start(out=p3A[:], in_=pos3_s[h]).then_inc(s_lh, 16)
            ld_h = bump("lh", 48)

            pending = []
            for s in range(8):
                B = SLOT_BUD[s]

                @blk.sync
                def _(e, s=s, h=h, wdv=mask_dv[0], wt=a2_tot):
                    if h == 0:
                        e.wait_ge(s_go, wt[0])
                        e.wait_ge(s_sk[0], wt[1])
                        e.wait_ge(s_sk[1], wt[2])
                        e.wait_ge(s_sv[0], wt[3])
                        e.wait_ge(s_sv[1], wt[4])
                    if wdv:
                        e.wait_ge(s_dv, wdv)
                    e.dma_start(out=maskt[:],
                                in_=masks_d[s].rearrange("c p q -> p c q")).then_inc(s_lm[s % 2], 16)
                    e.dma_start(out=qt2[s % 2][:],
                                in_=q_s[h][:, s * 512:(s + 1) * 512]).then_inc(s_lm[s % 2], 16)
                ld_m = bump(f"lm{s % 2}", 32)

                A = {c: [(lambda h=h, s=s, c=c, kp=kp, ld_h=ld_h, ld_m=ld_m:
                          U_A(h, s, c, kp, ld_h, ld_m)) for kp in range(B // 2)]
                     for c in range(4)}
                interleave(A[0]
                           + [lambda h=h, s=s: (U_MX(h, s, 0), U_EXP(h, s, 0))]
                           + A[1]
                           + [lambda h=h, s=s: (U_MX(h, s, 1), U_EXP(h, s, 1))],
                           pending)
                pair0 = []
                for kg in range(B):
                    pair0.append(lambda h=h, s=s, kg=kg: mk_T(h, s, 0, kg))
                    if kg >= 1:
                        pair0.append(lambda h=h, s=s, kg=kg - 1: mk_PV(h, s, 0, kg))
                pair0.append(lambda h=h, s=s, kg=B - 1: mk_PV(h, s, 0, kg))
                interleave(A[2]
                           + [lambda h=h, s=s: (U_MX(h, s, 2), U_EXP(h, s, 2))]
                           + A[3]
                           + [lambda h=h, s=s: (U_MX(h, s, 3), U_EXP(h, s, 3))],
                           pair0)
                pending = []
                for kg in range(B):
                    pending.append(lambda h=h, s=s, kg=kg: mk_T(h, s, 1, kg))
                    if kg >= 1:
                        pending.append(lambda h=h, s=s, kg=kg - 1: mk_PV(h, s, 1, kg))
                pending.append(lambda h=h, s=s, kg=B - 1: mk_PV(h, s, 1, kg))
                pending.append(lambda h=h, s=s, la=exp_sc[3]:
                               oev_box.__setitem__(0, U_DANCE(h, s, la)))
            for u in pending:
                u()
            pending = []

        a3_sv = (n["sv0"], n["sv1"])
        a3_pe = n["pe"]

        @blk.tensor
        def _(e, w1=n["dv"], w2=n["sc"], w3=a3_sv, w4=n["gl"], w5=n["lh"],
              w6=(n["lm0"], n["lm1"]), w7=(n["gd0"], n["gd1"])):
            e.wait_ge(s_dv, w1)
            e.wait_ge(s_sc, w2)
            e.wait_ge(s_sv[0], w3[0])
            e.wait_ge(s_sv[1], w3[1])
            e.wait_ge(s_gl, w4)
            e.wait_ge(s_lh, w5)
            e.wait_ge(s_lm[0], w6[0])
            e.wait_ge(s_lm[1], w6[1])
            e.wait_ge(s_gd[0], w7[0])
            e.wait_ge(s_gd[1], w7[1])
            e.sem_inc(s_pe, 1)
        a4bar = bump("pe")
        p3s.close()

        p4 = ExitStack()
        sb4 = lambda nm, s, dt: p4.enter_context(nc.sbuf_tensor(nm, s, dt))
        o_ep = sb4("o_ep", [128, NH, 512], BF16)
        p_ep = sb4("p_ep", [128, NH, 512], BF16)
        g_ep = sb4("g_ep", [128, NH, 512], BF16)
        woT = sb4("woT", [128, NH, 1024], BF16)
        out32b = [sb4(f"out32b{i}", [128, D], F32) for i in range(2)]
        xo_ep2 = [sb4(f"xo_ep2{i}", [128, D], F32) for i in range(2)]
        xmid_sb2 = [sb4(f"xmid_sb2{i}", [128, D], F32) for i in range(2)]
        xm4 = xo_ep2
        xng2 = [sb4(f"xng2_{i}", [128, D], BF16) for i in range(2)]
        xngT2 = [sb4(f"xngT2_{i}", [128, D], BF16) for i in range(2)]
        x2t_a4 = sb4("x2t_a4", [128, D], F32)
        stats4 = sb4("stats4", [128, 2, 8], F32)
        wol_b = sb4("wol_b", [128, D], F32)
        bol_b = sb4("bol_b", [128, D], F32)
        gwT = sb4("gwT_sb", [128, DC, 8], BF16)
        lbb_b = sb4("lbb_b", [128, 8], F32)
        iota_sb = sb4("iota_sb", [128, 8], F32)
        gsc2 = [sb4(f"gsc2_{i}", [128, 8], F32) for i in range(2)]
        max8 = sb4("max8", [128, 8], F32)
        idx2 = sb4("idx2", [128, 8], mybir.dt.uint32)
        idxf = sb4("idxf", [128, 2], F32)
        eqA = sb4("eqA", [128, 8], F32)
        eqB = sb4("eqB", [128, 8], F32)
        gt = sb4("gt", [128, 8], F32)
        Wc2 = [sb4(f"Wc2_{i}", [128, 8], F32) for i in range(2)]
        Mc2 = [sb4(f"Mc2_{i}", [128, 8], F32) for i in range(2)]
        SCacc = sb4("SCacc", [8, 2], F32)

        @blk.sync
        def _(e, wbar=a4bar):
            e.wait_ge(s_pe, wbar)
            e.dma_start(out=wol_b[:], in_=olw_d[None, :].broadcast_to((128, D))).then_inc(s_ld, 16)
            e.dma_start(out=bol_b[:], in_=olb_d[None, :].broadcast_to((128, D))).then_inc(s_ld, 16)
            e.dma_start(out=lbb_b[:], in_=lbb_d.broadcast_to((128, 8))).then_inc(s_ld, 16)
            e.dma_start(out=iota_sb[:], in_=iota_d.broadcast_to((128, 8))).then_inc(s_ld, 16)
        ld_a4 = bump("ld", 64)

        @blk.vector
        def _(e, wbar=a4bar):
            e.wait_ge(s_pe, wbar)
            e.memset(SCacc[:], 0.0).then_inc(s_dv, 1)
        scacc_dv = bump("dv")

        @blk.gpsimd
        def _(e, wbar=a4bar):
            e.wait_ge(s_pe, wbar)
            e.dma_start(out=woT[:],
                        in_=w_out_d.rearrange("(j p) q -> p j q", p=128)).then_inc(s_lw, 16)
            e.dma_start(out=gwT[:],
                        in_=gw_d.rearrange("(c p) e -> p c e", p=128)).then_inc(s_lw, 16)
        bump("lw", 32)
        a4_lw = n["lw"]

        @blk.scalar
        def _(e, w=a4_lw):
            e.wait_ge(s_lw, w)
            e.sem_inc(s_sc, 1)
        a4prep_sc = bump("sc")

        lrow_all = sb4("lrow_all", [1, NH * 512], F32)
        linv_ep = sb4("linv_ep", [128, NH, 512], F32)

        g_mark = [0]
        sil_mark = [0]
        linv_pe_mark = [0]
        linv_ev_mark = [0, 0]
        wmm_mark_par = [0]
        evac_mark = [0, 0]
        st_mark = [0, 0]
        go_mark = [0, 0]
        dvg_slot = [0, 0]

        def mk_prep(s):
            st = {}

            def u_load(s=s):
                @blk.sync
                def _(e, s=s, wld=ld_a4, wgo=a3_sv, wdv=g_mark[0]):
                    e.wait_ge(s_ld, wld)
                    e.wait_ge(s_sv[0], wgo[0])
                    e.wait_ge(s_sv[1], wgo[1])
                    if wdv:
                        e.wait_ge(s_dv, wdv)
                    e.dma_start(out=o_ep[:],
                                in_=o_s[:, :, s * 512:(s + 1) * 512]
                                .rearrange("h p q -> p h q")).then_inc(s_ld, 16)
                    e.dma_start(out=p_ep[:],
                                in_=p_s[:, :, s * 512:(s + 1) * 512]
                                .rearrange("h p q -> p h q")).then_inc(s_ld, 16)
                st["ld_ep"] = bump("ld", 32)

                @blk.gpsimd
                def _(e, s=s, wpe=linv_pe_mark[0], wbar=a4bar):
                    if s == 0:
                        e.wait_ge(s_pe, wbar)
                    if wpe:
                        e.wait_ge(s_pe, wpe)
                    e.dma_start(out=lrow_all[:],
                                in_=l_s[s].rearrange("h c p -> (h c p)")[None, :]).then_inc(s_gl, 16)
                st["gl"] = bump("gl", 16)

            def u_recip():
                @blk.vector
                def _(e, w=st["gl"]):
                    e.wait_ge(s_gl, w)
                    e.reciprocal(lrow_all[:], lrow_all[:]).then_inc(s_dv, 1)
                st["dv_lr"] = bump("dv")

            def u_linv(hg):
                @blk.tensor
                def _(e, hg=hg, wdv=st["dv_lr"], wsc=linv_ev_mark[hg % 2]):
                    e.wait_ge(s_dv, wdv)
                    if wsc:
                        e.wait_ge(s_sc, wsc)
                    e.matmul(ps_a[hg % 2][:, 0:512], ones128f[:],
                             lrow_all[:, (2 * hg) * 512:(2 * hg + 1) * 512],
                             start=True, stop=True)
                    ins = e.matmul(ps_a[hg % 2][:, 512:1024], ones128f[:],
                                   lrow_all[:, (2 * hg + 1) * 512:(2 * hg + 2) * 512],
                                   start=True, stop=True)
                    ins.then_inc(s_pe, 1)
                pe_lb = bump("pe")
                linv_pe_mark[0] = pe_lb

                @blk.scalar
                def _(e, hg=hg, pe_lb=pe_lb, wdv=g_mark[0]):
                    e.wait_ge(s_pe, pe_lb)
                    if hg == 0 and wdv:
                        e.wait_ge(s_dv, wdv)
                    e.copy(linv_ep[:, 2 * hg:2 * hg + 2, :].rearrange("p h q -> p (h q)"),
                           ps_a[hg % 2][:]).then_inc(s_sc, 1)
                linv_ev_mark[hg % 2] = bump("sc")

            def u_sil(s=s):
                @blk.scalar
                def _(e, wld=st["ld_ep"], w=a4prep_sc, wpe=wmm_mark_par[0]):
                    e.wait_ge(s_ld, wld)
                    e.wait_ge(s_sc, w)
                    if wpe:
                        e.wait_ge(s_pe, wpe)
                    e.activation(out=g_ep[:].rearrange("p h q -> p (h q)"),
                                 in_=p_ep[:].rearrange("p h q -> p (h q)"),
                                 func=AF.Silu).then_inc(s_sc, 1)
                st["sil"] = bump("sc")
                sil_mark[0] = st["sil"]

            def u_gmul(s=s):
                @blk.vector
                def _(e, s=s, wsil=st["sil"], wlv=linv_ev_mark[1], base=n["dv"]):
                    e.wait_ge(s_sc, max(wsil, wlv))
                    e.tensor_mul(g_ep[:].rearrange("p h q -> p (h q)"),
                                 g_ep[:].rearrange("p h q -> p (h q)"),
                                 o_ep[:].rearrange("p h q -> p (h q)")).then_inc(s_dv, 1)
                    e.wait_ge(s_dv, base + 1)
                    e.tensor_mul(g_ep[:].rearrange("p h q -> p (h q)"),
                                 g_ep[:].rearrange("p h q -> p (h q)"),
                                 linv_ep[:].rearrange("p h q -> p (h q)")).then_inc(s_dv, 1)
                g_mark[0] = bump("dv", 2)
                dvg_slot[s % 2] = g_mark[0]

            units = [u_load, u_recip]
            units += [(lambda hg=hg: u_linv(hg)) for hg in range(8)]
            return units, [u_sil, u_gmul]

        def mk_wchain(s):
            units = []
            for tc in range(4):
                def u_wmm(tc=tc, oh=0, s=s):
                    psx = ps_c if oh == 0 else ps_d

                    @blk.tensor
                    def _(e, s=s, tc=tc, psx=psx, wdv=dvg_slot[s % 2],
                          wsc=max(evac_mark[oh], a4prep_sc)):
                        e.wait_ge(s_dv, wdv)
                        e.wait_ge(s_sc, wsc)
                        for j in range(NH):
                            ins = e.matmul(psx[:],
                                           g_ep[:, j, tc * 128:(tc + 1) * 128],
                                           woT[:, j, oh * 512:(oh + 1) * 512],
                                           start=(j == 0), stop=(j == NH - 1))
                        ins.then_inc(s_pe, 1)
                    st_pe = bump("pe")
                    wmm_mark_par[0] = st_pe

                    @blk.scalar
                    def _(e, tc=tc, oh=oh, psx=psx, wpe=st_pe, wdv=st_mark[tc % 2]):
                        e.wait_ge(s_pe, wpe)
                        if oh == 0 and wdv:
                            e.wait_ge(s_dv, wdv)
                        e.copy(out32b[tc % 2][:, oh * 512:(oh + 1) * 512],
                               psx[:]).then_inc(s_sc, 1)
                    evac_mark[oh] = bump("sc")

                def u_wmm1(tc=tc, s=s):
                    u_wmm(tc, 1, s)

                def u_xo(tc=tc, s=s):
                    @blk.sync
                    def _(e, s=s, tc=tc, wdv=st_mark[tc % 2], wbar=a4bar):
                        if wdv:
                            e.wait_ge(s_dv, wdv)
                        else:
                            e.wait_ge(s_pe, wbar)
                        e.dma_start(out=xo_ep2[tc % 2][:],
                                    in_=xo_d[(s * 4 + tc) * 128:(s * 4 + tc + 1) * 128, :]).then_inc(s_xo[tc % 2], 16)
                    st_ld[tc % 2] = bump(f"xo{tc % 2}", 16)

                def u_sq(tc=tc):
                    @blk.scalar
                    def _(e, tc=tc, wsc=evac_mark[1], wdv=st_mark[tc % 2]):
                        e.wait_ge(s_sc, wsc)
                        if wdv:
                            e.wait_ge(s_dv, wdv)
                        e.activation(out=x2t_a4[:], in_=out32b[tc % 2][:], func=AF.Square,
                                     accum_out=stats4[:, tc % 2, 1:2]).then_inc(s_sc, 1)
                    st_sq[tc % 2] = bump("sc")

                def u_stats(tc=tc):
                    @blk.vector
                    def _(e, tc=tc, wsc=st_sq[tc % 2], base=n["dv"]):
                        p = tc % 2
                        e.wait_ge(s_sc, wsc)
                        e.reduce_sum(stats4[:, p, 0:1], out32b[p][:], axis=AX.X).then_inc(s_dv, 1)
                        e.wait_ge(s_dv, base + 1)
                        e.tensor_scalar_mul(stats4[:, p, 2:3], stats4[:, p, 0:1], 1.0 / D).then_inc(s_dv, 1)
                        e.wait_ge(s_dv, base + 2)
                        e.tensor_mul(stats4[:, p, 3:4], stats4[:, p, 2:3], stats4[:, p, 2:3]).then_inc(s_dv, 1)
                        e.wait_ge(s_dv, base + 3)
                        e.scalar_tensor_tensor(out=stats4[:, p, 3:4], in0=stats4[:, p, 1:2],
                                               scalar=1.0 / D, in1=stats4[:, p, 3:4],
                                               op0=ALU.mult, op1=ALU.subtract).then_inc(s_dv, 1)
                        e.wait_ge(s_dv, base + 4)
                        e.tensor_scalar_add(stats4[:, p, 4:5], stats4[:, p, 3:4], EPS).then_inc(s_dv, 1)
                    bump("dv", 5)
                    st_dvs[tc % 2] = n["dv"]

                def u_sqrt(tc=tc):
                    @blk.scalar
                    def _(e, tc=tc, wdv=st_dvs[tc % 2]):
                        e.wait_ge(s_dv, wdv)
                        e.sqrt(stats4[:, tc % 2, 5:6], stats4[:, tc % 2, 4:5]).then_inc(s_sc, 1)
                    st_rt[tc % 2] = bump("sc")

                def u_apply(tc=tc, s=s):
                    @blk.vector
                    def _(e, s=s, tc=tc, wsc=st_rt[tc % 2], wld=st_ld[tc % 2],
                          wgo=go_mark[tc % 2], base=n["dv"]):
                        p = tc % 2
                        e.wait_ge(s_sc, wsc)
                        e.reciprocal(stats4[:, p, 6:7], stats4[:, p, 5:6]).then_inc(s_dv, 1)
                        e.wait_ge(s_dv, base + 1)
                        e.tensor_scalar(out=out32b[p][:], in0=out32b[p][:],
                                        scalar1=stats4[:, p, 2:3],
                                        scalar2=stats4[:, p, 6:7], op0=ALU.subtract,
                                        op1=ALU.mult).then_inc(s_dv, 1)
                        e.wait_ge(s_dv, base + 2)
                        e.tensor_mul(out32b[p][:], out32b[p][:], wol_b[:]).then_inc(s_dv, 1)
                        e.wait_ge(s_dv, base + 3)
                        e.tensor_add(out32b[p][:], out32b[p][:], bol_b[:]).then_inc(s_dv, 1)
                        e.wait_ge(s_dv, base + 4)
                        e.wait_ge(s_xo[tc % 2], wld)
                        if wgo:
                            e.wait_ge(s_xm[tc % 2], wgo)
                        e.tensor_add(xmid_sb2[p][:], out32b[p][:], xo_ep2[p][:]).then_inc(s_dv, 1)
                    bump("dv", 5)
                    st_mark[tc % 2] = n["dv"]

                def u_store(tc=tc, s=s):
                    @blk.gpsimd
                    def _(e, s=s, tc=tc, wdv=st_mark[tc % 2]):
                        e.wait_ge(s_dv, wdv)
                        e.dma_start(out=xmid_d[(s * 4 + tc) * 128:(s * 4 + tc + 1) * 128, :],
                                    in_=xmid_sb2[tc % 2][:]).then_inc(s_xm[tc % 2], 16)
                    go_mark[tc % 2] = bump(f"xm{tc % 2}", 16)

                units += [lambda tc=tc: u_wmm(tc, 0), u_wmm1, u_xo, u_sq,
                          u_stats, u_sqrt, u_apply, u_store]
            return units

        st_ld = [0, 0]
        st_sq = [0, 0]
        st_dvs = [0, 0]
        st_rt = [0, 0]

        pe0, pl0 = mk_prep(0)
        for u in pe0 + pl0:
            u()
        for s in range(8):
            pe_n, pl_n = mk_prep(s + 1) if s < 7 else ([], [])
            interleave(mk_wchain(s), pe_n)
            for u in pl_n:
                u()
        a4a_go = (n["xm0"], n["xm1"])

        bT_ev = [0, 0]
        bgmm = [0, 0]
        bgsc = [0]
        bsig = [0, 0]
        btop = [0, 0]
        bsca = [0]
        for t in range(32):
            p = t % 2

            @blk.sync
            def _(e, t=t, p=p, wgo=a4a_go, wdv=st_mark[p]):
                e.wait_ge(s_xm[0], wgo[0])
                e.wait_ge(s_xm[1], wgo[1])
                if wdv:
                    e.wait_ge(s_dv, wdv)
                e.dma_start(out=xm4[p][:],
                            in_=xmid_d[t * 128:(t + 1) * 128, :]).then_inc(s_xo[p], 16)
            b_ld = bump(f"xo{p}", 16)

            @blk.scalar
            def _(e, p=p, wld=b_ld, wdv=st_mark[p]):
                e.wait_ge(s_xo[p], wld)
                if wdv:
                    e.wait_ge(s_dv, wdv)
                e.activation(out=x2t_a4[:], in_=xm4[p][:], func=AF.Square,
                             accum_out=stats4[:, p, 1:2]).then_inc(s_sc, 1)
            b_sq = bump("sc")

            @blk.vector
            def _(e, p=p, wsc=b_sq, base=n["dv"]):
                e.wait_ge(s_sc, wsc)
                e.reduce_sum(stats4[:, p, 0:1], xm4[p][:], axis=AX.X).then_inc(s_dv, 1)
                e.wait_ge(s_dv, base + 1)
                e.tensor_scalar_mul(stats4[:, p, 2:3], stats4[:, p, 0:1], 1.0 / D).then_inc(s_dv, 1)
                e.wait_ge(s_dv, base + 2)
                e.tensor_mul(stats4[:, p, 3:4], stats4[:, p, 2:3], stats4[:, p, 2:3]).then_inc(s_dv, 1)
                e.wait_ge(s_dv, base + 3)
                e.scalar_tensor_tensor(out=stats4[:, p, 3:4], in0=stats4[:, p, 1:2],
                                       scalar=1.0 / D, in1=stats4[:, p, 3:4],
                                       op0=ALU.mult, op1=ALU.subtract).then_inc(s_dv, 1)
                e.wait_ge(s_dv, base + 4)
                e.tensor_scalar_add(stats4[:, p, 4:5], stats4[:, p, 3:4], EPS).then_inc(s_dv, 1)
            bump("dv", 5)
            b_dvs = n["dv"]

            @blk.scalar
            def _(e, p=p, wdv=b_dvs):
                e.wait_ge(s_dv, wdv)
                e.sqrt(stats4[:, p, 5:6], stats4[:, p, 4:5]).then_inc(s_sc, 1)
            b_rt = bump("sc")

            @blk.vector
            def _(e, p=p, wsc=b_rt, base=n["dv"]):
                e.wait_ge(s_sc, wsc)
                e.reciprocal(stats4[:, p, 6:7], stats4[:, p, 5:6]).then_inc(s_dv, 1)
                e.wait_ge(s_dv, base + 1)
                e.tensor_scalar(out=xng2[p][:], in0=xm4[p][:], scalar1=stats4[:, p, 2:3],
                                scalar2=stats4[:, p, 6:7], op0=ALU.subtract,
                                op1=ALU.mult).then_inc(s_dv, 1)
            bump("dv", 2)
            st_mark[p] = n["dv"]
            b_xng = n["dv"]

            @blk.tensor
            def _(e, p=p, wdv=b_xng, wsc=bT_ev[p]):
                e.wait_ge(s_dv, wdv)
                if wsc:
                    e.wait_ge(s_sc, wsc)
                for k in range(DC):
                    ins = e.transpose(ps_b[p][:, k * 128:(k + 1) * 128],
                                      xng2[p][:, k * 128:(k + 1) * 128], ident[:])
                ins.then_inc(s_pe, 1)
            b_pet = bump("pe")

            @blk.scalar
            def _(e, p=p, wpe=b_pet, wp2=bgmm[p]):
                e.wait_ge(s_pe, max(wpe, wp2))
                e.copy(xngT2[p][:], ps_b[p][:]).then_inc(s_sc, 1)
            bT_ev[p] = bump("sc")

            @blk.tensor
            def _(e, p=p, wsc=bT_ev[p], wdv=bgsc[0]):
                e.wait_ge(s_sc, wsc)
                if wdv:
                    e.wait_ge(s_dv, wdv)
                for k in range(DC):
                    ins = e.matmul(ps_d[:, 0:8], xngT2[p][:, k * 128:(k + 1) * 128],
                                   gwT[:, k, :], start=(k == 0), stop=(k == DC - 1))
                ins.then_inc(s_pe, 1)
            bgmm[p] = bump("pe")

            @blk.vector
            def _(e, p=p, wpe=bgmm[p], wsc=bsig[p]):
                e.wait_ge(s_pe, wpe)
                if wsc:
                    e.wait_ge(s_sc, wsc)
                e.tensor_add(gsc2[p][:], ps_d[:, 0:8], lbb_b[:]).then_inc(s_dv, 1)
            bgsc[0] = bump("dv")

            @blk.scalar
            def _(e, p=p, wdv=bgsc[0]):
                e.wait_ge(s_dv, wdv)
                e.activation(out=gsc2[p][:], in_=gsc2[p][:], func=AF.Sigmoid).then_inc(s_sc, 1)
            bsig[p] = bump("sc")

            @blk.vector
            def _(e, p=p, wsc=bsig[p], wpe=btop[p], base=n["dv"]):
                e.wait_ge(s_sc, wsc)
                if wpe:
                    e.wait_ge(s_pe, wpe)
                e.max(max8[:], gsc2[p][:]).then_inc(s_dv, 1)
                e.wait_ge(s_dv, base + 1)
                e.max_index(idx2[:], max8[:], gsc2[p][:]).then_inc(s_dv, 1)
                e.wait_ge(s_dv, base + 2)
                e.tensor_copy(idxf[:], idx2[:, 0:2]).then_inc(s_dv, 1)
                e.tensor_add(gt[:, 2:3], max8[:, 0:1], max8[:, 1:2]).then_inc(s_dv, 1)
                e.wait_ge(s_dv, base + 4)
                e.tensor_scalar_add(gt[:, 2:3], gt[:, 2:3], 1e-8).then_inc(s_dv, 1)
                e.wait_ge(s_dv, base + 5)
                e.reciprocal(gt[:, 3:4], gt[:, 2:3]).then_inc(s_dv, 1)
                e.wait_ge(s_dv, base + 6)
                e.tensor_mul(gt[:, 4:5], max8[:, 0:1], gt[:, 3:4]).then_inc(s_dv, 1)
                e.tensor_mul(gt[:, 5:6], max8[:, 1:2], gt[:, 3:4]).then_inc(s_dv, 1)
                e.wait_ge(s_dv, base + 8)
                e.tensor_scalar(out=eqA[:], in0=iota_sb[:], scalar1=idxf[:, 0:1],
                                scalar2=None, op0=ALU.is_equal).then_inc(s_dv, 1)
                e.tensor_scalar(out=eqB[:], in0=iota_sb[:], scalar1=idxf[:, 1:2],
                                scalar2=None, op0=ALU.is_equal).then_inc(s_dv, 1)
                e.wait_ge(s_dv, base + 10)
                e.tensor_add(Mc2[p][:], eqA[:], eqB[:]).then_inc(s_dv, 1)
                e.wait_ge(s_dv, base + 11)
                e.tensor_scalar(out=eqA[:], in0=eqA[:], scalar1=gt[:, 4:5],
                                scalar2=None, op0=ALU.mult).then_inc(s_dv, 1)
                e.tensor_scalar(out=eqB[:], in0=eqB[:], scalar1=gt[:, 5:6],
                                scalar2=None, op0=ALU.mult).then_inc(s_dv, 1)
                e.wait_ge(s_dv, base + 13)
                e.tensor_add(Wc2[p][:], eqA[:], eqB[:]).then_inc(s_dv, 1)
            bump("dv", 14)
            b_top2 = n["dv"]

            @blk.tensor
            def _(e, p=p, wdv=b_top2, wd2=bsca[0]):
                e.wait_ge(s_dv, wdv)
                if wd2:
                    e.wait_ge(s_dv, wd2)
                e.matmul(ps_c[0:8, 0:1], Wc2[p][:], onecol[:],
                         start=True, stop=True)
                ins = e.matmul(ps_c[0:8, 1:2], Mc2[p][:], onecol[:],
                               start=True, stop=True)
                ins.then_inc(s_pe, 1)
            btop[p] = bump("pe")

            @blk.vector
            def _(e, p=p, wpe=btop[p], base=n["dv"]):
                e.wait_ge(s_pe, wpe)
                e.tensor_add(SCacc[:, 0:1], SCacc[:, 0:1], ps_c[0:8, 0:1]).then_inc(s_dv, 1)
                e.wait_ge(s_dv, base + 1)
                e.tensor_add(SCacc[:, 1:2], SCacc[:, 1:2], ps_c[0:8, 1:2]).then_inc(s_dv, 1)
            bsca[0] = bump("dv", 2)

        a4_dv = n["dv"]
        a4_go = n["go"]

        @blk.gpsimd
        def _(e, wdv=a4_dv, wgo=a4_go):
            e.wait_ge(s_dv, wdv)
            e.dma_start(out=part_d.rearrange("c p -> p c"), in_=SCacc[:],
                        allow_slow_non_contiguous=True).then_inc(s_go, 16)
        bump("go", 16)

        @blk.gpsimd
        def _(e, wgo=n["go"]):
            if DBG_A >= 1:
                e.dma_start(out=dbg["kp"], in_=kp_s).then_inc(s_go, 16)
                e.dma_start(out=dbg["v"], in_=v_s).then_inc(s_go, 16)
                e.dma_start(out=dbg["q"], in_=q_s).then_inc(s_go, 16)
                e.dma_start(out=dbg["p"], in_=p_s).then_inc(s_go, 16)
                e.dma_start(out=dbg["pos3"], in_=pos3_s).then_inc(s_go, 16)
                e.dma_start(out=dbg["xnf"], in_=xnf_s).then_inc(s_go, 16)
                bump("go", 96)
            if DBG_A >= 2:
                e.dma_start(out=dbg["o"], in_=o_s).then_inc(s_go, 16)
                bump("go", 16)
            e.wait_ge(s_go, n["go"])
            e.wait_ge(s_xm[0], n["xm0"])
            e.wait_ge(s_xm[1], n["xm1"])
            e.wait_ge(s_xo[0], n["xo0"])
            e.wait_ge(s_xo[1], n["xo1"])
        p4.close()

    return nc





TCH = 1024


def build_B():
    nc = bass.Bass("TRN2", target_bir_lowering=False)
    dr = lambda nm, s, dt, k="ExternalInput": nc.dram_tensor(nm, s, dt, kind=k).ap()

    xm_d = dr("xm", [TO, D], F32)
    pall_d = dr("pall", [8, 16], F32)
    gw_d = dr("gwT", [D, 8], F32)
    lbb_d = dr("lbb", [1, 8], F32)
    egw_d = dr("egwT", [8, D, 512], F32)
    euw_d = dr("euwT", [8, D, 512], F32)
    edw_d = dr("edwT", [8, 512, D], F32)
    sgw_d = dr("sgwT", [D, D], F32)
    suw_d = dr("suwT", [D, D], F32)
    sdw_d = dr("sdwT", [D, D], F32)
    ident_d = dr("ident", [128, 128], BF16)
    out_d = dr("out", [TO, D], F32, "ExternalOutput")

    scr = lambda nm, s, dt: nc.dram_tensor(nm, s, dt, kind="Internal").ap()
    wm_s = scr("wm_s", [1, 8], F32)

    gctx = ExitStack()
    gsb = lambda nm, s, dt: gctx.enter_context(nc.sbuf_tensor(nm, s, dt))
    pst = lambda nm, s, dt: gctx.enter_context(nc.psum_tensor(nm, s, dt))
    sem = lambda nm: gctx.enter_context(nc.semaphore(nm))

    ident = gsb("ident_s", [128, 128], BF16)
    part = gsb("part_s", [8, 16], F32)
    ones8 = gsb("ones8", [8, 1], F32)
    wvec = gsb("wvec", [8, 4], F32)
    wme_b = gsb("wme_b", [128, 8], F32)
    lbb_b = gsb("lbb_b", [128, 8], F32)
    gwT = gsb("gwT_sb", [128, DC, 8], BF16)
    stats = gsb("stats", [128, 8], F32)
    x2t = gsb("x2t", [128, D], F32)
    xin = [gsb(f"xin{i}", [128, D], F32) for i in range(2)]
    xng = [gsb(f"xng{i}", [128, D], BF16) for i in range(2)]
    xnfm = gsb("xnfm", [128, DC, TCH], BF16)
    lam = gsb("lam", [128, TO // 128, 8], F32)
    gsc = gsb("gsc", [128, 8], F32)
    max8 = gsb("max8", [128, 8], F32)
    idx2 = gsb("idx2", [128, 8], mybir.dt.uint32)
    gt = gsb("gt", [128, 8], F32)
    Mc = gsb("Mc", [128, 8], F32)
    wea = [gsb(f"wea{i}", [128, DC, 512], BF16) for i in range(2)]
    web = [gsb(f"web{i}", [128, DC, 512], BF16) for i in range(2)]
    wed = [gsb(f"wed{i}", [128, 4, D], BF16) for i in range(2)]
    wsa = gsb("wsa", [128, DC, D], BF16)
    wsb_ = gsb("wsb", [128, DC, D], BF16)
    wsd = gsb("wsd", [128, DC, D], BF16)
    hst = gsb("hst", [128, 512], BF16)
    hut = gsb("hut", [128, 512], BF16)
    hfm = gsb("hfm", [128, 8, TCH], BF16)
    acc = gsb("acc", [128, TCH // 128, D], F32)
    xmi = gsb("xmi", [128, D], F32)
    outt = gsb("outt", [128, D], F32)

    ps_a = [pst(f"ps_a{i}", [128, 1024], F32) for i in range(2)]
    ps_b = [pst(f"ps_b{i}", [128, 1024], BF16) for i in range(2)]
    ps_c = pst("ps_c", [128, 512], F32)
    ps_d = pst("ps_d", [128, 512], F32)

    s_lc = sem("s_lc")
    s_li = [sem("s_li0"), sem("s_li1")]
    s_lw = sem("s_lw")
    s_lwa = [sem("s_lwa0"), sem("s_lwa1")]
    s_lsh = sem("s_lsh")
    s_lx = sem("s_lx")
    s_pe = sem("s_pe")
    s_dv = sem("s_dv")
    s_sc = sem("s_sc")
    s_go = sem("s_go")
    s_gl = sem("s_gl")

    n = {"lc": 0, "li0": 0, "li1": 0, "lw": 0, "lwa0": 0, "lwa1": 0, "lx": 0,
         "lsh": 0, "pe": 0, "dv": 0, "sc": 0, "go": 0, "gl": 0}

    def bump(k, i=1):
        n[k] += i
        return n[k]

    with nc.Block() as blk:

        @blk.sync
        def _(e):
            e.dma_start(out=ident[:], in_=ident_d).then_inc(s_lc, 16)
            e.dma_start(out=part[:], in_=pall_d).then_inc(s_lc, 16)
            e.dma_start(out=lbb_b[:], in_=lbb_d.broadcast_to((128, 8))).then_inc(s_lc, 16)
        bump("lc", 48)

        @blk.vector
        def _(e, w=n["lc"]):
            e.wait_ge(s_lc, w)
            e.memset(ones8[:], 1.0).then_inc(s_dv, 1)
        dv0 = bump("dv")

        @blk.tensor
        def _(e, dv0=dv0):
            e.wait_ge(s_dv, dv0)
            e.matmul(ps_d[0:8, 0:1], part[:, 0:8], ones8[:], start=True, stop=True)
            ins = e.matmul(ps_d[0:8, 1:2], part[:, 8:16], ones8[:], start=True, stop=True)
            ins.then_inc(s_pe, 1)
        pe0 = bump("pe")

        @blk.vector
        def _(e, pe0=pe0, base=n["dv"]):
            e.wait_ge(s_pe, pe0)
            e.tensor_copy(wvec[:, 0:1], ps_d[0:8, 0:1]).then_inc(s_dv, 1)
            e.tensor_copy(wvec[:, 1:2], ps_d[0:8, 1:2]).then_inc(s_dv, 1)
            e.wait_ge(s_dv, base + 2)
            e.tensor_scalar_max(wvec[:, 2:3], wvec[:, 1:2], 1.0).then_inc(s_dv, 1)
            e.wait_ge(s_dv, base + 3)
            e.reciprocal(wvec[:, 2:3], wvec[:, 2:3]).then_inc(s_dv, 1)
            e.wait_ge(s_dv, base + 4)
            e.tensor_mul(wvec[:, 3:4], wvec[:, 0:1], wvec[:, 2:3]).then_inc(s_dv, 1)
        dv_wm = bump("dv", 5)

        @blk.gpsimd
        def _(e, dv_wm=dv_wm, base=n["gl"]):
            e.wait_ge(s_dv, dv_wm)
            e.dma_start(out=wm_s[0][:, None], in_=wvec[:, 3:4]).then_inc(s_gl, 16)
            e.wait_ge(s_gl, base + 16)
            e.dma_start(out=wme_b[:], in_=wm_s.broadcast_to((128, 8))).then_inc(s_gl, 16)
        gl_wm = bump("gl", 32)

        @blk.gpsimd
        def _(e):
            e.dma_start(out=gwT[:],
                        in_=gw_d.rearrange("(c p) e -> p c e", p=128)).then_inc(s_lw, 16)
        bump("lw", 16)

        @blk.scalar
        def _(e, w=n["lw"]):
            e.wait_ge(s_lw, w)
            e.sem_inc(s_sc, 1)
        bump("sc")

        b1_go = n["go"]

        @blk.gpsimd
        def _(e):
            e.dma_start(out=wsa[:],
                        in_=sgw_d.rearrange("(c p) q -> p c q", p=128)).then_inc(s_lsh, 16)
            e.dma_start(out=wsb_[:],
                        in_=suw_d.rearrange("(c p) q -> p c q", p=128)).then_inc(s_lsh, 16)
            e.dma_start(out=wsd[:],
                        in_=sdw_d.rearrange("(c p) q -> p c q", p=128)).then_inc(s_lsh, 16)
        lc_sh = bump("lsh", 48)

        psfree = {}
        ev_dv = [0, 0]
        ev_sc = [0, 0]
        wst_dv = {}
        out_go = 0
        for ch in range(TO // TCH):
            t0 = ch * TCH
            for tl in range(TCH // 128):
                t = ch * (TCH // 128) + tl
                b = t % 2

                @blk.sync
                def _(e, b=b, t=t, w=ev_dv[b]):
                    if w:
                        e.wait_ge(s_dv, w)
                    e.dma_start(out=xin[b][:], in_=xm_d[t * 128:(t + 1) * 128, :]).then_inc(s_li[b], 16)
                ld_t = bump(f"li{b}", 16)

                @blk.scalar
                def _(e, b=b, ld_t=ld_t):
                    e.wait_ge(s_li[b], ld_t)
                    e.activation(out=x2t[:], in_=xin[b][:], func=AF.Square,
                                 accum_out=stats[:, 1:2]).then_inc(s_sc, 1)
                sc_sq = bump("sc")

                @blk.vector
                def _(e, b=b, sc_sq=sc_sq, base=n["dv"]):
                    e.wait_ge(s_sc, sc_sq)
                    e.reduce_sum(stats[:, 0:1], xin[b][:], axis=AX.X).then_inc(s_dv, 1)
                    e.wait_ge(s_dv, base + 1)
                    e.tensor_scalar_mul(stats[:, 2:3], stats[:, 0:1], 1.0 / D).then_inc(s_dv, 1)
                    e.wait_ge(s_dv, base + 2)
                    e.tensor_mul(stats[:, 3:4], stats[:, 2:3], stats[:, 2:3]).then_inc(s_dv, 1)
                    e.wait_ge(s_dv, base + 3)
                    e.scalar_tensor_tensor(out=stats[:, 3:4], in0=stats[:, 1:2],
                                           scalar=1.0 / D, in1=stats[:, 3:4],
                                           op0=ALU.mult, op1=ALU.subtract).then_inc(s_dv, 1)
                    e.wait_ge(s_dv, base + 4)
                    e.tensor_scalar_add(stats[:, 4:5], stats[:, 3:4], EPS).then_inc(s_dv, 1)
                bump("dv", 5)
                dv_ab = n["dv"]

                @blk.scalar
                def _(e, dv_ab=dv_ab):
                    e.wait_ge(s_dv, dv_ab)
                    e.sqrt(stats[:, 5:6], stats[:, 4:5]).then_inc(s_sc, 1)
                sc_rt = bump("sc")

                @blk.vector
                def _(e, b=b, sc_rt=sc_rt, base=n["dv"]):
                    e.wait_ge(s_sc, sc_rt)
                    e.reciprocal(stats[:, 6:7], stats[:, 5:6]).then_inc(s_dv, 1)
                    e.wait_ge(s_dv, base + 1)
                    e.tensor_scalar(out=xng[b][:], in0=xin[b][:], scalar1=stats[:, 2:3],
                                    scalar2=stats[:, 6:7], op0=ALU.subtract,
                                    op1=ALU.mult).then_inc(s_dv, 1)
                bump("dv", 2)
                ev_dv[b] = n["dv"]
                dv_nrm = n["dv"]

                @blk.tensor
                def _(e, b=b, tl=tl, dv_nrm=dv_nrm, w=ev_sc[b], wpe=n["pe"]):
                    e.wait_ge(s_dv, dv_nrm)
                    if tl == 0:
                        e.wait_ge(s_pe, wpe)
                    if w:
                        e.wait_ge(s_sc, w)
                    for k in range(DC):
                        ins = e.transpose(ps_b[b][:, k * 128:(k + 1) * 128],
                                          xng[b][:, k * 128:(k + 1) * 128], ident[:])
                    ins.then_inc(s_pe, 1)
                pe_tr = bump("pe")

                @blk.scalar
                def _(e, b=b, tl=tl, pe_tr=pe_tr):
                    e.wait_ge(s_pe, pe_tr)
                    e.copy(xnfm[:, :, tl * 128:(tl + 1) * 128],
                           ps_b[b][:].rearrange("p (c q) -> p c q", c=DC)).then_inc(s_sc, 1)
                ev_sc[b] = bump("sc")
                sc_fm = n["sc"]

                @blk.tensor
                def _(e, tl=tl, sc_fm=sc_fm, wdv=n["dv"]):
                    e.wait_ge(s_sc, sc_fm)
                    e.wait_ge(s_dv, wdv)
                    for k in range(DC):
                        ins = e.matmul(ps_d[:, 0:8], xnfm[:, k, tl * 128:(tl + 1) * 128],
                                       gwT[:, k, :], start=(k == 0), stop=(k == DC - 1))
                    ins.then_inc(s_pe, 1)
                pe_gt = bump("pe")

                @blk.vector
                def _(e, pe_gt=pe_gt):
                    e.wait_ge(s_pe, pe_gt)
                    e.tensor_add(gsc[:], ps_d[:, 0:8], lbb_b[:]).then_inc(s_dv, 1)
                dv_gl = bump("dv")

                @blk.scalar
                def _(e, dv_gl=dv_gl):
                    e.wait_ge(s_dv, dv_gl)
                    e.activation(out=gsc[:], in_=gsc[:], func=AF.Sigmoid).then_inc(s_sc, 1)
                sc_gs = bump("sc")

                @blk.vector
                def _(e, t=t, sc_gs=sc_gs, wgl=gl_wm, base=n["dv"]):
                    e.wait_ge(s_sc, sc_gs)
                    e.wait_ge(s_gl, wgl)
                    e.max(max8[:], gsc[:]).then_inc(s_dv, 1)
                    e.wait_ge(s_dv, base + 1)
                    e.max_index(idx2[:], max8[:], gsc[:]).then_inc(s_dv, 1)
                    for e8 in range(8):
                        e.wait_ge(s_dv, base + 2 + e8 * 4)
                        e.tensor_single_scalar(gt[:, 0:1], idx2[:, 0:1], float(e8),
                                               ALU.is_equal).then_inc(s_dv, 1)
                        e.tensor_single_scalar(gt[:, 1:2], idx2[:, 1:2], float(e8),
                                               ALU.is_equal).then_inc(s_dv, 1)
                        e.wait_ge(s_dv, base + 4 + e8 * 4)
                        e.tensor_add(Mc[:, e8:e8 + 1], gt[:, 0:1], gt[:, 1:2]).then_inc(s_dv, 1)
                        e.wait_ge(s_dv, base + 5 + e8 * 4)
                        e.tensor_scalar(out=lam[:, t, e8:e8 + 1], in0=Mc[:, e8:e8 + 1],
                                        scalar1=wme_b[:, e8:e8 + 1], scalar2=None,
                                        op0=ALU.mult).then_inc(s_dv, 1)
                bump("dv", 2 + 8 * 4)

            @blk.vector
            def _(e, ch=ch, w=n["sc"], wo=out_go, base=n["dv"]):
                e.wait_ge(s_sc, w)
                if wo:
                    e.wait_ge(s_go, wo)
                e.memset(acc[:], 0.0).then_inc(s_dv, 1)
            acc0_dv = bump("dv")

            for ex in range(9):
                shared = ex == 8
                nhc = 8 if shared else 4
                if not shared:
                    bw = ex % 2

                    @blk.gpsimd
                    def _(e, ex=ex, bw=bw, w=wst_dv.get(bw, 0)):
                        if w:
                            e.wait_ge(s_pe, w)
                        e.dma_start(out=wea[bw][:],
                                    in_=egw_d[ex].rearrange("(c p) q -> p c q", p=128)).then_inc(s_lwa[bw], 16)
                        e.dma_start(out=web[bw][:],
                                    in_=euw_d[ex].rearrange("(c p) q -> p c q", p=128)).then_inc(s_lwa[bw], 16)
                        e.dma_start(out=wed[bw][:],
                                    in_=edw_d[ex].rearrange("(c p) q -> p c q", p=128)).then_inc(s_lwa[bw], 16)
                    ld_w = bump(f"lwa{bw}", 48)

                for hc in range(nhc):
                    for ts in range(TCH // 512):
                        hp = (hc * (TCH // 512) + ts) % 2
                        pslots = {0: ps_a[0][:, hp * 512:(hp + 1) * 512],
                                  1: ps_a[1][:, hp * 512:(hp + 1) * 512]}
                        ev_mark = {}
                        for which in range(2):
                            @blk.tensor
                            def _(e, ex=ex, hc=hc, ts=ts, which=which, ch=ch, hp=hp,
                                  shared=shared, ld_w=ld_w if not shared else lc_sh,
                                  wev=psfree.get((which, hp)), bw=(ex % 2) if not shared else 0):
                                if not shared:
                                    e.wait_ge(s_lwa[bw], ld_w)
                                else:
                                    e.wait_ge(s_lsh, ld_w)
                                if wev is not None:
                                    e.wait_ge(s_dv if wev[0] == "dv" else s_sc, wev[1])
                                wsrc = (wsa if shared else wea[bw]) if which == 0 else (wsb_ if shared else web[bw])
                                for k in range(DC):
                                    ins = e.matmul(
                                        pslots[which],
                                        wsrc[:, k, hc * 128:(hc + 1) * 128],
                                        xnfm[:, k, ts * 512:(ts + 1) * 512],
                                        start=(k == 0), stop=(k == DC - 1))
                                ins.then_inc(s_pe, 1)
                            pe_h = bump("pe")

                            if which == 0:
                                @blk.scalar
                                def _(e, pe_h=pe_h, hp=hp, wdv=n["dv"]):
                                    e.wait_ge(s_pe, pe_h)
                                    e.wait_ge(s_dv, wdv)
                                    e.activation(out=hst[:], in_=pslots[0],
                                                 func=AF.Silu).then_inc(s_sc, 1)
                                ev_mark[0] = ("sc", bump("sc"))
                                psfree[(0, hp)] = ev_mark[0]
                            else:
                                @blk.vector
                                def _(e, pe_h=pe_h, hp=hp, wdv=n["dv"]):
                                    e.wait_ge(s_pe, pe_h)
                                    e.wait_ge(s_dv, wdv)
                                    e.tensor_copy(hut[:], pslots[1]).then_inc(s_dv, 1)
                                ev_mark[1] = ("dv", bump("dv"))
                                psfree[(1, hp)] = ev_mark[1]

                        @blk.vector
                        def _(e, hc=hc, ts=ts, wsc=ev_mark[0], wdv=ev_mark[1]):
                            e.wait_ge(s_sc, wsc[1])
                            e.wait_ge(s_dv, wdv[1])
                            e.tensor_mul(hfm[:, hc, ts * 512:(ts + 1) * 512],
                                         hst[:], hut[:]).then_inc(s_dv, 1)
                        bump("dv")
                hdone_dv = n["dv"]

                for ts in range(TCH // 128):
                    for oh in range(2):
                        psx = ps_c if oh == 0 else ps_d

                        @blk.tensor
                        def _(e, ex=ex, ts=ts, oh=oh, psx=psx, shared=shared,
                              hdone_dv=hdone_dv, wdv=n["dv"], nhc=nhc,
                              bw=(ex % 2) if not shared else 0):
                            e.wait_ge(s_dv, max(hdone_dv, wdv))
                            wdsrc = wsd if shared else wed[bw]
                            for k in range(nhc):
                                ins = e.matmul(
                                    psx[:],
                                    hfm[:, k, ts * 128:(ts + 1) * 128],
                                    wdsrc[:, k if shared else k, oh * 512:(oh + 1) * 512],
                                    start=(k == 0), stop=(k == nhc - 1))
                            ins.then_inc(s_pe, 1)
                        pe_d = bump("pe")

                        @blk.vector
                        def _(e, ex=ex, ts=ts, oh=oh, psx=psx, shared=shared,
                              ch=ch, pe_d=pe_d, base=n["dv"]):
                            e.wait_ge(s_pe, pe_d)
                            e.wait_ge(s_dv, base)
                            if shared:
                                e.tensor_add(acc[:, ts, oh * 512:(oh + 1) * 512],
                                             acc[:, ts, oh * 512:(oh + 1) * 512],
                                             psx[:]).then_inc(s_dv, 1)
                            else:
                                e.scalar_tensor_tensor(
                                    out=acc[:, ts, oh * 512:(oh + 1) * 512],
                                    in0=psx[:],
                                    scalar=lam[:, ch * (TCH // 128) + ts, ex:ex + 1],
                                    in1=acc[:, ts, oh * 512:(oh + 1) * 512],
                                    op0=ALU.mult, op1=ALU.add).then_inc(s_dv, 1)
                        bump("dv")
                if not shared:
                    wst_dv[ex % 2] = n["pe"]

            for ts in range(TCH // 128):
                @blk.sync
                def _(e, ch=ch, ts=ts, w=n["dv"]):
                    e.wait_ge(s_dv, w)
                    e.dma_start(out=xmi[:],
                                in_=xm_d[(t0 + ts * 128):(t0 + (ts + 1) * 128), :]).then_inc(s_lx, 16)
                ld_xm = bump("lx", 16)

                @blk.vector
                def _(e, ts=ts, ld_xm=ld_xm, base=n["dv"], wo=out_go):
                    e.wait_ge(s_lx, ld_xm)
                    e.wait_ge(s_dv, base)
                    if wo:
                        e.wait_ge(s_go, wo)
                    e.tensor_add(outt[:], acc[:, ts, :], xmi[:]).then_inc(s_dv, 1)
                dv_out = bump("dv")

                @blk.gpsimd
                def _(e, ch=ch, ts=ts, dv_out=dv_out):
                    e.wait_ge(s_dv, dv_out)
                    e.dma_start(out=out_d[(t0 + ts * 128):(t0 + (ts + 1) * 128), :],
                                in_=outt[:]).then_inc(s_go, 16)
                out_go = bump("go", 16)

        @blk.gpsimd
        def _(e, w=n["go"]):
            e.wait_ge(s_go, w)

    return nc



def _core_masks(typ):
    needs = NEEDS[typ]
    m = np.zeros((8, 4, 128, 1024), dtype=np.float32)
    for s, bud in enumerate(SLOT_BUD):
        pos = needs[s] - 1
        k0 = (bud - 2) * 512
        kj = k0 + np.arange(1024)[None, :]
        for c in range(4):
            qpos = pos * 512 + c * 128 + np.arange(128)[:, None]
            m[s, c] = np.where(kj <= qpos, 0.0, NEG)
    return m.astype(BF16NP)


def _stage_inputs(inputs):
    x = np.ascontiguousarray(inputs["L0_attn_x"])
    common = {
        "w_inT": np.ascontiguousarray(inputs["L0_attn_W_in"].T),
        "b_in": np.ascontiguousarray(inputs["L0_attn_b_in"]),
        "ilw": np.ascontiguousarray(inputs["L0_attn_in_ln_w"]),
        "ilb": np.ascontiguousarray(inputs["L0_attn_in_ln_b"]),
        "w_outT": np.ascontiguousarray(inputs["L0_attn_W_out"].T),
        "olw": np.ascontiguousarray(inputs["L0_attn_out_ln_w"]),
        "olb": np.ascontiguousarray(inputs["L0_attn_out_ln_b"]),
        "smf": np.ascontiguousarray(inputs["L0_attn_smear_factor"]).reshape(1, NH),
        "lgs": np.ascontiguousarray(inputs["L0_attn_log_scale"]).reshape(1, NH),
        "gwT": np.ascontiguousarray(inputs["L0_ffn_gate_w"].T),
        "lbb": np.ascontiguousarray(inputs["L0_ffn_lb_bias"]).reshape(1, 8),
        "ident": np.eye(128, dtype=BF16NP),
        "ones3": np.ones((3, 128), dtype=BF16NP),
        "iota8": np.arange(8, dtype=np.float32).reshape(1, 8),
    }
    in_maps = []
    for c in range(8):
        bt, typ = c // 2, c % 2
        needs = NEEDS[typ]
        xo = np.concatenate([x[bt, (nn - 1) * 512:nn * 512] for nn in needs], 0)
        in_maps.append({"xf": np.ascontiguousarray(x[bt]),
                        "xo": np.ascontiguousarray(xo),
                        "masks": _core_masks(typ), **common})
    return x, in_maps


def kernel(**inputs):
    x, in_maps = _stage_inputs(inputs)
    ncA = build_A()
    resA = run_bass_kernel_spmd(ncA, in_maps, core_ids=list(range(8)))

    pall = np.stack([resA.results[c]["part"].reshape(16) for c in range(8)])
    ncB = build_B()
    common_b = {
        "pall": pall,
        "gwT": np.ascontiguousarray(inputs["L0_ffn_gate_w"].T),
        "lbb": np.ascontiguousarray(inputs["L0_ffn_lb_bias"]).reshape(1, 8),
        "egwT": np.ascontiguousarray(inputs["L0_ffn_expert_gate_w"].transpose(0, 2, 1)),
        "euwT": np.ascontiguousarray(inputs["L0_ffn_expert_up_w"].transpose(0, 2, 1)),
        "edwT": np.ascontiguousarray(inputs["L0_ffn_expert_down_w"].transpose(0, 2, 1)),
        "sgwT": np.ascontiguousarray(inputs["L0_ffn_shared_gate_w"].T),
        "suwT": np.ascontiguousarray(inputs["L0_ffn_shared_up_w"].T),
        "sdwT": np.ascontiguousarray(inputs["L0_ffn_shared_down_w"].T),
        "ident": np.eye(128, dtype=BF16NP),
    }
    in_maps_b = [{"xm": resA.results[c]["xmid"], **common_b} for c in range(8)]
    resB = run_bass_kernel_spmd(ncB, in_maps_b, core_ids=list(range(8)))

    out = np.zeros_like(x)
    for c in range(8):
        bt, typ = c // 2, c % 2
        needs = NEEDS[typ]
        ob = resB.results[c]["out"]
        for s, nn in enumerate(needs):
            out[bt, (nn - 1) * 512:nn * 512] = ob[s * 512:(s + 1) * 512]
    return out
